# version 25
# baseline (speedup 1.0000x reference)
import numpy as np
import ml_dtypes
from contextlib import ExitStack

import concourse.bass as bass
import concourse.mybir as mybir
from concourse.bass_utils import run_bass_kernel_spmd

F32 = mybir.dt.float32
BF16 = mybir.dt.bfloat16
AF = mybir.ActivationFunctionType
ALU = mybir.AluOpType
AX = mybir.AxisListType
BFNP = ml_dtypes.bfloat16

B, T, C = 4, 4096, 1024
OWN = T // 2
HEAD = 64
H = C // HEAD
NPAIR = H // 2
LORA_MIX = 32
LORA_DECAY = 64
E = 8
GN_EPS = 0.00064
LN_EPS = 1e-5
CK = 128
NCK_OWN = OWN // CK
NCK_ST = OWN // CK
NCC = C // 128
TB = 512
NTB_OWN = OWN // TB
NTT_OWN = OWN // 128
ATT_SCALE = 0.125
ATT_SHIFT = -10.0

ENGINES = ("pe", "act", "dve", "pool", "sp")


class Prog:

    def __init__(self, nc):
        self.nc = nc
        self.ops = {e: [] for e in ENGINES}
        self.marked = {e: set() for e in ENGINES}
        self.dma_tokens = set()

    def op(self, eng, fn, deps=(), dma=False):
        clean = [d for d in deps if d is not None]
        idx = len(self.ops[eng])
        self.ops[eng].append((fn, clean, 16 if dma else 1))
        if dma:
            self.marked[eng].add(idx)
            self.dma_tokens.add((eng, idx))
        return (eng, idx)

    @staticmethod
    def _implied(e, e2):
        return e == e2 == "pe"

    def finalize(self, engs, final_tokens):
        nc = self.nc
        for e in ENGINES:
            for i, (_, deps, _) in enumerate(self.ops[e]):
                for (e2, j) in deps:
                    if not self._implied(e, e2):
                        self.marked[e2].add(j)
        for d in final_tokens:
            self.marked[d[0]].add(d[1])
        self.semkey = {}
        self.cum = {}
        for e in ENGINES:
            counts = {}
            keys = []
            cums = []
            for i, (_, _, inc) in enumerate(self.ops[e]):
                key = "actd" if (e == "act" and (e, i) in self.dma_tokens) else e
                if i in self.marked[e]:
                    counts[key] = counts.get(key, 0) + inc
                keys.append(key)
                cums.append(counts.get(key, 0))
            self.semkey[e] = keys
            self.cum[e] = cums
        sems = dict(self._sems)

        def run_engine(e):
            def body(engine):
                hw = {}
                def wait(e2, j):
                    v = self.cum[e2][j]
                    key = self.semkey[e2][j]
                    assert j in self.marked[e2]
                    if v > hw.get(key, 0):
                        engine.wait_ge(sems[key], v)
                        hw[key] = v
                for i, (fn, deps, inc) in enumerate(self.ops[e]):
                    for (e2, j) in deps:
                        if self._implied(e, e2):
                            continue
                        wait(e2, j)
                    ins = fn(engine)
                    if i in self.marked[e]:
                        assert ins is not None, f"marked op {e}[{i}] returned None"
                        ins.then_inc(sems[self.semkey[e][i]], inc)
                if e == "sp":
                    for (e2, j) in final_tokens:
                        wait(e2, j)
            return body
        return run_engine


class Buf:
    __slots__ = ("w", "r", "sc")

    def __init__(self):
        self.w = None
        self.r = []
        self.sc = False

    def rd(self):
        return [self.w] if self.w is not None else []

    def wr(self):
        d = list(self.r)
        if self.w is not None:
            d.append(self.w)
        return d

    def did_read(self, tok):
        for i, (e, j) in enumerate(self.r):
            if e == tok[0]:
                if tok[1] > j:
                    self.r[i] = tok
                return
        self.r.append(tok)

    def did_write(self, tok):
        self.w = tok
        self.r = []


SCALAR_BUFS = {"consts", "lnst0", "lnst1", "gst", "comb", "attn_st",
               "dsb", "gnstats"}


def _slices_for_core(inputs, core):
    b = core // 2
    half = core % 2
    o0 = half * OWN

    g = lambda k: np.asarray(inputs[k], dtype=np.float32)
    x = g("L0_attn_query")[b]
    x_own = x[o0:o0 + OWN]

    col8 = lambda v: np.ascontiguousarray(np.asarray(v, np.float32).reshape(NCC, 128).T)

    d = {
        "x_full": np.ascontiguousarray(x),
        "x_own": np.ascontiguousarray(x_own),
        "is_odd": np.full((128, 1), float(half), np.float32),
        "maa_x": col8(g("L0_attn_time_maa_x")),
        "maa_w": col8(g("L0_attn_time_maa_w")),
        "maa_k": col8(g("L0_attn_time_maa_k")),
        "maa_v": col8(g("L0_attn_time_maa_v")),
        "maa_r": col8(g("L0_attn_time_maa_r")),
        "w1": g("L0_attn_time_maa_w1"),
        "w2": g("L0_attn_time_maa_w2").reshape(128, 1024),
        "tdecay_row": g("L0_attn_time_decay").reshape(1, C),
        "dw1": g("L0_attn_time_decay_w1"),
        "dw2": g("L0_attn_time_decay_w2"),
        "w_r": g("L0_attn_W_r"),
        "w_k": g("L0_attn_W_k"),
        "w_v": g("L0_attn_W_v"),
        "w_o": g("L0_attn_W_o"),
        "ln_w": col8(g("L0_attn_ln_w")),
        "ln_b": col8(g("L0_attn_ln_b")),
        "w_qkv": g("L0_ffn_W_qkv"),
        "bq_col": col8(g("L0_ffn_b_qkv")[0:C]),
        "bk_col": col8(g("L0_ffn_b_qkv")[C:2 * C]),
        "bv_row": g("L0_ffn_b_qkv")[2 * C:3 * C].reshape(1, C),
        "w_gate": g("L0_ffn_W_gate"),
        "b_gate": g("L0_ffn_b_gate").reshape(1, E),
        "w_exp": g("L0_ffn_W_experts").reshape(E * C, C),
        "b_exp": g("L0_ffn_b_experts"),
    }
    fa = g("L0_attn_time_faaaa")
    useg = np.zeros((128, H), np.float32)
    for p in range(NPAIR):
        useg[0:64, 2 * p] = fa[2 * p]
        useg[64:128, 2 * p + 1] = fa[2 * p + 1]
    d["useg"] = useg
    return d


def _consts():
    ident = np.eye(128, dtype=BFNP)
    ident_f = np.eye(128, dtype=np.float32)
    ltneg = -np.triu(np.ones((128, 128), np.float32)).astype(BFNP)
    mstrict = np.triu(np.ones((128, 128), np.float32), 1)
    mstrict4 = np.tile(mstrict, (1, 4)).astype(BFNP)
    ones_row = np.ones((1, 128), BFNP)
    return ident, ident_f, ltneg, mstrict4, ones_row


class K:

    def __init__(self, debug_keys=()):
        nc = bass.Bass("TRN2", target_bir_lowering=False, debug=False)
        self.nc = nc
        self.p = Prog(nc)
        self.stack = ExitStack()
        self.debug_keys = set(debug_keys)
        self.dbg = {}

        ap = lambda name, shape, dt=F32, kind="ExternalInput": nc.dram_tensor(
            name, list(shape), dt, kind=kind)
        self.in_x_full = ap("x_full", (T, C))
        self.in_x_own = ap("x_own", (OWN, C))
        self.in_is_odd = ap("is_odd", (128, 1))
        for nm, sh in [
            ("maa_x", (128, 8)), ("maa_w", (128, 8)), ("maa_k", (128, 8)),
            ("maa_v", (128, 8)), ("maa_r", (128, 8)),
            ("w1", (C, 128)), ("w2", (128, C)),
            ("tdecay_row", (1, C)), ("dw1", (C, 64)), ("dw2", (64, C)),
            ("w_r", (C, C)), ("w_k", (C, C)), ("w_v", (C, C)), ("w_o", (C, C)),
            ("ln_w", (128, 8)), ("ln_b", (128, 8)),
            ("w_qkv", (C, 3 * C)), ("bq_col", (128, 8)), ("bk_col", (128, 8)),
            ("bv_row", (1, C)), ("w_gate", (C, E)), ("b_gate", (1, E)),
            ("w_exp", (E * C, C)), ("b_exp", (E, C)), ("useg", (128, H)),
        ]:
            setattr(self, "in_" + nm, ap(nm, sh))
        self.out = ap("out", (OWN, C), kind="ExternalOutput")

        ident, ident_f, ltneg, mstrict4, ones_row = _consts()
        self.c_ident = nc.inline_tensor(ident, "c_ident")
        self.c_ident_f = nc.inline_tensor(ident_f, "c_ident_f")
        self.c_ltneg = nc.inline_tensor(ltneg, "c_ltneg")
        self.c_mstrict4 = nc.inline_tensor(mstrict4, "c_mstrict4")
        self.c_ones_row = nc.inline_tensor(ones_row, "c_ones_row")

        dram = lambda name, shape, dt: nc.dram_tensor(name, list(shape), dt, kind="Internal")
        self.d_xn = dram("d_xn", (T, C), BF16)
        self.d_xno = dram("d_xno", (OWN, C), BF16)
        self.d_yn = dram("d_yn", (OWN, C), BF16)
        self.d_wr = dram("d_wr", (C, C), BF16)
        self.d_wk = dram("d_wk", (C, C), BF16)
        self.d_wv = dram("d_wv", (C, C), BF16)
        self.d_wo = dram("d_wo", (C, C), BF16)
        self.d_wqkv = dram("d_wqkv", (C, 3 * C), BF16)
        self.d_wexp = dram("d_wexp", (E * C, C), BF16)
        self.d_vaug = dram("d_vaug", (T, 2 * 520), BF16)

        self.p._sems = {}
        for e in list(ENGINES) + ["actd"]:
            self.p._sems[e] = self.stack.enter_context(nc.semaphore("sem_" + e))

        sb = self._sb
        self.xn_fm_t = sb("xn_fm_t", (128, NCC * T), BF16)
        self.xno_fm_t = sb("xno_fm_t", (128, NCC * OWN), BF16)
        self.xn_fm = [self.xn_fm_t[:, i * T:(i + 1) * T] for i in range(NCC)]
        self.xno_fm = [self.xno_fm_t[:, i * OWN:(i + 1) * OWN] for i in range(NCC)]
        self.scr_a = sb("scr_a", (128, 12288), BF16)
        self.scr_b = sb("scr_b", (128, 4096), F32)
        self.scr_c = sb("scr_c", (128, 8192), BF16)
        self.xi = sb("xi_tb", (128, 4096), BF16)
        self.tanh1 = sb("tanh1", (128, 1024), BF16)
        self.eta = sb("eta", (64, 1024), BF16)
        self.pmask = sb("pmask_yn", (128, 2048), BF16)
        self.S0 = sb("S0", (128, 512), F32)
        self.S1 = sb("S1", (128, 512), F32)
        self.Sbf = sb("Sbf", (128, 512), BF16)
        self.ws = sb("ws", (128, 4096), BF16)
        self.stats = sb("stats", (128, 160), F32)
        self.xxb = sb("xxb", (128, 2048), BF16)
        self.lnb_bf = sb("lnb_bf", (128, 8), BF16)
        self.outt = sb("outt", (128, 1024), F32)
        xi_addr = nc.lookup_mloc(self.xi).addr
        self.acc_hi = nc.alloc_sbuf_tensor_at("acc_hi", [128, 4096], F32, offset=xi_addr)
        self.xtile = nc.alloc_sbuf_tensor_at("xtile", [128, 1024], F32, offset=xi_addr + 16384)
        assert nc.lookup_mloc(self.tanh1).addr == xi_addr + 8192
        assert nc.lookup_mloc(self.pmask).addr == xi_addr + 12288
        assert nc.lookup_mloc(self.S0).addr == xi_addr + 16384
        assert nc.lookup_mloc(self.Sbf).addr == xi_addr + 20480
        self.comb = sb("comb", (128, 128), F32)
        self.dsb = sb("dsb", (128, 32), F32)
        self.eal = sb("eal", (128, 8), F32)
        self.prev = sb("prev", (128, 8), F32)
        self.bb = sb("bb", (1, 512), BF16)
        self.consts = sb("consts", (128, 16), F32)
        self.ident_sb = sb("ident_sb", (128, 128), BF16)
        self.ident_f32 = sb("ident_f32", (128, 128), F32)
        self.ltneg_sb = sb("ltneg_sb", (128, 128), BF16)
        self.mstrict_sb = sb("mstrict_sb", (128, 512), BF16)
        self.onesrow_sb = sb("onesrow_sb", (1, 128), BF16)
        self.w1_sb = sb("w1_sb", (128, 8 * 128), BF16)
        self.w2_sb = sb("w2_sb", (128, C), BF16)
        self.dw1_sb = sb("dw1_sb", (128, 8 * 64), BF16)
        self.dw2_sb = sb("dw2_sb", (64, C), BF16)
        self.vec_sb = sb("vec_sb", (128, 96), F32)
        self.tdec_row = sb("s_tdec_row", (1, C), BF16)
        self.bv_row = sb("s_bv_row", (1, C), BF16)
        self.bgate_row = sb("s_bgate_row", (1, E), BF16)
        self.bexp_sb = sb("s_bexp", (E, C), BF16)
        self.wgate_sb = sb("wgate_sb", (128, 8 * E), BF16)
        self.useg_sb = sb("useg_sb", (128, H), BF16)

        ps = lambda name, shape, dt=F32: self.stack.enter_context(nc.psum_tensor(name, list(shape), dt))
        self.psA = ps("psA", (128, 1024))
        self.psB = ps("psB", (128, 1024))
        self.psC = ps("psC", (128, 512))
        self.psD = ps("psD", (128, 512))
        self.psE = ps("psE", (128, 512))
        self.psG = ps("psG", (128, 1024), BF16)

        self.bufs = {}

    def _sb(self, name, shape, dt):
        return self.stack.enter_context(self.nc.sbuf_tensor(name, list(shape), dt))

    def buf(self, name):
        if name not in self.bufs:
            self.bufs[name] = Buf()
            if name in SCALAR_BUFS:
                self.bufs[name].sc = True
        return self.bufs[name]

    def emit(self, eng, fn, reads=(), writes=(), extra=(), dma=False):
        deps = list(extra)
        dmatok = self.p.dma_tokens

        def keep(d, b):
            return True

        for b in reads:
            deps += [d for d in b.rd() if keep(d, b)]
        for b in writes:
            deps += [d for d in b.wr() if keep(d, b)]
        if dma and eng in ("act", "pool", "sp"):
            last = getattr(self, "_last_dma", {}).get(eng)
            if last is not None:
                deps.append(last)
        tok = self.p.op(eng, fn, deps, dma=dma)
        if dma and eng in ("act", "pool", "sp"):
            if not hasattr(self, "_last_dma"):
                self._last_dma = {}
            self._last_dma[eng] = tok
        for b in reads:
            b.did_read(tok)
        for b in writes:
            b.did_write(tok)
        return tok

    CONST_COLS = {-1.0: 0, 0.125: 1, ATT_SHIFT: 2, LN_EPS: 3, GN_EPS: 4,
                  1.0 / 1024: 5, 1.0 / 64: 6, 1.0: 7, 0.5: 8, -1.0 / 64: 9}

    def ccol(self, val):
        return self.consts[:, self.CONST_COLS[val]:self.CONST_COLS[val] + 1]

    def _init_consts(self):
        bcon = self.buf("consts")
        for val, i in self.CONST_COLS.items():
            self.emit("dve", lambda v, val=val, i=i: v.memset(self.consts[:, i:i + 1], val),
                      writes=[bcon])

    def prologue(self):
        n = self

        def ld(eng, dst_ap, src_ap, bufname, deps=()):
            b = n.buf(bufname)
            return n.emit(eng, lambda e, d=dst_ap, s=src_ap: e.dma_start(out=d, in_=s),
                          writes=[b], extra=deps, dma=True)

        prev = [None]

        def ldp(dst_ap, src_ap, bufname):
            t = ld("pool", dst_ap, src_ap, bufname,
                   deps=[prev[0]] if prev[0] else ())
            prev[0] = t

        ldp(n.useg_sb[:, :], n.in_useg.ap(), "useg")
        ldp(n.dw2_sb[:, :], n.in_dw2.ap(), "dw2")
        ldp(n.w2_sb[:, :], n.in_w2.ap(), "w2")
        ldp(n.w1_sb.ap().rearrange("p (cc l) -> p cc l", cc=8),
            n.in_w1.ap().rearrange("(cc p) l -> p cc l", p=128), "w1")
        ldp(n.dw1_sb.ap().rearrange("p (cc l) -> p cc l", cc=8),
            n.in_dw1.ap().rearrange("(cc p) l -> p cc l", p=128), "dw1")
        ldp(n.wgate_sb.ap().rearrange("p (cc e) -> p cc e", cc=8),
            n.in_w_gate.ap().rearrange("(cc p) e -> p cc e", p=128), "wgate")
        ldp(n.tdec_row[:, :], n.in_tdecay_row.ap(), "tdec_row")
        ldp(n.bv_row[:, :], n.in_bv_row.ap(), "bv_row")
        ldp(n.bgate_row[:, :], n.in_b_gate.ap(), "bgate_row")
        ldp(n.bexp_sb[:, :], n.in_b_exp.ap(), "bexp")
        ldp(n.onesrow_sb[:, :], n.c_ones_row.ap(), "onesrow")
        for nm in ("wk", "wv", "wr", "wqkv", "wo"):
            src = getattr(n, "in_w_" + nm[1:]) if nm in ("wr", "wk", "wv", "wo") else n.in_w_qkv
            dst = getattr(n, "d_" + nm)
            ldp(dst.ap(), src.ap(), "d_" + nm)
        for e8 in range(E):
            ldp(n.d_wexp.ap()[e8 * C:(e8 + 1) * C, :],
                n.in_w_exp.ap()[e8 * C:(e8 + 1) * C, :], f"d_wexp{e8}")

        ld("sp", n.ident_sb[:, :], n.c_ident.ap(), "ident")
        ld("sp", n.ident_f32[:, :], n.c_ident_f.ap(), "ident_f")
        ld("sp", n.ltneg_sb[:, :], n.c_ltneg.ap(), "ltneg")
        ld("sp", n.mstrict_sb[:, :], n.c_mstrict4.ap(), "mstrict")
        vecs = [("maa_x", 0), ("maa_w", 8), ("maa_k", 16), ("maa_v", 24),
                ("maa_r", 32), ("ln_w", 40), ("ln_b", 48), ("bq_col", 56),
                ("bk_col", 64)]
        for nm, off in vecs:
            key = {"bq_col": "bq", "bk_col": "bk"}.get(nm, nm)
            ld("sp", n.vec_sb[:, off:off + 8], getattr(n, "in_" + nm).ap(), "vec_" + key)
        ld("sp", n.vec_sb[:, 72:73], n.in_is_odd.ap(), "vec_is_odd")

    def vcol(self, name, i=0):
        off = {"maa_x": 0, "maa_w": 8, "maa_k": 16, "maa_v": 24, "maa_r": 32,
               "ln_w": 40, "ln_b": 48, "bq": 56, "bk": 64, "is_odd": 72}[name]
        return self.vec_sb[:, off + i:off + i + 1]

    def vbuf(self, name):
        return self.buf("vec_" + name)

    def ln_phase(self):
        n = self
        x_t = n.scr_b[:, 0:3 * C]
        xsq = n.scr_b[:, 3 * C:4 * C]
        st = n.stats[:, 0:16]
        xnout = n.scr_c[:, 0:4 * C]
        bx = [n.buf(f"lnx{i}") for i in range(3)]
        bsq = n.buf("lnsq0")
        bst = [n.buf(f"lnst{i}") for i in range(2)]
        bout = [n.buf(f"lnout{i}") for i in range(4)]
        bfm = [n.buf(f"xn_fm{cc}") for cc in range(NCC)]
        bfo = [n.buf(f"xno_fm{cc}") for cc in range(NCC)]
        pG = n.psbuf("G")
        bcon = n.buf("consts")

        jobs = []
        for tt in range(T // 128):
            jobs.append((n.in_x_full.ap()[tt * 128:(tt + 1) * 128, :], True, tt))
        for tt in range(OWN // 128):
            jobs.append((n.in_x_own.ap()[tt * 128:(tt + 1) * 128, :], False, tt))

        for j, (src, full, tt) in enumerate(jobs):
            k = j % 2
            k3 = j % 3
            k4 = j % 4
            xt = x_t[:, k3 * C:(k3 + 1) * C]
            s1 = st[:, k * 8 + 0:k * 8 + 1]
            s2 = st[:, k * 8 + 1:k * 8 + 2]
            mean = st[:, k * 8 + 2:k * 8 + 3]
            var = st[:, k * 8 + 3:k * 8 + 4]
            rstd = st[:, k * 8 + 4:k * 8 + 5]
            m2 = st[:, k * 8 + 5:k * 8 + 6]
            xo = xnout[:, k4 * C:(k4 + 1) * C]
            n.emit("sp", lambda e, xt=xt, src=src: e.dma_start(out=xt, in_=src),
                   writes=[bx[k3]], dma=True)
            n.emit("dve", lambda v, xt=xt, s1=s1: v.reduce_sum(s1, xt, axis=AX.X),
                   reads=[bx[k3]], writes=[bst[k]])
            n.emit("act", lambda a, xt=xt, s2=s2: a.activation(
                out=xsq, in_=xt, func=AF.Square, bias=0.0, scale=1.0, accum_out=s2),
                   reads=[bx[k3]], writes=[bsq, bst[k]])
            n.emit("dve", lambda v, s1=s1, mean=mean: v.tensor_scalar(
                mean, s1, n.ccol(1.0 / 1024), None, ALU.mult),
                   reads=[bst[k], bcon], writes=[bst[k]])
            n.emit("dve", lambda v, mean=mean, m2=m2: v.tensor_mul(m2, mean, mean),
                   reads=[bst[k]], writes=[bst[k]])
            n.emit("dve", lambda v, s2=s2, m2=m2, var=var: v.scalar_tensor_tensor(
                var, s2, n.ccol(1.0 / 1024), m2, ALU.mult, ALU.subtract),
                   reads=[bst[k], bcon], writes=[bst[k]])
            n.emit("act", lambda a, var=var, rstd=rstd: a.activation(
                out=rstd, in_=var, func=AF.Sqrt, bias=n.ccol(LN_EPS), scale=1.0),
                   reads=[bst[k], bcon], writes=[bst[k]])
            n.emit("dve", lambda v, rstd=rstd: v.reciprocal(rstd, rstd),
                   reads=[bst[k]], writes=[bst[k]])
            n.emit("dve", lambda v, xt=xt, xo=xo, mean=mean, rstd=rstd: v.tensor_scalar(
                xo, xt, mean, rstd, ALU.subtract, ALU.mult),
                   reads=[bx[k3], bst[k]], writes=[bout[k4]])
            for cc in range(NCC):
                n.emit("pe", lambda t, xo=xo, cc=cc: t.transpose(
                    n.psG[:, cc * 128:(cc + 1) * 128],
                    xo[:, cc * 128:(cc + 1) * 128], n.ident_sb[:, :]),
                    reads=[bout[k4], n.buf("ident")], writes=[pG])
            fmt = n.xn_fm_t if full else n.xno_fm_t
            SEQ = T if full else OWN
            dst3 = fmt.ap().rearrange("p (cc t) -> p cc t", cc=NCC)[
                :, :, tt * 128:(tt + 1) * 128]
            n.emit("act", lambda a, dst3=dst3: a.activation(
                out=dst3, in_=n.psG.ap().rearrange("p (cc t) -> p cc t", cc=NCC),
                func=AF.Copy, bias=0.0, scale=1.0),
                reads=[pG], writes=(bfm if full else bfo))
        bprev = n.buf("prev")
        for cc in range(NCC):
            n.emit("dve", lambda v, cc=cc: v.tensor_scalar(
                n.prev[:, cc:cc + 1], n.xn_fm[cc][:, OWN - 1:OWN],
                n.vcol("is_odd"), None, ALU.mult),
                   reads=[n.buf(f"xn_fm{cc}"), n.vbuf("is_odd")], writes=[bprev])
        for cc in range(NCC):
            n.debug_dump(f"xn_fm{cc}", n.xn_fm[cc][:, :], (128, T),
                         [n.buf(f"xn_fm{cc}")], BF16)
            n.debug_dump(f"xno_fm{cc}", n.xno_fm[cc][:, :], (128, OWN),
                         [n.buf(f"xno_fm{cc}")], BF16)

    def MM(self, out_ap, lhsT_ap, rhs_ap, start, stop, pos=None, reads=(), writes=()):
        return self.emit("pe", lambda t: t.matmul(
            out_ap, lhsT_ap, rhs_ap, start=start, stop=stop, tile_position=pos),
            reads=reads, writes=writes)

    def psbuf(self, name):
        return self.buf("ps_" + name)

    def _xi_pass(self, stream, t0, width, own, tslot=0):
        n = self
        self._xi_par = getattr(self, "_xi_par", 0) ^ 1
        col0 = 0 if width > 256 else self._xi_par * 2048
        bxis = [n.buf(f"xi_{col0}_{cc}") for cc in range(NCC)]
        bxx = n.buf("xxb")
        f = {"w": 0, "k": 1, "v": 2, "r": 3}.get(stream)
        if width > 256:
            for cc in range(NCC):
                bxis[cc].r += n.buf(f"xi_2048_{cc}").r + n.buf(f"xi_0_{cc}").r
        big = n.xno_fm_t if own else n.xn_fm_t
        SEQ = OWN if own else T
        bsrc = [n.buf(("xno_fm" if own else "xn_fm") + str(cc)) for cc in range(NCC)]
        nh = NCC * width // 2048
        ccph = NCC // nh
        for hxx in range(nh):
            c0 = hxx * ccph
            xdst = n.xxb.ap()[:, 0:ccph * width].rearrange("p (cc w) -> p cc w", cc=ccph)                 if nh == 2 and False else None
            xdst = n.xxb.ap()[:, hxx * 0:2048][:, 0:ccph * width].rearrange(
                "p (cc w) -> p cc w", cc=ccph)
            sview = big.ap()[:, c0 * SEQ:(c0 + ccph) * SEQ].rearrange(
                "p (cc t) -> p cc t", cc=ccph)
            if t0 == 0:
                n.emit("dve", lambda v, xdst=xdst, sview=sview: v.tensor_sub(
                    xdst[:, :, 1:width], sview[:, :, 0:width - 1], sview[:, :, 1:width]),
                    reads=bsrc, writes=[bxx])
                if own:
                    n.emit("dve", lambda v, xdst=xdst, sview=sview, c0=c0: v.tensor_sub(
                        xdst[:, :, 0:1],
                        n.prev.ap()[:, c0:c0 + ccph].rearrange("p c -> p c 1" if False else "p (c o) -> p c o", o=1),
                        sview[:, :, 0:1]),
                        reads=bsrc + [n.buf("prev")], writes=[bxx])
                else:
                    n.emit("dve", lambda v, xdst=xdst, sview=sview: v.tensor_scalar(
                        xdst[:, :, 0:1], sview[:, :, 0:1], -1.0, None, ALU.mult),
                        reads=bsrc, writes=[bxx])
            else:
                n.emit("dve", lambda v, xdst=xdst, sview=sview: v.tensor_sub(
                    xdst[:, :, 0:width], sview[:, :, t0 - 1:t0 + width - 1],
                    sview[:, :, t0:t0 + width]),
                    reads=bsrc, writes=[bxx])
            for ccl in range(ccph):
                cc = c0 + ccl
                xx_ap = n.xxb[:, ccl * width:(ccl + 1) * width]
                xi_ap = n.xi[:, col0 + cc * width:col0 + (cc + 1) * width]
                src2 = big[:, cc * SEQ + t0:cc * SEQ + t0 + width]
                if stream == "x":
                    n.emit("dve", lambda v, xx_ap=xx_ap, xi_ap=xi_ap, src2=src2, cc=cc:
                           v.scalar_tensor_tensor(
                               xi_ap, xx_ap, n.vcol("maa_x", cc), src2,
                               ALU.mult, ALU.add),
                           reads=[bxx, bsrc[cc], n.vbuf("maa_x")], writes=[bxis[cc]])
                    continue
                mslot = cc % 4
                mtens = n.psA if mslot < 2 else n.psB
                mps = mtens[:, (mslot % 2) * 512:(mslot % 2) * 512 + width]
                mb = n.psbuf(("A", "A", "B", "B")[mslot] + str(mslot % 2))
                n.MM(mps, n.w2_sb[f * 32:(f + 1) * 32, cc * 128:(cc + 1) * 128],
                     n.tanh1[f * 32:(f + 1) * 32, tslot * 512:tslot * 512 + width],
                     True, True, pos=(32 * f, 0),
                     reads=[n.buf("w2"), n.buf(f"tanh1_{tslot}")], writes=[mb])
                nmcol = "maa_" + stream
                n.emit("dve", lambda v, mps=mps, xx_ap=xx_ap, xi_ap=xi_ap, cc=cc, nmcol=nmcol:
                       v.scalar_tensor_tensor(xi_ap, mps, n.vcol(nmcol, cc), xx_ap,
                                              ALU.add, ALU.mult),
                       reads=[mb, bxx, n.vbuf(nmcol)], writes=[bxis[cc]])
            if stream != "x":
                xi3 = n.xi.ap()[:, col0 + c0 * width:col0 + (c0 + ccph) * width].rearrange(
                    "p (cc w) -> p cc w", cc=ccph)
                n.emit("dve", lambda v, xi3=xi3, sview=sview: v.tensor_add(
                    xi3, xi3, sview[:, :, t0:t0 + width]),
                    reads=bsrc + [bxis[c0 + i] for i in range(ccph)],
                    writes=[bxis[c0 + i] for i in range(ccph)])
        return col0, bxis

    def _proj_fm(self, wdram, wbuf_name, out_col0, xic, width):
        n = self
        xcol0, bxis = xic
        for q in range(4):
            slot = q % 2
            bws = n.buf("ws_a" if slot == 0 else "ws_b")
            ws_ap = n.ws.ap()[:, slot * 2048:(slot + 1) * 2048].rearrange(
                "p (cc w) -> p cc w", cc=8)
            src = wdram.ap().rearrange("(cc p) c -> p cc c", p=128)[:, :, q * 256:(q + 1) * 256]
            n.emit("sp", lambda e, ws_ap=ws_ap, src=src: e.dma_start(out=ws_ap, in_=src),
                   reads=[n.buf(wbuf_name)], writes=[bws], dma=True)
            for ic2 in range(2):
                ic = q * 2 + ic2
                ps = n.psC if ic2 == 0 else n.psD
                pb = n.psbuf("C" if ic2 == 0 else "D")
                for cc in range(NCC):
                    n.MM(ps[:, 0:width],
                         n.ws[:, slot * 2048 + cc * 256 + ic2 * 128:
                              slot * 2048 + cc * 256 + ic2 * 128 + 128],
                         n.xi[:, xcol0 + cc * width:xcol0 + (cc + 1) * width],
                         cc == 0, cc == NCC - 1,
                         reads=[bws, bxis[cc]], writes=[pb])
                dst = n.scr_a[:, out_col0 + ic * width:out_col0 + (ic + 1) * width]
                if ic % 2 == 0:
                    n.emit("act", lambda a, dst=dst, ps=ps: a.activation(
                        out=dst, in_=ps[:, 0:width], func=AF.Copy, bias=0.0, scale=1.0),
                        reads=[pb], writes=[n.buf("scra")])
                else:
                    n.emit("dve", lambda v, dst=dst, ps=ps: v.tensor_copy(dst, ps[:, 0:width]),
                           reads=[pb], writes=[n.buf("scra")])

    def _proj_tm(self, wdram, wbuf_name, out_col0, xic, width):
        n = self
        xcol0, bxis = xic
        ntt = width // 128
        for fh in range(2):
            if fh == 0:
                wtile = n.ws.ap()[:, 0:4096]
                bws = [n.buf("ws_a"), n.buf("ws_b")]
            else:
                wtile = n.scr_a.ap()[:, 8192:12288]
                bws = [n.buf("ws_wo")]
            src = wdram.ap().rearrange("(cc p) c -> p cc c", p=128)[
                :, :, fh * 512:(fh + 1) * 512]
            n.emit("sp", lambda e, wtile=wtile, src=src: e.dma_start(
                out=wtile.rearrange("p (cc w) -> p cc w", cc=8), in_=src),
                reads=[n.buf(wbuf_name)], writes=bws, dma=True)
            for tt in range(ntt):
                ps = n.psC if tt % 2 == 0 else n.psD
                pb = n.psbuf("C" if tt % 2 == 0 else "D")
                for cc in range(NCC):
                    n.MM(ps[:, 0:512],
                         n.xi[:, xcol0 + cc * width + tt * 128:xcol0 + cc * width + (tt + 1) * 128],
                         wtile[:, cc * 512:(cc + 1) * 512],
                         cc == 0, cc == NCC - 1,
                         reads=bws + [bxis[cc]], writes=[pb])
                dst = n.scr_a[:, out_col0 + tt * 1024 + fh * 512:
                              out_col0 + tt * 1024 + (fh + 1) * 512]
                if tt == 0:
                    n.emit("act", lambda a, dst=dst, ps=ps: a.activation(
                        out=dst, in_=ps[:, 0:512], func=AF.Copy, bias=0.0, scale=1.0),
                        reads=[pb], writes=[n.buf("scra")])
                else:
                    n.emit("dve", lambda v, dst=dst, ps=ps: v.tensor_copy(dst, ps[:, 0:512]),
                           reads=[pb], writes=[n.buf("scra")])

    def _lora_pass(self, t0, width, own, tslot):
        n = self
        xcol0, bxis = self._xi_pass("x", t0, width, own)
        pb = n.psbuf("C")
        for cc in range(NCC):
            n.MM(n.psC[:, 0:width], n.w1_sb[:, cc * 128:(cc + 1) * 128],
                 n.xi[:, xcol0 + cc * width:xcol0 + (cc + 1) * width],
                 cc == 0, cc == NCC - 1,
                 reads=[n.buf("w1"), bxis[cc]], writes=[pb])
        n.emit("act", lambda a: a.activation(
            out=n.tanh1[:, tslot * 512:tslot * 512 + width], in_=n.psC[:, 0:width],
            func=AF.Tanh, bias=0.0, scale=1.0),
            reads=[pb], writes=[n.buf(f"tanh1_{tslot}")])

    def _eta_pass(self, t0, width, own, tslot):
        n = self
        xcol0, bxis = self._xi_pass("w", t0, width, own, tslot=tslot)
        pb = n.psbuf("E")
        for cc in range(NCC):
            n.MM(n.psE[0:64, 0:width], n.dw1_sb[:, cc * 64:(cc + 1) * 64],
                 n.xi[:, xcol0 + cc * width:xcol0 + (cc + 1) * width],
                 cc == 0, cc == NCC - 1,
                 reads=[n.buf("dw1"), bxis[cc]], writes=[pb])
        n.emit("act", lambda a: a.activation(
            out=n.eta[0:64, tslot * 512:tslot * 512 + width], in_=n.psE[0:64, 0:width],
            func=AF.Tanh, bias=0.0, scale=1.0),
            reads=[pb], writes=[n.buf(f"eta_{tslot}")])

    def _wkv_front(self, ckw, own, kcol, vcol, chunk_idx, tslot=0):
        n = self
        par2 = chunk_idx % 2
        s_old = n.S0 if par2 == 0 else n.S1
        s_new = n.S1 if par2 == 0 else n.S0
        bso = n.buf("S0" if par2 == 0 else "S1")
        bsn = n.buf("S1" if par2 == 0 else "S0")
        bsbf = n.buf("Sbf")
        pA0, pA1 = n.psbuf("A0"), n.psbuf("A1")
        pB0, pB1 = n.psbuf("B0"), n.psbuf("B1")
        pC, pD, pG = n.psbuf("C"), n.psbuf("D"), n.psbuf("G")
        CSLOT = lambda i: n.scr_c[:, i * 1024:(i + 1) * 1024]
        bsl = [n.buf(f"cs{i}") for i in range(8)]
        ktm = n.scr_a[:, kcol:kcol + 1024]
        vtm = n.scr_a[:, vcol:vcol + 1024]
        bscra = n.buf("scra")
        beta = n.buf(f"eta_{tslot}")

        for half in range(2):
            pb = pA0 if half == 0 else pA1
            n.MM(n.psA[:, half * 512:(half + 1) * 512],
                 n.eta[0:64, tslot * 512 + ckw * 128:tslot * 512 + (ckw + 1) * 128],
                 n.dw2_sb[0:64, half * 512:(half + 1) * 512], True, False,
                 reads=[beta, n.buf("dw2")], writes=[pb])
            n.MM(n.psA[:, half * 512:(half + 1) * 512],
                 n.onesrow_sb[0:1, 0:128],
                 n.tdec_row[0:1, half * 512:(half + 1) * 512], False, True,
                 reads=[n.buf("onesrow"), n.buf("tdec_row")], writes=[pb])
        n.emit("act", lambda a: a.activation(out=CSLOT(0), in_=n.psA[:, :], func=AF.Exp,
                                             bias=0.0, scale=1.0),
               reads=[pA0, pA1], writes=[bsl[0]])
        for half in range(2):
            pb = pA0 if half == 0 else pA1
            n.MM(n.psA[:, half * 512:(half + 1) * 512], n.ltneg_sb[:, :],
                 CSLOT(0)[:, half * 512:(half + 1) * 512], True, True,
                 reads=[n.buf("ltneg"), bsl[0]], writes=[pb])
        n.emit("act", lambda a: a.activation(out=CSLOT(1), in_=n.psA[:, :], func=AF.Exp,
                                             bias=0.0, scale=n.ccol(-1.0)),
               reads=[pA0, pA1, n.buf("consts")], writes=[bsl[1]])
        if own:
            for ic in range(NCC):
                pb = pB0 if ic < 4 else pB1
                n.MM(n.psB[:, ic * 128:(ic + 1) * 128],
                     CSLOT(0)[:, ic * 128:(ic + 1) * 128], n.ltneg_sb[:, :],
                     True, True, reads=[bsl[0], n.buf("ltneg")], writes=[pb])
            n.emit("act", lambda a: a.activation(
                out=n.eal[:, 0:8],
                in_=n.psB.ap().rearrange("p (ic t) -> p ic t", ic=8)[:, :, 127:128],
                func=AF.Exp, bias=0.0, scale=1.0),
                reads=[pB0, pB1], writes=[n.buf("eal")])
            n.emit("act", lambda a: a.activation(out=CSLOT(3), in_=n.psB[:, :], func=AF.Exp,
                                                 bias=0.0, scale=n.ccol(-1.0)),
                   reads=[pB0, pB1, n.buf("consts")], writes=[bsl[3]])
            n.emit("act", lambda a: a.activation(
                out=CSLOT(4).rearrange("p (ic t) -> p ic t", ic=8)[:, :, 1:128],
                in_=n.psB.ap().rearrange("p (ic t) -> p ic t", ic=8)[:, :, 0:127],
                func=AF.Exp, bias=0.0, scale=1.0),
                reads=[pB0, pB1], writes=[bsl[4]])
            n.emit("dve", lambda v: v.memset(
                CSLOT(4).rearrange("p (ic t) -> p ic t", ic=8)[:, :, 0:1], 1.0),
                writes=[bsl[4]])
            for cc in range(NCC):
                n.emit("pe", lambda t, cc=cc: t.transpose(
                    n.psG[:, cc * 128:(cc + 1) * 128],
                    ktm[:, cc * 128:(cc + 1) * 128], n.ident_sb[:, :]),
                    reads=[bscra, n.buf("ident")], writes=[pG])
            kfm_dst = n.scr_a.ap()[:, 0:2048].rearrange("p (ic w) -> p ic w", ic=8)[
                :, :, ckw * 128:(ckw + 1) * 128]
            n.emit("act", lambda a, kfm_dst=kfm_dst: a.activation(
                out=kfm_dst, in_=n.psG.ap().rearrange("p (ic w) -> p ic w", ic=8),
                func=AF.Copy, bias=0.0, scale=1.0),
                reads=[pG], writes=[bscra])
        else:
            for ic in range(NCC):
                n.MM(n.psB[:, ic:ic + 1], CSLOT(0)[:, ic * 128:(ic + 1) * 128],
                     n.ltneg_sb[:, 127:128], True, True,
                     reads=[bsl[0], n.buf("ltneg")], writes=[pB0])
            n.emit("act", lambda a: a.activation(out=n.eal[:, 0:8], in_=n.psB[:, 0:8],
                                                 func=AF.Exp, bias=0.0, scale=1.0),
                   reads=[pB0], writes=[n.buf("eal")])
        n.emit("dve", lambda v: v.tensor_mul(CSLOT(2), ktm, CSLOT(1)),
               reads=[bscra, bsl[1]], writes=[bsl[2]])
        if own:
            rfm = n.scr_a.ap()[:, 2048:4096].rearrange("p (ic w) -> p ic w", ic=8)
            kfm = n.scr_a.ap()[:, 0:2048].rearrange("p (ic w) -> p ic w", ic=8)
            sl = slice(ckw * 128, (ckw + 1) * 128)
            c3 = lambda t: t.rearrange("p (ic w) -> p ic w", ic=8)
            n.emit("dve", lambda v: v.tensor_mul(c3(CSLOT(5)), rfm[:, :, sl], c3(CSLOT(4))),
                   reads=[bscra, bsl[4]], writes=[bsl[5]])
            n.emit("dve", lambda v: v.tensor_mul(c3(CSLOT(6)), kfm[:, :, sl], c3(CSLOT(3))),
                   reads=[bscra, bsl[3]], writes=[bsl[6]])
            n.emit("dve", lambda v: v.tensor_mul(c3(CSLOT(1)), rfm[:, :, sl], kfm[:, :, sl]),
                   reads=[bscra], writes=[bsl[1]])
        for p8 in range(NPAIR):
            ps = n.psC if p8 < 4 else n.psD
            pb = pC if p8 < 4 else pD
            n.MM(ps[:, (p8 % 4) * 128:(p8 % 4 + 1) * 128],
                 CSLOT(2)[:, p8 * 128:(p8 + 1) * 128],
                 vtm[:, p8 * 128:(p8 + 1) * 128],
                 True, True, reads=[bsl[2], bscra], writes=[pb])
        tmp = n.scr_b[:, 2048:2560]
        btmp = n.buf("s_tmp")
        for bank in range(2):
            ps = n.psC if bank == 0 else n.psD
            pb = pC if bank == 0 else pD
            for par in range(2):
                u3 = ps.ap().rearrange("p (q w) -> p q w", q=4)[
                    par * 64:(par + 1) * 64, :, par * 64:(par + 1) * 64]
                so3 = s_old.ap().rearrange("p (q w) -> p q w", q=8)[
                    par * 64:(par + 1) * 64, bank * 4:(bank + 1) * 4, :]
                t3 = tmp.rearrange("p (q w) -> p q w", q=8)[
                    par * 64:(par + 1) * 64, bank * 4:(bank + 1) * 4, :]
                n.emit("dve", lambda v, t3=t3, so3=so3, u3=u3: v.tensor_add(t3, so3, u3),
                       reads=[pb, bso], writes=[btmp])
        for p8 in range(NPAIR):
            n.emit("dve", lambda v, p8=p8: v.tensor_scalar(
                s_new[:, p8 * 64:(p8 + 1) * 64], tmp[:, p8 * 64:(p8 + 1) * 64],
                n.eal[:, p8:p8 + 1], None, ALU.mult),
                reads=[btmp, n.buf("eal")], writes=[bsn])
        if not own:
            n.emit("dve", lambda v: v.tensor_copy(n.Sbf[:, :], s_new[:, :]),
                   reads=[bsn], writes=[bsbf])
            if getattr(n, "dbg_ck", None) == chunk_idx:
                for nm2, islot in [("ew", 0), ("ktil", 2)]:
                    n.debug_dump(f"ck_{nm2}", CSLOT(islot), (128, 1024), [bsl[islot]], BF16)
                n.debug_dump("ck_eal", n.eal[:, 0:8], (128, 8), [n.buf("eal")])
            return
        if own:
            qbank = [n.psA[:, 0:512], n.psA[:, 512:1024], n.psB[:, 0:512], n.psB[:, 512:1024]]
            qb = [pA0, pA1, pB0, pB1]
            for h in range(H):
                ic, par = h // 2, h % 2
                bk, slot = h % 4, h // 4
                n.MM(qbank[bk][:, slot * 128:(slot + 1) * 128],
                     CSLOT(6)[par * 64:par * 64 + 64, ic * 128:(ic + 1) * 128],
                     CSLOT(5)[par * 64:par * 64 + 64, ic * 128:(ic + 1) * 128],
                     True, True, pos=(64 * par, 0),
                     reads=[bsl[6], bsl[5]], writes=[qb[bk]])
            bpm = [n.buf("pm0"), n.buf("pm1")]
            ybank = [n.psC, n.psD]
            yb = [pC, pD]
            for bk in range(4):
                pslot = bk % 2
                pm_ap = n.pmask[:, pslot * 512:(pslot + 1) * 512]
                n.emit("dve", lambda v, pm_ap=pm_ap, bk=bk: v.tensor_mul(
                    pm_ap, qbank[bk], n.mstrict_sb[:, :]),
                    reads=[qb[bk], n.buf("mstrict")], writes=[bpm[pslot]])
                for slot in range(4):
                    h = bk + slot * 4 if False else slot * 4 + bk
                    ic, par = h // 2, h % 2
                    ypar = h % 2
                    n.MM(ybank[ypar][:, (h // 2) * 64:(h // 2) * 64 + 64],
                         pm_ap[:, slot * 128:(slot + 1) * 128],
                         vtm[:, h * 64:(h + 1) * 64],
                         True, False,
                         reads=[bpm[pslot], bscra], writes=[yb[ypar]])
                    n.MM(ybank[ypar][:, (h // 2) * 64:(h // 2) * 64 + 64],
                         CSLOT(5)[par * 64:par * 64 + 64, ic * 128:(ic + 1) * 128],
                         n.Sbf[par * 64:par * 64 + 64, (h // 2) * 64:(h // 2) * 64 + 64],
                         False, True, pos=(64 * par, 0),
                         reads=[bsl[5], bsbf], writes=[yb[ypar]])
            dsb_ap = n.dsb[:, par2 * 16:par2 * 16 + 16]
            for ic in range(NCC):
                n.MM(n.psB[:, 2 * ic:2 * ic + 2],
                     CSLOT(1)[:, ic * 128:(ic + 1) * 128], n.useg_sb[:, 2 * ic:2 * ic + 2],
                     True, True, reads=[bsl[1], n.buf("useg")], writes=[pB0])
            n.emit("dve", lambda v: v.tensor_copy(dsb_ap, n.psB[:, 0:16]),
                   reads=[pB0], writes=[n.buf("dsb")])
            ycopy = n.scr_b[:, 3072:4096]
            byc = n.buf("ycopy")
            for ypar in range(2):
                n.emit("dve", lambda v, ypar=ypar: v.tensor_copy(
                    ycopy[:, ypar * 512:(ypar + 1) * 512], ybank[ypar][:, 0:512]),
                    reads=[yb[ypar]], writes=[byc])
            y_sb = n.scr_b[:, par2 * 1024:(par2 + 1) * 1024]
            by = n.buf("y_sb" if par2 == 0 else "ysq")
            for h in range(H):
                ypar = h % 2
                n.emit("dve", lambda v, h=h, ypar=ypar: v.scalar_tensor_tensor(
                    y_sb[:, h * 64:(h + 1) * 64], vtm[:, h * 64:(h + 1) * 64],
                    dsb_ap[:, h:h + 1],
                    ycopy[:, ypar * 512 + (h // 2) * 64:ypar * 512 + (h // 2) * 64 + 64],
                    ALU.mult, ALU.add),
                    reads=[bscra, n.buf("dsb"), byc], writes=[by])
            n.emit("dve", lambda v: v.tensor_copy(n.Sbf[:, :], s_new[:, :]),
                   reads=[bsn], writes=[bsbf])
        if getattr(n, "dbg_ck", None) == chunk_idx:
            for nm2, islot in [("ew", 0), ("ktil", 2), ("ktf", 6)]:
                n.debug_dump(f"ck_{nm2}", CSLOT(islot), (128, 1024), [bsl[islot]], BF16)
            for nm2, islot in [("enA_fm", 3), ("eAex", 4), ("rt", 5)]:
                n.debug_dump(f"ck_{nm2}", CSLOT(islot), (128, 1024), [bsl[islot]], BF16)
            n.debug_dump("ck_dsb", n.dsb[:, par2 * 16:par2 * 16 + 16], (128, 16),
                         [n.buf("dsb")])
            n.debug_dump("ck_ysb", n.scr_b[:, par2 * 1024:(par2 + 1) * 1024], (128, 1024),
                         [by])
            n.debug_dump("ck_pm", n.pmask[:, 0:1024], (128, 1024),
                         [n.buf("pm0"), n.buf("pm1")], BF16)
            n.debug_dump("ck_eal", n.eal[:, 0:8], (128, 8), [n.buf("eal")])

    def _wkv_tail(self, chunk_idx, dyn_row0):
        n = self
        par2 = chunk_idx % 2
        y_sb = n.scr_b[:, par2 * 1024:(par2 + 1) * 1024]
        by = n.buf("y_sb" if par2 == 0 else "ysq")
        st = n.stats
        bst = n.buf("gnstats")
        y3 = y_sb.rearrange("p (h j) -> p h j", h=H)
        n.emit("dve", lambda v: v.reduce_sum(st[:, 16:32], y3, axis=AX.X),
               reads=[by], writes=[bst])
        ysq = n.outt[:, 0:1024]
        bsq = [n.buf("ot0"), n.buf("ot1")]
        n.emit("act", lambda a: a.activation(
            out=ysq, in_=y_sb, func=AF.Square, bias=0.0, scale=1.0),
            reads=[by], writes=bsq)
        n.emit("dve", lambda v: v.reduce_sum(
            st[:, 32:48], ysq.rearrange("p (h j) -> p h j", h=H), axis=AX.X),
            reads=bsq, writes=[bst])
        n.emit("dve", lambda v: v.tensor_scalar(
            st[:, 48:64], st[:, 16:32], n.ccol(-1.0 / 64), None, ALU.mult),
            reads=[bst, n.buf("consts")], writes=[bst])
        n.emit("dve", lambda v: v.tensor_mul(st[:, 80:96], st[:, 48:64], st[:, 48:64]),
               reads=[bst], writes=[bst])
        n.emit("dve", lambda v: v.scalar_tensor_tensor(
            st[:, 64:80], st[:, 32:48], n.ccol(1.0 / 64), st[:, 80:96],
            ALU.mult, ALU.subtract), reads=[bst, n.buf("consts")], writes=[bst])
        n.emit("act", lambda a: a.activation(
            out=st[:, 80:96], in_=st[:, 64:80], func=AF.Sqrt,
            bias=n.ccol(GN_EPS), scale=1.0),
            reads=[bst, n.buf("consts")], writes=[bst])
        n.emit("dve", lambda v: v.reciprocal(st[:, 80:96], st[:, 80:96]),
               reads=[bst], writes=[bst])
        n.emit("dve", lambda v: v.tensor_mul(st[:, 48:64], st[:, 48:64], st[:, 80:96]),
               reads=[bst], writes=[bst])
        yn_ap = n.pmask[:, 1024:2048]
        byn = n.buf("yn_st")
        for h in range(H):
            n.emit("act", lambda a2, h=h: a2.activation(
                out=yn_ap[:, h * 64:(h + 1) * 64], in_=y_sb[:, h * 64:(h + 1) * 64],
                func=AF.Identity, scale=st[:, 80 + h:81 + h], bias=st[:, 48 + h:49 + h]),
                reads=[by, bst], writes=[byn])
        n.emit("act", lambda a2: a2.dma_start(
            out=n.d_yn.ap()[dyn_row0:dyn_row0 + 128, :], in_=yn_ap),
            reads=[byn], writes=[n.buf("d_yn")], dma=True)

    def tmix_phase(self):
        n = self
        for dst, srcs in [("cs0", ["lnout0"]), ("cs1", ["lnout1"]),
                          ("cs2", ["lnout2"]), ("cs3", ["lnout3"]),
                          ("y_sb", ["lnx0"]), ("ysq", ["lnx1"]),
                          ("s_tmp", ["lnx2"]), ("ycopy", ["lnsq0"])]:
            n.alias_fence(dst, srcs)
        n.emit("dve", lambda v: v.memset(n.S0[:, :], 0.0), writes=[n.buf("S0")])
        n.emit("dve", lambda v: v.memset(n.Sbf[:, :], 0.0), writes=[n.buf("Sbf")])
        ci = 0
        NTB = 16
        tslot = lambda tb: tb % 2

        def t0_of(tb):
            return (tb % 8) * 256

        self._lora_pass(0, 256, False, 0)
        self._eta_pass(0, 256, False, 0)
        xic_k = self._xi_pass("k", 0, 256, False, tslot=0)
        pending_tails = []
        for tb in range(NTB):
            own = tb >= 8
            t0 = t0_of(tb)
            ts, tsn = tb % 2, (tb + 1) % 2
            if pending_tails:
                self._wkv_tail(*pending_tails[0])
            self._proj_tm(n.d_wk, "d_wk", 4096, xic_k, 256)
            if pending_tails:
                self._wkv_tail(*pending_tails[1])
            pending_tails = []
            xic = self._xi_pass("v", t0, 256, own, tslot=ts)
            self._proj_tm(n.d_wv, "d_wv", 6144, xic, 256)
            if own:
                xic = self._xi_pass("r", t0, 256, own, tslot=ts)
                self._proj_fm(n.d_wr, "d_wr", 2048, xic, 256)
            if tb < NTB - 1:
                nown = (tb + 1) >= 8
                nt0 = t0_of(tb + 1)
                self._lora_pass(nt0, 256, nown, tsn)
                self._eta_pass(nt0, 256, nown, tsn)
                xic_k = self._xi_pass("k", nt0, 256, nown, tslot=tsn)
            if tb == 8:
                n.debug_dump("tb0_tanh1", n.tanh1[:, 0:256], (128, 256), [n.buf("tanh1_0")], BF16)
                n.debug_dump("tb0_eta", n.eta[0:64, 0:256], (64, 256), [n.buf("eta_0")], BF16)
                n.debug_dump("tb0_kfm", n.scr_a[:, 0:2048], (128, 2048), [n.buf("scra")], BF16)
                n.debug_dump("tb0_rfm", n.scr_a[:, 2048:4096], (128, 2048), [n.buf("scra")], BF16)
                n.debug_dump("tb0_ktm", n.scr_a[:, 4096:6144], (128, 2048), [n.buf("scra")], BF16)
                n.debug_dump("tb0_vtm", n.scr_a[:, 6144:8192], (128, 2048), [n.buf("scra")], BF16)
            for ckw in range(2):
                self._wkv_front(ckw, own, 4096 + ckw * 1024, 6144 + ckw * 1024,
                                ci + ckw, tslot=ts)
            if own:
                pending_tails = [(ci, t0), (ci + 1, t0 + 128)]
            ci += 2
            if tb == 7:
                s_cur = n.S0 if ci % 2 == 0 else n.S1
                bsc = n.buf("S0" if ci % 2 == 0 else "S1")
                n.emit("dve", lambda v, s_cur=s_cur: v.tensor_scalar(
                    s_cur[:, :], s_cur[:, :], n.vcol("is_odd"), None, ALU.mult),
                    reads=[bsc, n.vbuf("is_odd")], writes=[bsc])
                n.emit("dve", lambda v, s_cur=s_cur: v.tensor_copy(n.Sbf[:, :], s_cur[:, :]),
                       reads=[bsc], writes=[n.buf("Sbf")])
                n.debug_dump("S_bound", n.Sbf[:, :], (128, 512), [n.buf("Sbf")], BF16)
                for cc in range(NCC):
                    n.alias_fence(f"xi_2048_{cc}", [f"xi_0_{cc}"])
        for args in pending_tails:
            self._wkv_tail(*args)
        n.debug_dump_dram("yn", n.d_yn, (OWN, C), [n.buf("d_yn")], BF16)

    def alias_fence(self, dst, srcs):
        bd = self.buf(dst)
        for s in srcs:
            b = self.buf(s)
            bd.r += b.r
            if b.w is not None:
                bd.r.append(b.w)

    def gate_phase(self):
        n = self
        pE = n.psbuf("E")
        pC = n.psbuf("C")
        st = n.stats
        bst = n.buf("gst")
        bcomb = n.buf("comb")
        bcombf = n.buf("comb_fm")
        for tt in range(NTT_OWN):
            zp = n.psE[:, tt * 8:(tt + 1) * 8]
            for cc in range(NCC):
                n.MM(zp, n.xno_fm[cc][:, tt * 128:(tt + 1) * 128],
                     n.wgate_sb[:, cc * 8:(cc + 1) * 8], cc == 0, False,
                     reads=[n.buf(f"xno_fm{cc}"), n.buf("wgate")], writes=[pE])
            n.MM(zp, n.onesrow_sb[0:1, 0:128], n.bgate_row[0:1, 0:8],
                 False, True, reads=[n.buf("onesrow"), n.buf("bgate_row")], writes=[pE])
            cslot = n.comb[:, tt * 8:(tt + 1) * 8]
            n.emit("dve", lambda v, zp=zp: v.reduce_max(st[:, 96:97], zp,
                                                        axis=AX.X, negate=True),
                   reads=[pE], writes=[bst])
            n.emit("act", lambda a2, zp=zp: a2.activation(out=st[:, 104:112], in_=zp,
                                                          func=AF.Exp, bias=st[:, 96:97], scale=1.0),
                   reads=[pE, bst], writes=[bst])
            n.emit("dve", lambda v: v.reduce_sum(st[:, 97:98], st[:, 104:112], axis=AX.X),
                   reads=[bst], writes=[bst])
            n.emit("dve", lambda v: v.reciprocal(st[:, 98:99], st[:, 97:98]),
                   reads=[bst], writes=[bst])
            n.emit("dve", lambda v, cslot=cslot: v.tensor_scalar(
                cslot, st[:, 104:112], st[:, 98:99], None, ALU.mult),
                reads=[bst], writes=[bcomb])
            n.emit("dve", lambda v, cslot=cslot: v.reduce_max(st[:, 99:100], cslot, axis=AX.X),
                   reads=[bcomb], writes=[bst])
            n.emit("dve", lambda v, cslot=cslot: v.tensor_scalar(
                st[:, 104:112], cslot, st[:, 99:100], None, ALU.is_lt),
                reads=[bcomb, bst], writes=[bst])
            n.emit("dve", lambda v, cslot=cslot: v.tensor_mul(st[:, 104:112], st[:, 104:112], cslot),
                   reads=[bcomb, bst], writes=[bst])
            n.emit("dve", lambda v: v.reduce_max(st[:, 100:101], st[:, 104:112], axis=AX.X),
                   reads=[bst], writes=[bst])
            n.emit("dve", lambda v, cslot=cslot: v.tensor_scalar(
                st[:, 104:112], cslot, st[:, 100:101], None, ALU.is_ge),
                reads=[bcomb, bst], writes=[bst])
            n.emit("dve", lambda v, cslot=cslot: v.tensor_mul(cslot, cslot, st[:, 104:112]),
                   reads=[bst, bcomb], writes=[bcomb])
            if tt == 0:
                n.emit("dve", lambda v: v.tensor_copy(n.outt[:, 0:8], n.psE[:, 0:8]),
                       reads=[pE], writes=[n.buf("zdbg")])
                n.debug_dump("z0", n.outt[:, 0:8], (128, 8), [n.buf("zdbg")])
        n.debug_dump("comb_early", n.comb[:, :], (128, 128), [n.buf("comb")])

    def _comb_fm_build(self):
        n = self
        pC = n.psbuf("C")
        bcomb, bcombf = n.buf("comb"), n.buf("comb_fm")
        for tt in range(NTT_OWN):
            n.emit("pe", lambda t, tt=tt: t.transpose(
                n.psC[0:8, 0:128], n.comb[:, tt * 8:(tt + 1) * 8],
                n.ident_f32[0:128, 0:128]),
                reads=[bcomb, n.buf("ident_f")], writes=[pC])
            n.emit("dve", lambda v, tt=tt: v.tensor_copy(
                n.scr_c[0:8, tt * 128:(tt + 1) * 128], n.psC[0:8, 0:128]),
                reads=[pC], writes=[bcombf])

    def _vaug_build(self):
        n = self
        pC, pD = n.psbuf("C"), n.psbuf("D")
        bvst = n.buf("vstage")
        vstage = n.scr_c[:, 2048:2568]
        ap1 = n.scr_c.ap()[:, 2048:2568].rearrange("p (k o) -> p k o", k=8)[:, :, 64:65]
        n.emit("dve", lambda v: v.memset(ap1, 1.0), writes=[bvst])
        for jb in range(2):
            bws = n.buf("ws_a")
            bws2 = n.buf("ws_b")
            ws_ap = n.ws.ap()[:, 0:4096].rearrange("p (cc w) -> p cc w", cc=8)
            srcw = n.d_wqkv.ap().rearrange("(cc p) c -> p cc c", p=128)[
                :, :, 2 * C + jb * 512:2 * C + (jb + 1) * 512]
            n.emit("sp", lambda e, ws_ap=ws_ap, srcw=srcw: e.dma_start(out=ws_ap, in_=srcw),
                   reads=[n.buf("d_wqkv")], writes=[bws, bws2], dma=True)
            for sc in range(32):
                ps, pb = (n.psC, pC) if sc % 2 == 0 else (n.psD, pD)
                for cc in range(NCC):
                    n.MM(ps[:, 0:512], n.xn_fm[cc][:, sc * 128:(sc + 1) * 128],
                         n.ws[:, cc * 512:(cc + 1) * 512], cc == 0, False,
                         reads=[bws if cc < 4 else bws2, n.buf(f"xn_fm{cc}")], writes=[pb])
                n.MM(ps[:, 0:512], n.onesrow_sb[0:1, 0:128],
                     n.bv_row[0:1, jb * 512:(jb + 1) * 512], False, True,
                     reads=[n.buf("onesrow"), n.buf("bv_row")], writes=[pb])
                dst = n.scr_c.ap()[:, 2048:2568].rearrange(
                    "p (hp par j) -> p hp par j", hp=4, par=2)[:, :, :, 0:64]
                srcp = ps.ap()[:, 0:512].rearrange("p (hp par j) -> p hp par j", hp=4, par=2)
                n.emit("dve", lambda v, dst=dst, srcp=srcp: v.tensor_copy(dst, srcp),
                       reads=[pb], writes=[bvst])
                n.emit("sp", lambda e, sc=sc, jb=jb: e.dma_start(
                    out=n.d_vaug.ap()[sc * 128:(sc + 1) * 128, jb * 520:(jb + 1) * 520],
                    in_=vstage), reads=[bvst], writes=[n.buf("d_vaug")], dma=True)

    def _attn_pair(self, hp, half):
        n = self
        pA0, pA1 = n.psbuf("A0"), n.psbuf("A1")
        pB0, pB1 = n.psbuf("B0"), n.psbuf("B1")
        pC, pD, pG = n.psbuf("C"), n.psbuf("D"), n.psbuf("G")
        bscra, bst = n.buf("scra"), n.buf("attn_st")
        bws0 = n.buf("ws_a")
        bacc = n.buf("acc")
        hq2 = hp - 4 * half
        jb, hp4 = hp // 4, hp % 4
        for j, col0 in enumerate([hp * 128, C + hp * 128]):
            dst = n.ws.ap()[:, j * 1024:(j + 1) * 1024].rearrange("p (cc w) -> p cc w", cc=8)
            srcw = n.d_wqkv.ap().rearrange("(cc p) c -> p cc c", p=128)[:, :, col0:col0 + 128]
            n.emit("sp", lambda e, dst=dst, srcw=srcw: e.dma_start(out=dst, in_=srcw),
                   reads=[n.buf("d_wqkv")], writes=[bws0], dma=True)
        srcv = n.d_vaug.ap().rearrange("(sc p) c -> p sc c", p=128)[
            :, :, jb * 520 + hp4 * 130:jb * 520 + (hp4 + 1) * 130]
        n.emit("sp", lambda e: e.dma_start(
            out=n.scr_a.ap()[:, 0:4160].rearrange("p (sc c) -> p sc c", sc=32), in_=srcv),
            reads=[n.buf("d_vaug")], writes=[bscra], dma=True)
        for tb4 in range(4):
            ps, pb = (n.psC, pC) if tb4 % 2 == 0 else (n.psD, pD)
            for cc in range(NCC):
                n.MM(ps[:, 0:512], n.ws[:, cc * 128:(cc + 1) * 128],
                     n.xno_fm[cc][:, tb4 * 512:(tb4 + 1) * 512], cc == 0, cc == NCC - 1,
                     reads=[bws0, n.buf(f"xno_fm{cc}")], writes=[pb])
            n.emit("dve", lambda v, ps=ps, tb4=tb4: v.tensor_scalar(
                n.scr_a[:, 4160 + tb4 * 512:4160 + (tb4 + 1) * 512], ps[:, 0:512],
                n.vcol("bq", hp), None, ALU.add),
                reads=[pb, n.vbuf("bq")], writes=[bscra])
        for tb8 in range(8):
            ps, pb = (n.psC, pC) if tb8 % 2 == 0 else (n.psD, pD)
            for cc in range(NCC):
                n.MM(ps[:, 0:512], n.ws[:, 1024 + cc * 128:1024 + (cc + 1) * 128],
                     n.xn_fm[cc][:, tb8 * 512:(tb8 + 1) * 512], cc == 0, cc == NCC - 1,
                     reads=[bws0, n.buf(f"xn_fm{cc}")], writes=[pb])
            n.emit("dve", lambda v, ps=ps, tb8=tb8: v.tensor_scalar(
                n.scr_a[:, 6208 + tb8 * 512:6208 + (tb8 + 1) * 512], ps[:, 0:512],
                n.vcol("bk", hp), None, ALU.add),
                reads=[pb, n.vbuf("bk")], writes=[bscra])
        ES_COL = (4096, 5120, 2568, 7168)
        for tb4 in range(4):
            def emit_v(sc):
                es = n.scr_c[:, ES_COL[sc % 4]:ES_COL[sc % 4] + 1024]
                bes = n.buf(f"expS{sc % 4}")
                for par in range(2):
                    n.MM((n.psC if par == 0 else n.psD)[0:65, :],
                         n.scr_a[:, sc * 130 + par * 65:sc * 130 + (par + 1) * 65],
                         es[:, par * 512:(par + 1) * 512], sc == 0, sc == 31,
                         reads=[bscra, bes], writes=[pC if par == 0 else pD])
            for sc in range(32):
                qk, qb0, qb1 = (n.psA, pA0, pA1) if sc % 2 == 0 else (n.psB, pB0, pB1)
                for par in range(2):
                    n.MM(qk[:, par * 512:(par + 1) * 512],
                         n.scr_a[par * 64:par * 64 + 64, 6208 + sc * 128:6208 + (sc + 1) * 128],
                         n.scr_a[par * 64:par * 64 + 64, 4160 + tb4 * 512:4160 + (tb4 + 1) * 512],
                         True, True, pos=(64 * par, 0),
                         reads=[bscra], writes=[qb0 if par == 0 else qb1])
                es = n.scr_c[:, ES_COL[sc % 4]:ES_COL[sc % 4] + 1024]
                bes = n.buf(f"expS{sc % 4}")
                n.emit("act", lambda a2, qk=qk, es=es: a2.activation(
                    out=es, in_=qk[:, :], func=AF.Exp, bias=n.ccol(ATT_SHIFT),
                    scale=n.ccol(0.125)),
                    reads=[qb0, qb1, n.buf("consts")], writes=[bes])
                if sc > 1:
                    emit_v(sc - 2)
            emit_v(30)
            emit_v(31)
            for par in range(2):
                atmp = n.scr_c[0:65, 6144 + par * 512:6144 + (par + 1) * 512]
                batm = n.buf(f"atmp{par}")
                n.emit("dve", lambda v, atmp=atmp, par=par: v.tensor_copy(
                    atmp, (n.psC if par == 0 else n.psD)[0:65, :]),
                    reads=[pC if par == 0 else pD], writes=[batm])
                for tt4 in range(4):
                    slot = par * 4 + tt4
                    n.emit("pe", lambda t, atmp=atmp, slot=slot, tt4=tt4: t.transpose(
                        n.psG[:, slot * 128:slot * 128 + 65],
                        atmp[0:65, tt4 * 128:(tt4 + 1) * 128],
                        n.ident_sb[0:65, 0:65]),
                        reads=[batm, n.buf("ident")], writes=[pG])
            for par in range(2):
                for tt4 in range(4):
                    slot = par * 4 + tt4
                    tt = tb4 * 4 + tt4
                    hq = hq2 * 2 + par
                    piece = n.scr_b if hq < 4 else n.acc_hi
                    col = tt * 256 + (hq % 4) * 64
                    rcol = n.stats[:, 112 + slot:113 + slot]
                    n.emit("dve", lambda v, slot=slot, rcol=rcol: v.reciprocal(
                        rcol, n.psG[:, slot * 128 + 64:slot * 128 + 65]),
                        reads=[pG], writes=[bst])
                    n.emit("dve", lambda v, slot=slot, rcol=rcol, col=col, piece=piece:
                           v.tensor_scalar(
                               piece[:, col:col + 64], n.psG[:, slot * 128:slot * 128 + 64],
                               rcol, None, ALU.mult),
                           reads=[pG, bst], writes=[bacc])

    def _moe_half(self, half):
        n = self
        pC, pD = n.psbuf("C"), n.psbuf("D")
        bacc, bcomb, bcombf = n.buf("acc"), n.buf("comb"), n.buf("comb_fm")
        bscra = n.buf("scra")
        dstw = n.scr_a.ap()[:, 8192:12288].rearrange("p (cc w) -> p cc w", cc=8)
        srcw = n.d_wo.ap().rearrange("(cc p) c -> p cc c", p=128)[
            :, :, half * 512:(half + 1) * 512]
        n.emit("sp", lambda e: e.dma_start(out=dstw, in_=srcw),
               reads=[n.buf("d_wo")], writes=[n.buf("ws_wo"), bscra], dma=True)
        for c4 in range(4):
            ydst = n.scr_a[:, c4 * 2048:(c4 + 1) * 2048]
            n.emit("act", lambda a2, ydst=ydst, c4=c4: a2.dma_start(
                out=ydst, in_=n.d_yn.ap()[:, c4 * 128:(c4 + 1) * 128], transpose=True),
                reads=[n.buf("d_yn")], writes=[bscra], dma=True)
        for tt in range(NTT_OWN):
            n.MM(n.psE[:, 0:512], n.scr_c[0:8, tt * 128:(tt + 1) * 128],
                 n.bexp_sb[0:8, half * 512:(half + 1) * 512], True, True,
                 reads=[bcombf, n.buf("bexp")], writes=[n.psbuf("E")])
            for piece2 in range(2):
                dstp = (n.scr_b if piece2 == 0 else n.acc_hi)[:, tt * 256:(tt + 1) * 256]
                n.emit("dve", lambda v, dstp=dstp, piece2=piece2: v.tensor_add(
                    dstp, dstp, n.psE[:, piece2 * 256:(piece2 + 1) * 256]),
                    reads=[n.psbuf("E"), bacc], writes=[bacc])
        if half == 0:
            n.debug_dump("acc_bias", n.scr_b[:, 0:4096], (128, 4096), [n.buf("acc")])
        for e8 in range(E):
            if e8 % 2 == 0:
                warena = n.ws.ap()[:, 0:4096]
                bwsh = [[n.buf("ws_a")], [n.buf("ws_b")]]
            else:
                warena = n.scr_c.ap()[:, 4096:8192]
                bwsh = [[n.buf("expS0"), n.buf("expS1")],
                        [n.buf("atmp0"), n.buf("atmp1"), n.buf("expS3")]]
            for hcc in range(2):
                src = n.d_wexp.ap()[e8 * C:(e8 + 1) * C, :].rearrange(
                    "(cc p) c -> p cc c", p=128)[
                    :, hcc * 4:(hcc + 1) * 4, half * 512:(half + 1) * 512]
                dst = warena[:, hcc * 2048:(hcc + 1) * 2048].rearrange(
                    "p (cc w) -> p cc w", cc=4)
                n.emit("sp", lambda e, dst=dst, src=src: e.dma_start(out=dst, in_=src),
                       reads=[n.buf(f"d_wexp{e8}")], writes=bwsh[hcc], dma=True)
            for tt in range(NTT_OWN):
                ps, pb = (n.psC, pC) if tt % 2 == 0 else (n.psD, pD)
                for cc in range(NCC):
                    n.MM(ps[:, 0:512], n.xno_fm[cc][:, tt * 128:(tt + 1) * 128],
                         warena[:, cc * 512:(cc + 1) * 512], cc == 0, cc == NCC - 1,
                         reads=bwsh[cc // 4] + [n.buf(f"xno_fm{cc}")], writes=[pb])
                for piece2 in range(2):
                    dstp = (n.scr_b if piece2 == 0 else n.acc_hi)[:, tt * 256:(tt + 1) * 256]
                    n.emit("dve", lambda v, ps=ps, dstp=dstp, tt=tt, e8=e8, piece2=piece2:
                           v.scalar_tensor_tensor(
                               dstp, ps[:, piece2 * 256:(piece2 + 1) * 256],
                               n.comb[:, tt * 8 + e8:tt * 8 + e8 + 1], dstp,
                               ALU.mult, ALU.add),
                           reads=[pb, bcomb, bacc], writes=[bacc])

    def _wo_epilogue_half(self, half):
        n = self
        pC, pD, pE = n.psbuf("C"), n.psbuf("D"), n.psbuf("E")
        bacc = n.buf("acc")
        bg = n.buf("g")
        bscra = n.buf("scra")
        bsacc = n.buf("s_acc")
        for tt in range(NTT_OWN):
            for piece2 in range(2):
                srcp = (n.scr_b if piece2 == 0 else n.acc_hi)[:, tt * 256:(tt + 1) * 256]
                n.emit("act", lambda a2, tt=tt, piece2=piece2, srcp=srcp: a2.activation(
                    out=n.scr_c[:, tt * 512 + piece2 * 256:tt * 512 + (piece2 + 1) * 256],
                    in_=srcp, func=AF.Silu, bias=0.0, scale=1.0),
                    reads=[bacc], writes=[bg])
        bws = n.buf("scra")
        wo = lambda cc: n.scr_a[:, 8192 + cc * 512:8192 + (cc + 1) * 512]
        for cc in range(NCC):
            n.MM(n.psE[0:1, 0:512], n.lnb_bf[:, cc:cc + 1],
                 wo(cc), cc == 0, cc == NCC - 1,
                 reads=[bws, n.buf("lnb_bf")], writes=[pE])
        n.emit("dve", lambda v: v.tensor_copy(n.bb[0:1, 0:512], n.psE[0:1, 0:512]),
               reads=[pE], writes=[n.buf("bbst")])
        for cc in range(NCC):
            n.emit("dve", lambda v, cc=cc: v.tensor_scalar(
                wo(cc), wo(cc), n.vcol("ln_w", cc), None, ALU.mult),
                reads=[bws, n.vbuf("ln_w")], writes=[bws])
        for ch in range(2):
            if ch == 1:
                for c4 in range(4):
                    cc = 4 + c4
                    ydst = n.scr_a[:, c4 * 2048:(c4 + 1) * 2048]
                    n.emit("act", lambda a2, ydst=ydst, cc=cc: a2.dma_start(
                        out=ydst, in_=n.d_yn.ap()[:, cc * 128:(cc + 1) * 128], transpose=True),
                        reads=[n.buf("d_yn")], writes=[bscra], dma=True)
            for tt in range(NTT_OWN):
                ps, pb = (n.psC, pC) if tt % 2 == 0 else (n.psD, pD)
                for c4 in range(4):
                    cc = ch * 4 + c4
                    n.MM(ps[:, 0:512],
                         n.scr_a[:, c4 * 2048 + tt * 128:c4 * 2048 + (tt + 1) * 128],
                         wo(cc), c4 == 0, ch == 1 and c4 == 3,
                         reads=[bscra, bws], writes=[pb])
                if ch == 0:
                    n.MM(ps[:, 0:512], n.onesrow_sb[0:1, 0:128], n.bb[0:1, 0:512],
                         False, True, reads=[n.buf("onesrow"), n.buf("bbst")], writes=[pb])
                    n.emit("dve", lambda v, ps=ps, tt=tt: v.tensor_copy(
                        n.scr_b[:, tt * 256:(tt + 1) * 256], ps[:, 0:256]),
                        reads=[pb, bg], writes=[bsacc])
                    n.emit("dve", lambda v, ps=ps, tt=tt: v.tensor_copy(
                        n.acc_hi[:, tt * 256:(tt + 1) * 256], ps[:, 256:512]),
                        reads=[pb, bg], writes=[bsacc])
                else:
                    xt = n.xtile[:, (tt % 2) * 512:(tt % 2) * 512 + 512]
                    bxt = n.buf(f"xt{tt % 2}")
                    n.emit("sp", lambda e, xt=xt, tt=tt: e.dma_start(
                        out=xt, in_=n.in_x_own.ap()[tt * 128:(tt + 1) * 128,
                                                    half * 512:(half + 1) * 512]),
                        writes=[bxt], dma=True)
                    ot = n.outt[:, (tt % 2) * 512:(tt % 2) * 512 + 512]
                    bot = n.buf(f"ot{tt % 2}")
                    for piece2 in range(2):
                        sap = (n.scr_b if piece2 == 0 else n.acc_hi)[:, tt * 256:(tt + 1) * 256]
                        n.emit("dve", lambda v, ps=ps, ot=ot, sap=sap, piece2=piece2:
                               v.tensor_add(ot[:, piece2 * 256:(piece2 + 1) * 256],
                                            ps[:, piece2 * 256:(piece2 + 1) * 256], sap),
                               reads=[pb, bsacc], writes=[bot])
                        n.emit("dve", lambda v, ot=ot, tt=tt, piece2=piece2: v.tensor_mul(
                            ot[:, piece2 * 256:(piece2 + 1) * 256],
                            ot[:, piece2 * 256:(piece2 + 1) * 256],
                            n.scr_c[:, tt * 512 + piece2 * 256:tt * 512 + (piece2 + 1) * 256]),
                            reads=[bg, bot], writes=[bot])
                        n.emit("dve", lambda v, ot=ot, xt=xt, piece2=piece2: v.tensor_add(
                            ot[:, piece2 * 256:(piece2 + 1) * 256],
                            ot[:, piece2 * 256:(piece2 + 1) * 256],
                            xt[:, piece2 * 256:(piece2 + 1) * 256]),
                            reads=[bot, bxt], writes=[bot])
                    n.emit("sp", lambda e, tt=tt, ot=ot: e.dma_start(
                        out=n.out.ap()[tt * 128:(tt + 1) * 128, half * 512:(half + 1) * 512],
                        in_=ot), reads=[bot], writes=[n.buf("out_w")], dma=True)

    def ffn_phase(self):
        n = self
        n.alias_fence("acc", ["y_sb", "ysq", "s_tmp", "utmp", "gnstats", "sqdump",
                              "lnx0", "lnx1", "lnx2", "lnsq0"]
                      + [f"xi_{c}_{cc}" for c in (0, 2048) for cc in range(NCC)]
                      + ["tanh1_0", "tanh1_1", "eta_0", "eta_1", "pm0", "pm1", "yn_st"])
        n.alias_fence("xt0", ["S0", "S1"])
        n.alias_fence("xt1", ["S0", "S1", "Sbf"])
        n.alias_fence("scra", ["ws_wo"])
        for nm in ("g", "expS0", "expS1", "expS2", "expS3", "atmp0", "atmp1",
                   "comb_fm", "vstage"):
            n.alias_fence(nm, [f"cs{i}" for i in range(8)]
                          + [f"lnout{i}" for i in range(4)])
        n.emit("dve", lambda v: v.tensor_copy(n.lnb_bf[:, :], n.vec_sb[:, 48:56]),
               reads=[n.vbuf("ln_b")], writes=[n.buf("lnb_bf")])
        n.debug_dump("comb", n.comb[:, :], (128, 128), [n.buf("comb")])
        self._vaug_build()
        for half in range(2):
            n.alias_fence("acc", ["s_acc"])
            for nm in ("expS0", "expS1", "expS2", "expS3", "atmp0", "atmp1",
                       "comb_fm"):
                n.alias_fence(nm, ["g"])
            n.alias_fence("g", ["expS0", "expS1", "expS2", "expS3",
                                "atmp0", "atmp1"])
            self._comb_fm_build()
            for hp in range(4 * half, 4 * half + 4):
                self._attn_pair(hp, half)
            if half == 0:
                n.debug_dump("acc_attn", n.scr_b[:, 0:4096], (128, 4096), [n.buf("acc")])
            self._moe_half(half)
            if half == 0:
                n.debug_dump("acc_moe", n.scr_b[:, 0:4096], (128, 4096), [n.buf("acc")])
            self._wo_epilogue_half(half)

    def debug_dump_dram(self, key, dram_t, shape, after_bufs, dt=F32):
        if key not in self.debug_keys:
            return
        t = self.nc.dram_tensor("dbg_" + key, list(shape), dt, kind="ExternalOutput")
        self.dbg[key] = t
        return self.emit("sp", lambda e: e.dma_start(out=t.ap(), in_=dram_t.ap()),
                         reads=list(after_bufs), writes=[self.buf("dbg_" + key)], dma=True)

    def debug_dump(self, key, sbuf_ap, shape, after_bufs, dt=F32):
        if key not in self.debug_keys:
            return
        t = self.nc.dram_tensor("dbg_" + key, list(shape), dt, kind="ExternalOutput")
        self.dbg[key] = t
        tok = self.emit("sp", lambda e: e.dma_start(out=t.ap(), in_=sbuf_ap),
                        reads=list(after_bufs), writes=[self.buf("dbg_" + key)], dma=True)
        return tok

    def build(self, phases):
        self._init_consts()
        self.prologue()
        for ph in phases:
            ph()
        finals = []
        bout = self.buf("out_w")
        if bout.w is not None:
            finals.append(bout.w)
        for key in self.dbg:
            finals.append(self.buf("dbg_" + key).w)
        runner = self.p.finalize(None, finals)
        nc = self.nc
        with nc.Block() as block:
            block.tensor(runner("pe"))
            block.scalar(runner("act"))
            block.vector(runner("dve"))
            block.gpsimd(runner("pool"))
            block.sync(runner("sp"))
        self.stack.close()
        return nc


def kernel(**inputs):
    k = K()
    nc = k.build([k.ln_phase, k.gate_phase, k.tmix_phase, k.ffn_phase])
    in_maps = [_slices_for_core(inputs, c) for c in range(8)]
    res = run_bass_kernel_spmd(nc, in_maps, core_ids=list(range(8)))
    out = np.zeros((B, T, C), np.float32)
    for c in range(8):
        b, half = c // 2, c % 2
        out[b, half * OWN:(half + 1) * OWN] = res.results[c]["out"]
    return out


# version 26 (final)
# speedup vs baseline: 1.0437x; 1.0437x over previous
import numpy as np
import ml_dtypes
from contextlib import ExitStack

import concourse.bass as bass
import concourse.mybir as mybir
from concourse.bass_utils import run_bass_kernel_spmd

F32 = mybir.dt.float32
BF16 = mybir.dt.bfloat16
AF = mybir.ActivationFunctionType
ALU = mybir.AluOpType
AX = mybir.AxisListType
BFNP = ml_dtypes.bfloat16

B, T, C = 4, 4096, 1024
OWN = T // 2
HEAD = 64
H = C // HEAD
NPAIR = H // 2
LORA_MIX = 32
LORA_DECAY = 64
E = 8
GN_EPS = 0.00064
LN_EPS = 1e-5
CK = 128
NCK_OWN = OWN // CK
NCK_ST = OWN // CK
NCC = C // 128
TB = 512
NTB_OWN = OWN // TB
NTT_OWN = OWN // 128
ATT_SCALE = 0.125
ATT_SHIFT = -10.0

ENGINES = ("pe", "act", "dve", "pool", "sp")


class Prog:

    def __init__(self, nc):
        self.nc = nc
        self.ops = {e: [] for e in ENGINES}
        self.marked = {e: set() for e in ENGINES}
        self.dma_tokens = set()

    def op(self, eng, fn, deps=(), dma=False):
        clean = [d for d in deps if d is not None]
        idx = len(self.ops[eng])
        self.ops[eng].append((fn, clean, 16 if dma else 1))
        if dma:
            self.marked[eng].add(idx)
            self.dma_tokens.add((eng, idx))
        return (eng, idx)

    @staticmethod
    def _implied(e, e2):
        return e == e2 == "pe"

    def finalize(self, engs, final_tokens):
        nc = self.nc
        for e in ENGINES:
            for i, (_, deps, _) in enumerate(self.ops[e]):
                for (e2, j) in deps:
                    if not self._implied(e, e2):
                        self.marked[e2].add(j)
        for d in final_tokens:
            self.marked[d[0]].add(d[1])
        self.semkey = {}
        self.cum = {}
        for e in ENGINES:
            counts = {}
            keys = []
            cums = []
            for i, (_, _, inc) in enumerate(self.ops[e]):
                key = "actd" if (e == "act" and (e, i) in self.dma_tokens) else e
                if i in self.marked[e]:
                    counts[key] = counts.get(key, 0) + inc
                keys.append(key)
                cums.append(counts.get(key, 0))
            self.semkey[e] = keys
            self.cum[e] = cums
        sems = dict(self._sems)

        def run_engine(e):
            def body(engine):
                hw = {}
                def wait(e2, j):
                    v = self.cum[e2][j]
                    key = self.semkey[e2][j]
                    assert j in self.marked[e2]
                    if v > hw.get(key, 0):
                        engine.wait_ge(sems[key], v)
                        hw[key] = v
                for i, (fn, deps, inc) in enumerate(self.ops[e]):
                    for (e2, j) in deps:
                        if self._implied(e, e2):
                            continue
                        wait(e2, j)
                    ins = fn(engine)
                    if i in self.marked[e]:
                        assert ins is not None, f"marked op {e}[{i}] returned None"
                        ins.then_inc(sems[self.semkey[e][i]], inc)
                if e == "sp":
                    for (e2, j) in final_tokens:
                        wait(e2, j)
            return body
        return run_engine


class Buf:
    __slots__ = ("w", "r", "sc")

    def __init__(self):
        self.w = None
        self.r = []
        self.sc = False

    def rd(self):
        return [self.w] if self.w is not None else []

    def wr(self):
        d = list(self.r)
        if self.w is not None:
            d.append(self.w)
        return d

    def did_read(self, tok):
        for i, (e, j) in enumerate(self.r):
            if e == tok[0]:
                if tok[1] > j:
                    self.r[i] = tok
                return
        self.r.append(tok)

    def did_write(self, tok):
        self.w = tok
        self.r = []


SCALAR_BUFS = {"consts", "lnst0", "lnst1", "gst", "comb", "attn_st",
               "dsb", "gnstats"}


def _slices_for_core(inputs, core):
    b = core // 2
    half = core % 2
    o0 = half * OWN

    g = lambda k: np.asarray(inputs[k], dtype=np.float32)
    x = g("L0_attn_query")[b]
    x_own = x[o0:o0 + OWN]

    col8 = lambda v: np.ascontiguousarray(np.asarray(v, np.float32).reshape(NCC, 128).T)

    d = {
        "x_full": np.ascontiguousarray(x),
        "x_own": np.ascontiguousarray(x_own),
        "is_odd": np.full((128, 1), float(half), np.float32),
        "maa_x": col8(g("L0_attn_time_maa_x")),
        "maa_w": col8(g("L0_attn_time_maa_w")),
        "maa_k": col8(g("L0_attn_time_maa_k")),
        "maa_v": col8(g("L0_attn_time_maa_v")),
        "maa_r": col8(g("L0_attn_time_maa_r")),
        "w1": g("L0_attn_time_maa_w1"),
        "w2": g("L0_attn_time_maa_w2").reshape(128, 1024),
        "tdecay_row": g("L0_attn_time_decay").reshape(1, C),
        "dw1": g("L0_attn_time_decay_w1"),
        "dw2": g("L0_attn_time_decay_w2"),
        "w_r": g("L0_attn_W_r"),
        "w_k": g("L0_attn_W_k"),
        "w_v": g("L0_attn_W_v"),
        "w_o": g("L0_attn_W_o"),
        "ln_w": col8(g("L0_attn_ln_w")),
        "ln_b": col8(g("L0_attn_ln_b")),
        "w_qkv": g("L0_ffn_W_qkv"),
        "bq_col": col8(g("L0_ffn_b_qkv")[0:C]),
        "bk_col": col8(g("L0_ffn_b_qkv")[C:2 * C]),
        "bv_row": g("L0_ffn_b_qkv")[2 * C:3 * C].reshape(1, C),
        "w_gate": g("L0_ffn_W_gate"),
        "b_gate": g("L0_ffn_b_gate").reshape(1, E),
        "w_exp": g("L0_ffn_W_experts").reshape(E * C, C),
        "b_exp": g("L0_ffn_b_experts"),
    }
    fa = g("L0_attn_time_faaaa")
    useg = np.zeros((128, H), np.float32)
    for p in range(NPAIR):
        useg[0:64, 2 * p] = fa[2 * p]
        useg[64:128, 2 * p + 1] = fa[2 * p + 1]
    d["useg"] = useg
    return d


def _consts():
    ident = np.eye(128, dtype=BFNP)
    ident_f = np.eye(128, dtype=np.float32)
    ltneg = -np.triu(np.ones((128, 128), np.float32)).astype(BFNP)
    mstrict = np.triu(np.ones((128, 128), np.float32), 1)
    mstrict4 = np.tile(mstrict, (1, 4)).astype(BFNP)
    ones_row = np.ones((1, 128), BFNP)
    return ident, ident_f, ltneg, mstrict4, ones_row


class K:

    def __init__(self, debug_keys=()):
        nc = bass.Bass("TRN2", target_bir_lowering=False, debug=False)
        self.nc = nc
        self.p = Prog(nc)
        self.stack = ExitStack()
        self.debug_keys = set(debug_keys)
        self.dbg = {}

        ap = lambda name, shape, dt=F32, kind="ExternalInput": nc.dram_tensor(
            name, list(shape), dt, kind=kind)
        self.in_x_full = ap("x_full", (T, C))
        self.in_x_own = ap("x_own", (OWN, C))
        self.in_is_odd = ap("is_odd", (128, 1))
        for nm, sh in [
            ("maa_x", (128, 8)), ("maa_w", (128, 8)), ("maa_k", (128, 8)),
            ("maa_v", (128, 8)), ("maa_r", (128, 8)),
            ("w1", (C, 128)), ("w2", (128, C)),
            ("tdecay_row", (1, C)), ("dw1", (C, 64)), ("dw2", (64, C)),
            ("w_r", (C, C)), ("w_k", (C, C)), ("w_v", (C, C)), ("w_o", (C, C)),
            ("ln_w", (128, 8)), ("ln_b", (128, 8)),
            ("w_qkv", (C, 3 * C)), ("bq_col", (128, 8)), ("bk_col", (128, 8)),
            ("bv_row", (1, C)), ("w_gate", (C, E)), ("b_gate", (1, E)),
            ("w_exp", (E * C, C)), ("b_exp", (E, C)), ("useg", (128, H)),
        ]:
            setattr(self, "in_" + nm, ap(nm, sh))
        self.out = ap("out", (OWN, C), kind="ExternalOutput")

        ident, ident_f, ltneg, mstrict4, ones_row = _consts()
        self.c_ident = nc.inline_tensor(ident, "c_ident")
        self.c_ident_f = nc.inline_tensor(ident_f, "c_ident_f")
        self.c_ltneg = nc.inline_tensor(ltneg, "c_ltneg")
        self.c_mstrict4 = nc.inline_tensor(mstrict4, "c_mstrict4")
        self.c_ones_row = nc.inline_tensor(ones_row, "c_ones_row")

        dram = lambda name, shape, dt: nc.dram_tensor(name, list(shape), dt, kind="Internal")
        self.d_xn = dram("d_xn", (T, C), BF16)
        self.d_xno = dram("d_xno", (OWN, C), BF16)
        self.d_yn = dram("d_yn", (OWN, C), BF16)
        self.d_wr = dram("d_wr", (C, C), BF16)
        self.d_wk = dram("d_wk", (C, C), BF16)
        self.d_wv = dram("d_wv", (C, C), BF16)
        self.d_wo = dram("d_wo", (C, C), BF16)
        self.d_wqkv = dram("d_wqkv", (C, 3 * C), BF16)
        self.d_wexp = dram("d_wexp", (E * C, C), BF16)
        self.d_vaug = dram("d_vaug", (T, 2 * 520), BF16)

        self.p._sems = {}
        for e in list(ENGINES) + ["actd"]:
            self.p._sems[e] = self.stack.enter_context(nc.semaphore("sem_" + e))

        sb = self._sb
        self.xn_fm_t = sb("xn_fm_t", (128, NCC * T), BF16)
        self.xno_fm_t = sb("xno_fm_t", (128, NCC * OWN), BF16)
        self.xn_fm = [self.xn_fm_t[:, i * T:(i + 1) * T] for i in range(NCC)]
        self.xno_fm = [self.xno_fm_t[:, i * OWN:(i + 1) * OWN] for i in range(NCC)]
        self.scr_a = sb("scr_a", (128, 12288), BF16)
        self.scr_b = sb("scr_b", (128, 4096), F32)
        self.scr_c = sb("scr_c", (128, 8192), BF16)
        self.xi = sb("xi_tb", (128, 4096), BF16)
        self.tanh1 = sb("tanh1", (128, 1024), BF16)
        self.eta = sb("eta", (64, 1024), BF16)
        self.pmask = sb("pmask_yn", (128, 2048), BF16)
        self.S0 = sb("S0", (128, 512), F32)
        self.S1 = sb("S1", (128, 512), F32)
        self.Sbf = sb("Sbf", (128, 512), BF16)
        self.ws = sb("ws", (128, 4096), BF16)
        self.stats = sb("stats", (128, 160), F32)
        self.xxb = sb("xxb", (128, 2048), BF16)
        self.lnb_bf = sb("lnb_bf", (128, 8), BF16)
        self.outt = sb("outt", (128, 1024), F32)
        xi_addr = nc.lookup_mloc(self.xi).addr
        self.acc_hi = nc.alloc_sbuf_tensor_at("acc_hi", [128, 4096], F32, offset=xi_addr)
        self.xtile = nc.alloc_sbuf_tensor_at("xtile", [128, 1024], F32, offset=xi_addr + 16384)
        assert nc.lookup_mloc(self.tanh1).addr == xi_addr + 8192
        assert nc.lookup_mloc(self.pmask).addr == xi_addr + 12288
        assert nc.lookup_mloc(self.S0).addr == xi_addr + 16384
        assert nc.lookup_mloc(self.Sbf).addr == xi_addr + 20480
        self.comb = sb("comb", (128, 128), F32)
        self.dsb = sb("dsb", (128, 32), F32)
        self.eal = sb("eal", (128, 8), F32)
        self.prev = sb("prev", (128, 8), F32)
        self.bb = sb("bb", (1, 512), BF16)
        self.consts = sb("consts", (128, 16), F32)
        self.ident_sb = sb("ident_sb", (128, 128), BF16)
        self.ident_f32 = sb("ident_f32", (128, 128), F32)
        self.ltneg_sb = sb("ltneg_sb", (128, 128), BF16)
        self.mstrict_sb = sb("mstrict_sb", (128, 512), BF16)
        self.onesrow_sb = sb("onesrow_sb", (1, 128), BF16)
        self.w1_sb = sb("w1_sb", (128, 8 * 128), BF16)
        self.w2_sb = sb("w2_sb", (128, C), BF16)
        self.dw1_sb = sb("dw1_sb", (128, 8 * 64), BF16)
        self.dw2_sb = sb("dw2_sb", (64, C), BF16)
        self.vec_sb = sb("vec_sb", (128, 96), F32)
        self.tdec_row = sb("s_tdec_row", (1, C), BF16)
        self.bv_row = sb("s_bv_row", (1, C), BF16)
        self.bgate_row = sb("s_bgate_row", (1, E), BF16)
        self.bexp_sb = sb("s_bexp", (E, C), BF16)
        self.wgate_sb = sb("wgate_sb", (128, 8 * E), BF16)
        self.useg_sb = sb("useg_sb", (128, H), BF16)

        ps = lambda name, shape, dt=F32: self.stack.enter_context(nc.psum_tensor(name, list(shape), dt))
        self.psA = ps("psA", (128, 1024))
        self.psB = ps("psB", (128, 1024))
        self.psC = ps("psC", (128, 512))
        self.psD = ps("psD", (128, 512))
        self.psE = ps("psE", (128, 512))
        self.psG = ps("psG", (128, 1024), BF16)

        self.bufs = {}

    def _sb(self, name, shape, dt):
        return self.stack.enter_context(self.nc.sbuf_tensor(name, list(shape), dt))

    def buf(self, name):
        if name not in self.bufs:
            self.bufs[name] = Buf()
            if name in SCALAR_BUFS:
                self.bufs[name].sc = True
        return self.bufs[name]

    def emit(self, eng, fn, reads=(), writes=(), extra=(), dma=False):
        deps = list(extra)
        dmatok = self.p.dma_tokens

        def keep(d, b):
            return True

        for b in reads:
            deps += [d for d in b.rd() if keep(d, b)]
        for b in writes:
            deps += [d for d in b.wr() if keep(d, b)]
        if dma and eng in ("act", "pool", "sp"):
            last = getattr(self, "_last_dma", {}).get(eng)
            if last is not None:
                deps.append(last)
        tok = self.p.op(eng, fn, deps, dma=dma)
        if dma and eng in ("act", "pool", "sp"):
            if not hasattr(self, "_last_dma"):
                self._last_dma = {}
            self._last_dma[eng] = tok
        for b in reads:
            b.did_read(tok)
        for b in writes:
            b.did_write(tok)
        return tok

    CONST_COLS = {-1.0: 0, 0.125: 1, ATT_SHIFT: 2, LN_EPS: 3, GN_EPS: 4,
                  1.0 / 1024: 5, 1.0 / 64: 6, 1.0: 7, 0.5: 8, -1.0 / 64: 9}

    def ccol(self, val):
        return self.consts[:, self.CONST_COLS[val]:self.CONST_COLS[val] + 1]

    def _init_consts(self):
        bcon = self.buf("consts")
        for val, i in self.CONST_COLS.items():
            self.emit("dve", lambda v, val=val, i=i: v.memset(self.consts[:, i:i + 1], val),
                      writes=[bcon])

    def prologue(self):
        n = self

        def ld(eng, dst_ap, src_ap, bufname, deps=()):
            b = n.buf(bufname)
            return n.emit(eng, lambda e, d=dst_ap, s=src_ap: e.dma_start(out=d, in_=s),
                          writes=[b], extra=deps, dma=True)

        prev = [None]

        def ldp(dst_ap, src_ap, bufname):
            t = ld("pool", dst_ap, src_ap, bufname,
                   deps=[prev[0]] if prev[0] else ())
            prev[0] = t

        ldp(n.useg_sb[:, :], n.in_useg.ap(), "useg")
        ldp(n.dw2_sb[:, :], n.in_dw2.ap(), "dw2")
        ldp(n.w2_sb[:, :], n.in_w2.ap(), "w2")
        ldp(n.w1_sb.ap().rearrange("p (cc l) -> p cc l", cc=8),
            n.in_w1.ap().rearrange("(cc p) l -> p cc l", p=128), "w1")
        ldp(n.dw1_sb.ap().rearrange("p (cc l) -> p cc l", cc=8),
            n.in_dw1.ap().rearrange("(cc p) l -> p cc l", p=128), "dw1")
        ldp(n.wgate_sb.ap().rearrange("p (cc e) -> p cc e", cc=8),
            n.in_w_gate.ap().rearrange("(cc p) e -> p cc e", p=128), "wgate")
        ldp(n.tdec_row[:, :], n.in_tdecay_row.ap(), "tdec_row")
        ldp(n.bv_row[:, :], n.in_bv_row.ap(), "bv_row")
        ldp(n.bgate_row[:, :], n.in_b_gate.ap(), "bgate_row")
        ldp(n.bexp_sb[:, :], n.in_b_exp.ap(), "bexp")
        ldp(n.onesrow_sb[:, :], n.c_ones_row.ap(), "onesrow")
        for nm in ("wk", "wv", "wr", "wqkv", "wo"):
            src = getattr(n, "in_w_" + nm[1:]) if nm in ("wr", "wk", "wv", "wo") else n.in_w_qkv
            dst = getattr(n, "d_" + nm)
            ldp(dst.ap(), src.ap(), "d_" + nm)
        for e8 in range(E):
            ldp(n.d_wexp.ap()[e8 * C:(e8 + 1) * C, :],
                n.in_w_exp.ap()[e8 * C:(e8 + 1) * C, :], f"d_wexp{e8}")

        ld("sp", n.ident_sb[:, :], n.c_ident.ap(), "ident")
        ld("sp", n.ident_f32[:, :], n.c_ident_f.ap(), "ident_f")
        ld("sp", n.ltneg_sb[:, :], n.c_ltneg.ap(), "ltneg")
        ld("sp", n.mstrict_sb[:, :], n.c_mstrict4.ap(), "mstrict")
        vecs = [("maa_x", 0), ("maa_w", 8), ("maa_k", 16), ("maa_v", 24),
                ("maa_r", 32), ("ln_w", 40), ("ln_b", 48), ("bq_col", 56),
                ("bk_col", 64)]
        for nm, off in vecs:
            key = {"bq_col": "bq", "bk_col": "bk"}.get(nm, nm)
            ld("sp", n.vec_sb[:, off:off + 8], getattr(n, "in_" + nm).ap(), "vec_" + key)
        ld("sp", n.vec_sb[:, 72:73], n.in_is_odd.ap(), "vec_is_odd")

    def vcol(self, name, i=0):
        off = {"maa_x": 0, "maa_w": 8, "maa_k": 16, "maa_v": 24, "maa_r": 32,
               "ln_w": 40, "ln_b": 48, "bq": 56, "bk": 64, "is_odd": 72}[name]
        return self.vec_sb[:, off + i:off + i + 1]

    def vbuf(self, name):
        return self.buf("vec_" + name)

    def ln_phase(self):
        n = self
        x_t = n.scr_b[:, 0:3 * C]
        xsq = n.scr_b[:, 3 * C:4 * C]
        st = n.stats[:, 0:16]
        xnout = n.scr_c[:, 0:4 * C]
        bx = [n.buf(f"lnx{i}") for i in range(3)]
        bsq = n.buf("lnsq0")
        bst = [n.buf(f"lnst{i}") for i in range(2)]
        bout = [n.buf(f"lnout{i}") for i in range(4)]
        bfm = [n.buf(f"xn_fm{cc}") for cc in range(NCC)]
        bfo = [n.buf(f"xno_fm{cc}") for cc in range(NCC)]
        pG = n.psbuf("G")
        bcon = n.buf("consts")

        jobs = []
        for tt in range(T // 128):
            jobs.append((n.in_x_full.ap()[tt * 128:(tt + 1) * 128, :], True, tt))
        for tt in range(OWN // 128):
            jobs.append((n.in_x_own.ap()[tt * 128:(tt + 1) * 128, :], False, tt))

        for j, (src, full, tt) in enumerate(jobs):
            k = j % 2
            k3 = j % 3
            k4 = j % 4
            xt = x_t[:, k3 * C:(k3 + 1) * C]
            s1 = st[:, k * 8 + 0:k * 8 + 1]
            s2 = st[:, k * 8 + 1:k * 8 + 2]
            mean = st[:, k * 8 + 2:k * 8 + 3]
            var = st[:, k * 8 + 3:k * 8 + 4]
            rstd = st[:, k * 8 + 4:k * 8 + 5]
            m2 = st[:, k * 8 + 5:k * 8 + 6]
            xo = xnout[:, k4 * C:(k4 + 1) * C]
            n.emit("sp", lambda e, xt=xt, src=src: e.dma_start(out=xt, in_=src),
                   writes=[bx[k3]], dma=True)
            n.emit("dve", lambda v, xt=xt, s1=s1: v.reduce_sum(s1, xt, axis=AX.X),
                   reads=[bx[k3]], writes=[bst[k]])
            n.emit("act", lambda a, xt=xt, s2=s2: a.activation(
                out=xsq, in_=xt, func=AF.Square, bias=0.0, scale=1.0, accum_out=s2),
                   reads=[bx[k3]], writes=[bsq, bst[k]])
            n.emit("dve", lambda v, s1=s1, mean=mean: v.tensor_scalar(
                mean, s1, n.ccol(1.0 / 1024), None, ALU.mult),
                   reads=[bst[k], bcon], writes=[bst[k]])
            n.emit("dve", lambda v, mean=mean, m2=m2: v.tensor_mul(m2, mean, mean),
                   reads=[bst[k]], writes=[bst[k]])
            n.emit("dve", lambda v, s2=s2, m2=m2, var=var: v.scalar_tensor_tensor(
                var, s2, n.ccol(1.0 / 1024), m2, ALU.mult, ALU.subtract),
                   reads=[bst[k], bcon], writes=[bst[k]])
            n.emit("act", lambda a, var=var, rstd=rstd: a.activation(
                out=rstd, in_=var, func=AF.Sqrt, bias=n.ccol(LN_EPS), scale=1.0),
                   reads=[bst[k], bcon], writes=[bst[k]])
            n.emit("dve", lambda v, rstd=rstd: v.reciprocal(rstd, rstd),
                   reads=[bst[k]], writes=[bst[k]])
            n.emit("dve", lambda v, xt=xt, xo=xo, mean=mean, rstd=rstd: v.tensor_scalar(
                xo, xt, mean, rstd, ALU.subtract, ALU.mult),
                   reads=[bx[k3], bst[k]], writes=[bout[k4]])
            for cc in range(NCC):
                n.emit("pe", lambda t, xo=xo, cc=cc: t.transpose(
                    n.psG[:, cc * 128:(cc + 1) * 128],
                    xo[:, cc * 128:(cc + 1) * 128], n.ident_sb[:, :]),
                    reads=[bout[k4], n.buf("ident")], writes=[pG])
            fmt = n.xn_fm_t if full else n.xno_fm_t
            SEQ = T if full else OWN
            dst3 = fmt.ap().rearrange("p (cc t) -> p cc t", cc=NCC)[
                :, :, tt * 128:(tt + 1) * 128]
            n.emit("act", lambda a, dst3=dst3: a.activation(
                out=dst3, in_=n.psG.ap().rearrange("p (cc t) -> p cc t", cc=NCC),
                func=AF.Copy, bias=0.0, scale=1.0),
                reads=[pG], writes=(bfm if full else bfo))
        bprev = n.buf("prev")
        for cc in range(NCC):
            n.emit("dve", lambda v, cc=cc: v.tensor_scalar(
                n.prev[:, cc:cc + 1], n.xn_fm[cc][:, OWN - 1:OWN],
                n.vcol("is_odd"), None, ALU.mult),
                   reads=[n.buf(f"xn_fm{cc}"), n.vbuf("is_odd")], writes=[bprev])
        for cc in range(NCC):
            n.debug_dump(f"xn_fm{cc}", n.xn_fm[cc][:, :], (128, T),
                         [n.buf(f"xn_fm{cc}")], BF16)
            n.debug_dump(f"xno_fm{cc}", n.xno_fm[cc][:, :], (128, OWN),
                         [n.buf(f"xno_fm{cc}")], BF16)

    def MM(self, out_ap, lhsT_ap, rhs_ap, start, stop, pos=None, reads=(), writes=()):
        return self.emit("pe", lambda t: t.matmul(
            out_ap, lhsT_ap, rhs_ap, start=start, stop=stop, tile_position=pos),
            reads=reads, writes=writes)

    def psbuf(self, name):
        return self.buf("ps_" + name)

    def _xi_pass(self, stream, t0, width, own, tslot=0):
        n = self
        self._xi_par = getattr(self, "_xi_par", 0) ^ 1
        col0 = 0 if width > 256 else self._xi_par * 2048
        bxis = [n.buf(f"xi_{col0}_{cc}") for cc in range(NCC)]
        bxx = n.buf("xxb")
        f = {"w": 0, "k": 1, "v": 2, "r": 3}.get(stream)
        if width > 256:
            for cc in range(NCC):
                bxis[cc].r += n.buf(f"xi_2048_{cc}").r + n.buf(f"xi_0_{cc}").r
        big = n.xno_fm_t if own else n.xn_fm_t
        SEQ = OWN if own else T
        bsrc = [n.buf(("xno_fm" if own else "xn_fm") + str(cc)) for cc in range(NCC)]
        nh = NCC * width // 2048
        ccph = NCC // nh
        for hxx in range(nh):
            c0 = hxx * ccph
            xdst = n.xxb.ap()[:, 0:ccph * width].rearrange("p (cc w) -> p cc w", cc=ccph)                 if nh == 2 and False else None
            xdst = n.xxb.ap()[:, hxx * 0:2048][:, 0:ccph * width].rearrange(
                "p (cc w) -> p cc w", cc=ccph)
            sview = big.ap()[:, c0 * SEQ:(c0 + ccph) * SEQ].rearrange(
                "p (cc t) -> p cc t", cc=ccph)
            if t0 == 0:
                n.emit("dve", lambda v, xdst=xdst, sview=sview: v.tensor_sub(
                    xdst[:, :, 1:width], sview[:, :, 0:width - 1], sview[:, :, 1:width]),
                    reads=bsrc, writes=[bxx])
                if own:
                    n.emit("dve", lambda v, xdst=xdst, sview=sview, c0=c0: v.tensor_sub(
                        xdst[:, :, 0:1],
                        n.prev.ap()[:, c0:c0 + ccph].rearrange("p c -> p c 1" if False else "p (c o) -> p c o", o=1),
                        sview[:, :, 0:1]),
                        reads=bsrc + [n.buf("prev")], writes=[bxx])
                else:
                    n.emit("dve", lambda v, xdst=xdst, sview=sview: v.tensor_scalar(
                        xdst[:, :, 0:1], sview[:, :, 0:1], -1.0, None, ALU.mult),
                        reads=bsrc, writes=[bxx])
            else:
                n.emit("dve", lambda v, xdst=xdst, sview=sview: v.tensor_sub(
                    xdst[:, :, 0:width], sview[:, :, t0 - 1:t0 + width - 1],
                    sview[:, :, t0:t0 + width]),
                    reads=bsrc, writes=[bxx])
            for ccl in range(ccph):
                cc = c0 + ccl
                xx_ap = n.xxb[:, ccl * width:(ccl + 1) * width]
                xi_ap = n.xi[:, col0 + cc * width:col0 + (cc + 1) * width]
                src2 = big[:, cc * SEQ + t0:cc * SEQ + t0 + width]
                if stream == "x":
                    n.emit("dve", lambda v, xx_ap=xx_ap, xi_ap=xi_ap, src2=src2, cc=cc:
                           v.scalar_tensor_tensor(
                               xi_ap, xx_ap, n.vcol("maa_x", cc), src2,
                               ALU.mult, ALU.add),
                           reads=[bxx, bsrc[cc], n.vbuf("maa_x")], writes=[bxis[cc]])
                    continue
                mslot = cc % 4
                mtens = n.psA if mslot < 2 else n.psB
                mps = mtens[:, (mslot % 2) * 512:(mslot % 2) * 512 + width]
                mb = n.psbuf(("A", "A", "B", "B")[mslot] + str(mslot % 2))
                n.MM(mps, n.w2_sb[f * 32:(f + 1) * 32, cc * 128:(cc + 1) * 128],
                     n.tanh1[f * 32:(f + 1) * 32, tslot * 512:tslot * 512 + width],
                     True, True, pos=(32 * f, 0),
                     reads=[n.buf("w2"), n.buf(f"tanh1_{tslot}")], writes=[mb])
                nmcol = "maa_" + stream
                n.emit("dve", lambda v, mps=mps, xx_ap=xx_ap, xi_ap=xi_ap, cc=cc, nmcol=nmcol:
                       v.scalar_tensor_tensor(xi_ap, mps, n.vcol(nmcol, cc), xx_ap,
                                              ALU.add, ALU.mult),
                       reads=[mb, bxx, n.vbuf(nmcol)], writes=[bxis[cc]])
            if stream != "x":
                xi3 = n.xi.ap()[:, col0 + c0 * width:col0 + (c0 + ccph) * width].rearrange(
                    "p (cc w) -> p cc w", cc=ccph)
                n.emit("dve", lambda v, xi3=xi3, sview=sview: v.tensor_add(
                    xi3, xi3, sview[:, :, t0:t0 + width]),
                    reads=bsrc + [bxis[c0 + i] for i in range(ccph)],
                    writes=[bxis[c0 + i] for i in range(ccph)])
        return col0, bxis

    def _proj_fm(self, wdram, wbuf_name, out_col0, xic, width):
        n = self
        xcol0, bxis = xic
        for q in range(4):
            slot = q % 2
            bws = n.buf("ws_a" if slot == 0 else "ws_b")
            ws_ap = n.ws.ap()[:, slot * 2048:(slot + 1) * 2048].rearrange(
                "p (cc w) -> p cc w", cc=8)
            src = wdram.ap().rearrange("(cc p) c -> p cc c", p=128)[:, :, q * 256:(q + 1) * 256]
            n.emit("sp", lambda e, ws_ap=ws_ap, src=src: e.dma_start(out=ws_ap, in_=src),
                   reads=[n.buf(wbuf_name)], writes=[bws], dma=True)
            for ic2 in range(2):
                ic = q * 2 + ic2
                ps = n.psC if ic2 == 0 else n.psD
                pb = n.psbuf("C" if ic2 == 0 else "D")
                for cc in range(NCC):
                    n.MM(ps[:, 0:width],
                         n.ws[:, slot * 2048 + cc * 256 + ic2 * 128:
                              slot * 2048 + cc * 256 + ic2 * 128 + 128],
                         n.xi[:, xcol0 + cc * width:xcol0 + (cc + 1) * width],
                         cc == 0, cc == NCC - 1,
                         reads=[bws, bxis[cc]], writes=[pb])
                dst = n.scr_a[:, out_col0 + ic * width:out_col0 + (ic + 1) * width]
                n.emit("act", lambda a, dst=dst, ps=ps: a.activation(
                    out=dst, in_=ps[:, 0:width], func=AF.Copy, bias=0.0, scale=1.0),
                    reads=[pb], writes=[n.buf("scra")])

    def _proj_tm(self, wdram, wbuf_name, out_col0, xic, width):
        n = self
        xcol0, bxis = xic
        ntt = width // 128
        for fh in range(2):
            if fh == 0:
                wtile = n.ws.ap()[:, 0:4096]
                bws = [n.buf("ws_a"), n.buf("ws_b")]
            else:
                wtile = n.scr_a.ap()[:, 8192:12288]
                bws = [n.buf("ws_wo")]
            src = wdram.ap().rearrange("(cc p) c -> p cc c", p=128)[
                :, :, fh * 512:(fh + 1) * 512]
            n.emit("sp", lambda e, wtile=wtile, src=src: e.dma_start(
                out=wtile.rearrange("p (cc w) -> p cc w", cc=8), in_=src),
                reads=[n.buf(wbuf_name)], writes=bws, dma=True)
            for tt in range(ntt):
                ps = n.psC if tt % 2 == 0 else n.psD
                pb = n.psbuf("C" if tt % 2 == 0 else "D")
                for cc in range(NCC):
                    n.MM(ps[:, 0:512],
                         n.xi[:, xcol0 + cc * width + tt * 128:xcol0 + cc * width + (tt + 1) * 128],
                         wtile[:, cc * 512:(cc + 1) * 512],
                         cc == 0, cc == NCC - 1,
                         reads=bws + [bxis[cc]], writes=[pb])
                dst = n.scr_a[:, out_col0 + tt * 1024 + fh * 512:
                              out_col0 + tt * 1024 + (fh + 1) * 512]
                n.emit("act", lambda a, dst=dst, ps=ps: a.activation(
                    out=dst, in_=ps[:, 0:512], func=AF.Copy, bias=0.0, scale=1.0),
                    reads=[pb], writes=[n.buf("scra")])

    def _lora_pass(self, t0, width, own, tslot):
        n = self
        xcol0, bxis = self._xi_pass("x", t0, width, own)
        pb = n.psbuf("C")
        for cc in range(NCC):
            n.MM(n.psC[:, 0:width], n.w1_sb[:, cc * 128:(cc + 1) * 128],
                 n.xi[:, xcol0 + cc * width:xcol0 + (cc + 1) * width],
                 cc == 0, cc == NCC - 1,
                 reads=[n.buf("w1"), bxis[cc]], writes=[pb])
        n.emit("act", lambda a: a.activation(
            out=n.tanh1[:, tslot * 512:tslot * 512 + width], in_=n.psC[:, 0:width],
            func=AF.Tanh, bias=0.0, scale=1.0),
            reads=[pb], writes=[n.buf(f"tanh1_{tslot}")])

    def _eta_pass(self, t0, width, own, tslot):
        n = self
        xcol0, bxis = self._xi_pass("w", t0, width, own, tslot=tslot)
        pb = n.psbuf("E")
        for cc in range(NCC):
            n.MM(n.psE[0:64, 0:width], n.dw1_sb[:, cc * 64:(cc + 1) * 64],
                 n.xi[:, xcol0 + cc * width:xcol0 + (cc + 1) * width],
                 cc == 0, cc == NCC - 1,
                 reads=[n.buf("dw1"), bxis[cc]], writes=[pb])
        n.emit("act", lambda a: a.activation(
            out=n.eta[0:64, tslot * 512:tslot * 512 + width], in_=n.psE[0:64, 0:width],
            func=AF.Tanh, bias=0.0, scale=1.0),
            reads=[pb], writes=[n.buf(f"eta_{tslot}")])

    def _wkv_front(self, ckw, own, kcol, vcol, chunk_idx, tslot=0):
        n = self
        par2 = chunk_idx % 2
        s_old = n.S0 if par2 == 0 else n.S1
        s_new = n.S1 if par2 == 0 else n.S0
        bso = n.buf("S0" if par2 == 0 else "S1")
        bsn = n.buf("S1" if par2 == 0 else "S0")
        bsbf = n.buf("Sbf")
        pA0, pA1 = n.psbuf("A0"), n.psbuf("A1")
        pB0, pB1 = n.psbuf("B0"), n.psbuf("B1")
        pC, pD, pG = n.psbuf("C"), n.psbuf("D"), n.psbuf("G")
        CSLOT = lambda i: n.scr_c[:, i * 1024:(i + 1) * 1024]
        bsl = [n.buf(f"cs{i}") for i in range(8)]
        ktm = n.scr_a[:, kcol:kcol + 1024]
        vtm = n.scr_a[:, vcol:vcol + 1024]
        bscra = n.buf("scra")
        beta = n.buf(f"eta_{tslot}")

        for half in range(2):
            pb = pA0 if half == 0 else pA1
            n.MM(n.psA[:, half * 512:(half + 1) * 512],
                 n.eta[0:64, tslot * 512 + ckw * 128:tslot * 512 + (ckw + 1) * 128],
                 n.dw2_sb[0:64, half * 512:(half + 1) * 512], True, False,
                 reads=[beta, n.buf("dw2")], writes=[pb])
            n.MM(n.psA[:, half * 512:(half + 1) * 512],
                 n.onesrow_sb[0:1, 0:128],
                 n.tdec_row[0:1, half * 512:(half + 1) * 512], False, True,
                 reads=[n.buf("onesrow"), n.buf("tdec_row")], writes=[pb])
        n.emit("act", lambda a: a.activation(out=CSLOT(0), in_=n.psA[:, :], func=AF.Exp,
                                             bias=0.0, scale=1.0),
               reads=[pA0, pA1], writes=[bsl[0]])
        for half in range(2):
            pb = pA0 if half == 0 else pA1
            n.MM(n.psA[:, half * 512:(half + 1) * 512], n.ltneg_sb[:, :],
                 CSLOT(0)[:, half * 512:(half + 1) * 512], True, True,
                 reads=[n.buf("ltneg"), bsl[0]], writes=[pb])
        n.emit("act", lambda a: a.activation(out=CSLOT(1), in_=n.psA[:, :], func=AF.Exp,
                                             bias=0.0, scale=n.ccol(-1.0)),
               reads=[pA0, pA1, n.buf("consts")], writes=[bsl[1]])
        if own:
            for ic in range(NCC):
                pb = pB0 if ic < 4 else pB1
                n.MM(n.psB[:, ic * 128:(ic + 1) * 128],
                     CSLOT(0)[:, ic * 128:(ic + 1) * 128], n.ltneg_sb[:, :],
                     True, True, reads=[bsl[0], n.buf("ltneg")], writes=[pb])
            n.emit("act", lambda a: a.activation(
                out=n.eal[:, 0:8],
                in_=n.psB.ap().rearrange("p (ic t) -> p ic t", ic=8)[:, :, 127:128],
                func=AF.Exp, bias=0.0, scale=1.0),
                reads=[pB0, pB1], writes=[n.buf("eal")])
            n.emit("act", lambda a: a.activation(out=CSLOT(3), in_=n.psB[:, :], func=AF.Exp,
                                                 bias=0.0, scale=n.ccol(-1.0)),
                   reads=[pB0, pB1, n.buf("consts")], writes=[bsl[3]])
            n.emit("act", lambda a: a.activation(
                out=CSLOT(4).rearrange("p (ic t) -> p ic t", ic=8)[:, :, 1:128],
                in_=n.psB.ap().rearrange("p (ic t) -> p ic t", ic=8)[:, :, 0:127],
                func=AF.Exp, bias=0.0, scale=1.0),
                reads=[pB0, pB1], writes=[bsl[4]])
            n.emit("dve", lambda v: v.memset(
                CSLOT(4).rearrange("p (ic t) -> p ic t", ic=8)[:, :, 0:1], 1.0),
                writes=[bsl[4]])
            for cc in range(NCC):
                n.emit("pe", lambda t, cc=cc: t.transpose(
                    n.psG[:, cc * 128:(cc + 1) * 128],
                    ktm[:, cc * 128:(cc + 1) * 128], n.ident_sb[:, :]),
                    reads=[bscra, n.buf("ident")], writes=[pG])
            kfm_dst = n.scr_a.ap()[:, 0:2048].rearrange("p (ic w) -> p ic w", ic=8)[
                :, :, ckw * 128:(ckw + 1) * 128]
            n.emit("act", lambda a, kfm_dst=kfm_dst: a.activation(
                out=kfm_dst, in_=n.psG.ap().rearrange("p (ic w) -> p ic w", ic=8),
                func=AF.Copy, bias=0.0, scale=1.0),
                reads=[pG], writes=[bscra])
        else:
            for ic in range(NCC):
                n.MM(n.psB[:, ic:ic + 1], CSLOT(0)[:, ic * 128:(ic + 1) * 128],
                     n.ltneg_sb[:, 127:128], True, True,
                     reads=[bsl[0], n.buf("ltneg")], writes=[pB0])
            n.emit("act", lambda a: a.activation(out=n.eal[:, 0:8], in_=n.psB[:, 0:8],
                                                 func=AF.Exp, bias=0.0, scale=1.0),
                   reads=[pB0], writes=[n.buf("eal")])
        n.emit("dve", lambda v: v.tensor_mul(CSLOT(2), ktm, CSLOT(1)),
               reads=[bscra, bsl[1]], writes=[bsl[2]])
        if own:
            rfm = n.scr_a.ap()[:, 2048:4096].rearrange("p (ic w) -> p ic w", ic=8)
            kfm = n.scr_a.ap()[:, 0:2048].rearrange("p (ic w) -> p ic w", ic=8)
            sl = slice(ckw * 128, (ckw + 1) * 128)
            c3 = lambda t: t.rearrange("p (ic w) -> p ic w", ic=8)
            n.emit("dve", lambda v: v.tensor_mul(c3(CSLOT(5)), rfm[:, :, sl], c3(CSLOT(4))),
                   reads=[bscra, bsl[4]], writes=[bsl[5]])
            n.emit("dve", lambda v: v.tensor_mul(c3(CSLOT(6)), kfm[:, :, sl], c3(CSLOT(3))),
                   reads=[bscra, bsl[3]], writes=[bsl[6]])
            n.emit("dve", lambda v: v.tensor_mul(c3(CSLOT(1)), rfm[:, :, sl], kfm[:, :, sl]),
                   reads=[bscra], writes=[bsl[1]])
        for p8 in range(NPAIR):
            ps = n.psC if p8 < 4 else n.psD
            pb = pC if p8 < 4 else pD
            n.MM(ps[:, (p8 % 4) * 128:(p8 % 4 + 1) * 128],
                 CSLOT(2)[:, p8 * 128:(p8 + 1) * 128],
                 vtm[:, p8 * 128:(p8 + 1) * 128],
                 True, True, reads=[bsl[2], bscra], writes=[pb])
        tmp = n.scr_b[:, 2048:2560]
        btmp = n.buf("s_tmp")
        for bank in range(2):
            ps = n.psC if bank == 0 else n.psD
            pb = pC if bank == 0 else pD
            for par in range(2):
                u3 = ps.ap().rearrange("p (q w) -> p q w", q=4)[
                    par * 64:(par + 1) * 64, :, par * 64:(par + 1) * 64]
                so3 = s_old.ap().rearrange("p (q w) -> p q w", q=8)[
                    par * 64:(par + 1) * 64, bank * 4:(bank + 1) * 4, :]
                t3 = tmp.rearrange("p (q w) -> p q w", q=8)[
                    par * 64:(par + 1) * 64, bank * 4:(bank + 1) * 4, :]
                n.emit("dve", lambda v, t3=t3, so3=so3, u3=u3: v.tensor_add(t3, so3, u3),
                       reads=[pb, bso], writes=[btmp])
        for p8 in range(NPAIR):
            n.emit("dve", lambda v, p8=p8: v.tensor_scalar(
                s_new[:, p8 * 64:(p8 + 1) * 64], tmp[:, p8 * 64:(p8 + 1) * 64],
                n.eal[:, p8:p8 + 1], None, ALU.mult),
                reads=[btmp, n.buf("eal")], writes=[bsn])
        if not own:
            n.emit("dve", lambda v: v.tensor_copy(n.Sbf[:, :], s_new[:, :]),
                   reads=[bsn], writes=[bsbf])
            if getattr(n, "dbg_ck", None) == chunk_idx:
                for nm2, islot in [("ew", 0), ("ktil", 2)]:
                    n.debug_dump(f"ck_{nm2}", CSLOT(islot), (128, 1024), [bsl[islot]], BF16)
                n.debug_dump("ck_eal", n.eal[:, 0:8], (128, 8), [n.buf("eal")])
            return
        if own:
            qbank = [n.psA[:, 0:512], n.psA[:, 512:1024], n.psB[:, 0:512], n.psB[:, 512:1024]]
            qb = [pA0, pA1, pB0, pB1]
            for h in range(H):
                ic, par = h // 2, h % 2
                bk, slot = h % 4, h // 4
                n.MM(qbank[bk][:, slot * 128:(slot + 1) * 128],
                     CSLOT(6)[par * 64:par * 64 + 64, ic * 128:(ic + 1) * 128],
                     CSLOT(5)[par * 64:par * 64 + 64, ic * 128:(ic + 1) * 128],
                     True, True, pos=(64 * par, 0),
                     reads=[bsl[6], bsl[5]], writes=[qb[bk]])
            bpm = [n.buf("pm0"), n.buf("pm1")]
            ybank = [n.psC, n.psD]
            yb = [pC, pD]
            for bk in range(4):
                pslot = bk % 2
                pm_ap = n.pmask[:, pslot * 512:(pslot + 1) * 512]
                n.emit("dve", lambda v, pm_ap=pm_ap, bk=bk: v.tensor_mul(
                    pm_ap, qbank[bk], n.mstrict_sb[:, :]),
                    reads=[qb[bk], n.buf("mstrict")], writes=[bpm[pslot]])
                for slot in range(4):
                    h = bk + slot * 4 if False else slot * 4 + bk
                    ic, par = h // 2, h % 2
                    ypar = h % 2
                    n.MM(ybank[ypar][:, (h // 2) * 64:(h // 2) * 64 + 64],
                         pm_ap[:, slot * 128:(slot + 1) * 128],
                         vtm[:, h * 64:(h + 1) * 64],
                         True, False,
                         reads=[bpm[pslot], bscra], writes=[yb[ypar]])
                    n.MM(ybank[ypar][:, (h // 2) * 64:(h // 2) * 64 + 64],
                         CSLOT(5)[par * 64:par * 64 + 64, ic * 128:(ic + 1) * 128],
                         n.Sbf[par * 64:par * 64 + 64, (h // 2) * 64:(h // 2) * 64 + 64],
                         False, True, pos=(64 * par, 0),
                         reads=[bsl[5], bsbf], writes=[yb[ypar]])
            dsb_ap = n.dsb[:, par2 * 16:par2 * 16 + 16]
            for ic in range(NCC):
                n.MM(n.psB[:, 2 * ic:2 * ic + 2],
                     CSLOT(1)[:, ic * 128:(ic + 1) * 128], n.useg_sb[:, 2 * ic:2 * ic + 2],
                     True, True, reads=[bsl[1], n.buf("useg")], writes=[pB0])
            n.emit("dve", lambda v: v.tensor_copy(dsb_ap, n.psB[:, 0:16]),
                   reads=[pB0], writes=[n.buf("dsb")])
            ycopy = n.scr_b[:, 3072:4096]
            byc = n.buf("ycopy")
            for ypar in range(2):
                n.emit("dve", lambda v, ypar=ypar: v.tensor_copy(
                    ycopy[:, ypar * 512:(ypar + 1) * 512], ybank[ypar][:, 0:512]),
                    reads=[yb[ypar]], writes=[byc])
            y_sb = n.scr_b[:, par2 * 1024:(par2 + 1) * 1024]
            by = n.buf("y_sb" if par2 == 0 else "ysq")
            for h in range(H):
                ypar = h % 2
                n.emit("dve", lambda v, h=h, ypar=ypar: v.scalar_tensor_tensor(
                    y_sb[:, h * 64:(h + 1) * 64], vtm[:, h * 64:(h + 1) * 64],
                    dsb_ap[:, h:h + 1],
                    ycopy[:, ypar * 512 + (h // 2) * 64:ypar * 512 + (h // 2) * 64 + 64],
                    ALU.mult, ALU.add),
                    reads=[bscra, n.buf("dsb"), byc], writes=[by])
            n.emit("dve", lambda v: v.tensor_copy(n.Sbf[:, :], s_new[:, :]),
                   reads=[bsn], writes=[bsbf])
        if getattr(n, "dbg_ck", None) == chunk_idx:
            for nm2, islot in [("ew", 0), ("ktil", 2), ("ktf", 6)]:
                n.debug_dump(f"ck_{nm2}", CSLOT(islot), (128, 1024), [bsl[islot]], BF16)
            for nm2, islot in [("enA_fm", 3), ("eAex", 4), ("rt", 5)]:
                n.debug_dump(f"ck_{nm2}", CSLOT(islot), (128, 1024), [bsl[islot]], BF16)
            n.debug_dump("ck_dsb", n.dsb[:, par2 * 16:par2 * 16 + 16], (128, 16),
                         [n.buf("dsb")])
            n.debug_dump("ck_ysb", n.scr_b[:, par2 * 1024:(par2 + 1) * 1024], (128, 1024),
                         [by])
            n.debug_dump("ck_pm", n.pmask[:, 0:1024], (128, 1024),
                         [n.buf("pm0"), n.buf("pm1")], BF16)
            n.debug_dump("ck_eal", n.eal[:, 0:8], (128, 8), [n.buf("eal")])

    def _wkv_tail(self, chunk_idx, dyn_row0):
        n = self
        par2 = chunk_idx % 2
        y_sb = n.scr_b[:, par2 * 1024:(par2 + 1) * 1024]
        by = n.buf("y_sb" if par2 == 0 else "ysq")
        st = n.stats
        bst = n.buf("gnstats")
        y3 = y_sb.rearrange("p (h j) -> p h j", h=H)
        n.emit("dve", lambda v: v.reduce_sum(st[:, 16:32], y3, axis=AX.X),
               reads=[by], writes=[bst])
        ysq = n.outt[:, 0:1024]
        bsq = [n.buf("ot0"), n.buf("ot1")]
        n.emit("act", lambda a: a.activation(
            out=ysq, in_=y_sb, func=AF.Square, bias=0.0, scale=1.0),
            reads=[by], writes=bsq)
        n.emit("dve", lambda v: v.reduce_sum(
            st[:, 32:48], ysq.rearrange("p (h j) -> p h j", h=H), axis=AX.X),
            reads=bsq, writes=[bst])
        n.emit("dve", lambda v: v.tensor_scalar(
            st[:, 48:64], st[:, 16:32], n.ccol(-1.0 / 64), None, ALU.mult),
            reads=[bst, n.buf("consts")], writes=[bst])
        n.emit("dve", lambda v: v.tensor_mul(st[:, 80:96], st[:, 48:64], st[:, 48:64]),
               reads=[bst], writes=[bst])
        n.emit("dve", lambda v: v.scalar_tensor_tensor(
            st[:, 64:80], st[:, 32:48], n.ccol(1.0 / 64), st[:, 80:96],
            ALU.mult, ALU.subtract), reads=[bst, n.buf("consts")], writes=[bst])
        n.emit("act", lambda a: a.activation(
            out=st[:, 80:96], in_=st[:, 64:80], func=AF.Sqrt,
            bias=n.ccol(GN_EPS), scale=1.0),
            reads=[bst, n.buf("consts")], writes=[bst])
        n.emit("dve", lambda v: v.reciprocal(st[:, 80:96], st[:, 80:96]),
               reads=[bst], writes=[bst])
        n.emit("dve", lambda v: v.tensor_mul(st[:, 48:64], st[:, 48:64], st[:, 80:96]),
               reads=[bst], writes=[bst])
        yn_ap = n.pmask[:, 1024:2048]
        byn = n.buf("yn_st")
        for h in range(H):
            n.emit("act", lambda a2, h=h: a2.activation(
                out=yn_ap[:, h * 64:(h + 1) * 64], in_=y_sb[:, h * 64:(h + 1) * 64],
                func=AF.Identity, scale=st[:, 80 + h:81 + h], bias=st[:, 48 + h:49 + h]),
                reads=[by, bst], writes=[byn])
        n.emit("act", lambda a2: a2.dma_start(
            out=n.d_yn.ap()[dyn_row0:dyn_row0 + 128, :], in_=yn_ap),
            reads=[byn], writes=[n.buf("d_yn")], dma=True)

    def tmix_phase(self):
        n = self
        for dst, srcs in [("cs0", ["lnout0"]), ("cs1", ["lnout1"]),
                          ("cs2", ["lnout2"]), ("cs3", ["lnout3"]),
                          ("y_sb", ["lnx0"]), ("ysq", ["lnx1"]),
                          ("s_tmp", ["lnx2"]), ("ycopy", ["lnsq0"])]:
            n.alias_fence(dst, srcs)
        n.emit("dve", lambda v: v.memset(n.S0[:, :], 0.0), writes=[n.buf("S0")])
        n.emit("dve", lambda v: v.memset(n.Sbf[:, :], 0.0), writes=[n.buf("Sbf")])
        ci = 0
        NTB = 16
        tslot = lambda tb: tb % 2

        def t0_of(tb):
            return (tb % 8) * 256

        self._lora_pass(0, 256, False, 0)
        self._eta_pass(0, 256, False, 0)
        xic_k = self._xi_pass("k", 0, 256, False, tslot=0)
        pending_tails = []
        for tb in range(NTB):
            own = tb >= 8
            t0 = t0_of(tb)
            ts, tsn = tb % 2, (tb + 1) % 2
            if pending_tails:
                self._wkv_tail(*pending_tails[0])
            self._proj_tm(n.d_wk, "d_wk", 4096, xic_k, 256)
            if pending_tails:
                self._wkv_tail(*pending_tails[1])
            pending_tails = []
            xic = self._xi_pass("v", t0, 256, own, tslot=ts)
            self._proj_tm(n.d_wv, "d_wv", 6144, xic, 256)
            if own:
                xic = self._xi_pass("r", t0, 256, own, tslot=ts)
                self._proj_fm(n.d_wr, "d_wr", 2048, xic, 256)
            if tb < NTB - 1:
                nown = (tb + 1) >= 8
                nt0 = t0_of(tb + 1)
                self._lora_pass(nt0, 256, nown, tsn)
                self._eta_pass(nt0, 256, nown, tsn)
                xic_k = self._xi_pass("k", nt0, 256, nown, tslot=tsn)
            if tb == 8:
                n.debug_dump("tb0_tanh1", n.tanh1[:, 0:256], (128, 256), [n.buf("tanh1_0")], BF16)
                n.debug_dump("tb0_eta", n.eta[0:64, 0:256], (64, 256), [n.buf("eta_0")], BF16)
                n.debug_dump("tb0_kfm", n.scr_a[:, 0:2048], (128, 2048), [n.buf("scra")], BF16)
                n.debug_dump("tb0_rfm", n.scr_a[:, 2048:4096], (128, 2048), [n.buf("scra")], BF16)
                n.debug_dump("tb0_ktm", n.scr_a[:, 4096:6144], (128, 2048), [n.buf("scra")], BF16)
                n.debug_dump("tb0_vtm", n.scr_a[:, 6144:8192], (128, 2048), [n.buf("scra")], BF16)
            for ckw in range(2):
                self._wkv_front(ckw, own, 4096 + ckw * 1024, 6144 + ckw * 1024,
                                ci + ckw, tslot=ts)
            if own:
                pending_tails = [(ci, t0), (ci + 1, t0 + 128)]
            ci += 2
            if tb == 7:
                s_cur = n.S0 if ci % 2 == 0 else n.S1
                bsc = n.buf("S0" if ci % 2 == 0 else "S1")
                n.emit("dve", lambda v, s_cur=s_cur: v.tensor_scalar(
                    s_cur[:, :], s_cur[:, :], n.vcol("is_odd"), None, ALU.mult),
                    reads=[bsc, n.vbuf("is_odd")], writes=[bsc])
                n.emit("dve", lambda v, s_cur=s_cur: v.tensor_copy(n.Sbf[:, :], s_cur[:, :]),
                       reads=[bsc], writes=[n.buf("Sbf")])
                n.debug_dump("S_bound", n.Sbf[:, :], (128, 512), [n.buf("Sbf")], BF16)
                for cc in range(NCC):
                    n.alias_fence(f"xi_2048_{cc}", [f"xi_0_{cc}"])
        for args in pending_tails:
            self._wkv_tail(*args)
        n.debug_dump_dram("yn", n.d_yn, (OWN, C), [n.buf("d_yn")], BF16)

    def alias_fence(self, dst, srcs):
        bd = self.buf(dst)
        for s in srcs:
            b = self.buf(s)
            bd.r += b.r
            if b.w is not None:
                bd.r.append(b.w)

    def gate_phase(self):
        n = self
        pE = n.psbuf("E")
        pC = n.psbuf("C")
        st = n.stats
        bst = n.buf("gst")
        bcomb = n.buf("comb")
        bcombf = n.buf("comb_fm")
        for tt in range(NTT_OWN):
            zp = n.psE[:, tt * 8:(tt + 1) * 8]
            for cc in range(NCC):
                n.MM(zp, n.xno_fm[cc][:, tt * 128:(tt + 1) * 128],
                     n.wgate_sb[:, cc * 8:(cc + 1) * 8], cc == 0, False,
                     reads=[n.buf(f"xno_fm{cc}"), n.buf("wgate")], writes=[pE])
            n.MM(zp, n.onesrow_sb[0:1, 0:128], n.bgate_row[0:1, 0:8],
                 False, True, reads=[n.buf("onesrow"), n.buf("bgate_row")], writes=[pE])
            cslot = n.comb[:, tt * 8:(tt + 1) * 8]
            n.emit("dve", lambda v, zp=zp: v.reduce_max(st[:, 96:97], zp,
                                                        axis=AX.X, negate=True),
                   reads=[pE], writes=[bst])
            n.emit("act", lambda a2, zp=zp: a2.activation(out=st[:, 104:112], in_=zp,
                                                          func=AF.Exp, bias=st[:, 96:97], scale=1.0),
                   reads=[pE, bst], writes=[bst])
            n.emit("dve", lambda v: v.reduce_sum(st[:, 97:98], st[:, 104:112], axis=AX.X),
                   reads=[bst], writes=[bst])
            n.emit("dve", lambda v: v.reciprocal(st[:, 98:99], st[:, 97:98]),
                   reads=[bst], writes=[bst])
            n.emit("dve", lambda v, cslot=cslot: v.tensor_scalar(
                cslot, st[:, 104:112], st[:, 98:99], None, ALU.mult),
                reads=[bst], writes=[bcomb])
            n.emit("dve", lambda v, cslot=cslot: v.reduce_max(st[:, 99:100], cslot, axis=AX.X),
                   reads=[bcomb], writes=[bst])
            n.emit("dve", lambda v, cslot=cslot: v.tensor_scalar(
                st[:, 104:112], cslot, st[:, 99:100], None, ALU.is_lt),
                reads=[bcomb, bst], writes=[bst])
            n.emit("dve", lambda v, cslot=cslot: v.tensor_mul(st[:, 104:112], st[:, 104:112], cslot),
                   reads=[bcomb, bst], writes=[bst])
            n.emit("dve", lambda v: v.reduce_max(st[:, 100:101], st[:, 104:112], axis=AX.X),
                   reads=[bst], writes=[bst])
            n.emit("dve", lambda v, cslot=cslot: v.tensor_scalar(
                st[:, 104:112], cslot, st[:, 100:101], None, ALU.is_ge),
                reads=[bcomb, bst], writes=[bst])
            n.emit("dve", lambda v, cslot=cslot: v.tensor_mul(cslot, cslot, st[:, 104:112]),
                   reads=[bst, bcomb], writes=[bcomb])
            if tt == 0:
                n.emit("dve", lambda v: v.tensor_copy(n.outt[:, 0:8], n.psE[:, 0:8]),
                       reads=[pE], writes=[n.buf("zdbg")])
                n.debug_dump("z0", n.outt[:, 0:8], (128, 8), [n.buf("zdbg")])
        n.debug_dump("comb_early", n.comb[:, :], (128, 128), [n.buf("comb")])

    def _comb_fm_build(self):
        n = self
        pC = n.psbuf("C")
        bcomb, bcombf = n.buf("comb"), n.buf("comb_fm")
        for tt in range(NTT_OWN):
            n.emit("pe", lambda t, tt=tt: t.transpose(
                n.psC[0:8, 0:128], n.comb[:, tt * 8:(tt + 1) * 8],
                n.ident_f32[0:128, 0:128]),
                reads=[bcomb, n.buf("ident_f")], writes=[pC])
            n.emit("dve", lambda v, tt=tt: v.tensor_copy(
                n.scr_c[0:8, tt * 128:(tt + 1) * 128], n.psC[0:8, 0:128]),
                reads=[pC], writes=[bcombf])

    def _vaug_build(self):
        n = self
        pC, pD = n.psbuf("C"), n.psbuf("D")
        bvst = n.buf("vstage")
        vstage = n.scr_c[:, 2048:2568]
        ap1 = n.scr_c.ap()[:, 2048:2568].rearrange("p (k o) -> p k o", k=8)[:, :, 64:65]
        n.emit("dve", lambda v: v.memset(ap1, 1.0), writes=[bvst])
        for jb in range(2):
            bws = n.buf("ws_a")
            bws2 = n.buf("ws_b")
            ws_ap = n.ws.ap()[:, 0:4096].rearrange("p (cc w) -> p cc w", cc=8)
            srcw = n.d_wqkv.ap().rearrange("(cc p) c -> p cc c", p=128)[
                :, :, 2 * C + jb * 512:2 * C + (jb + 1) * 512]
            n.emit("sp", lambda e, ws_ap=ws_ap, srcw=srcw: e.dma_start(out=ws_ap, in_=srcw),
                   reads=[n.buf("d_wqkv")], writes=[bws, bws2], dma=True)
            for sc in range(32):
                ps, pb = (n.psC, pC) if sc % 2 == 0 else (n.psD, pD)
                for cc in range(NCC):
                    n.MM(ps[:, 0:512], n.xn_fm[cc][:, sc * 128:(sc + 1) * 128],
                         n.ws[:, cc * 512:(cc + 1) * 512], cc == 0, False,
                         reads=[bws if cc < 4 else bws2, n.buf(f"xn_fm{cc}")], writes=[pb])
                n.MM(ps[:, 0:512], n.onesrow_sb[0:1, 0:128],
                     n.bv_row[0:1, jb * 512:(jb + 1) * 512], False, True,
                     reads=[n.buf("onesrow"), n.buf("bv_row")], writes=[pb])
                dst = n.scr_c.ap()[:, 2048:2568].rearrange(
                    "p (hp par j) -> p hp par j", hp=4, par=2)[:, :, :, 0:64]
                srcp = ps.ap()[:, 0:512].rearrange("p (hp par j) -> p hp par j", hp=4, par=2)
                n.emit("dve", lambda v, dst=dst, srcp=srcp: v.tensor_copy(dst, srcp),
                       reads=[pb], writes=[bvst])
                n.emit("sp", lambda e, sc=sc, jb=jb: e.dma_start(
                    out=n.d_vaug.ap()[sc * 128:(sc + 1) * 128, jb * 520:(jb + 1) * 520],
                    in_=vstage), reads=[bvst], writes=[n.buf("d_vaug")], dma=True)

    def _attn_pair(self, hp, half):
        n = self
        pA0, pA1 = n.psbuf("A0"), n.psbuf("A1")
        pB0, pB1 = n.psbuf("B0"), n.psbuf("B1")
        pC, pD, pG = n.psbuf("C"), n.psbuf("D"), n.psbuf("G")
        bscra, bst = n.buf("scra"), n.buf("attn_st")
        bws0 = n.buf("ws_a")
        bacc = n.buf("acc")
        hq2 = hp - 4 * half
        jb, hp4 = hp // 4, hp % 4
        for j, col0 in enumerate([hp * 128, C + hp * 128]):
            dst = n.ws.ap()[:, j * 1024:(j + 1) * 1024].rearrange("p (cc w) -> p cc w", cc=8)
            srcw = n.d_wqkv.ap().rearrange("(cc p) c -> p cc c", p=128)[:, :, col0:col0 + 128]
            n.emit("sp", lambda e, dst=dst, srcw=srcw: e.dma_start(out=dst, in_=srcw),
                   reads=[n.buf("d_wqkv")], writes=[bws0], dma=True)
        srcv = n.d_vaug.ap().rearrange("(sc p) c -> p sc c", p=128)[
            :, :, jb * 520 + hp4 * 130:jb * 520 + (hp4 + 1) * 130]
        n.emit("sp", lambda e: e.dma_start(
            out=n.scr_a.ap()[:, 0:4160].rearrange("p (sc c) -> p sc c", sc=32), in_=srcv),
            reads=[n.buf("d_vaug")], writes=[bscra], dma=True)
        for tb4 in range(4):
            ps, pb = (n.psC, pC) if tb4 % 2 == 0 else (n.psD, pD)
            for cc in range(NCC):
                n.MM(ps[:, 0:512], n.ws[:, cc * 128:(cc + 1) * 128],
                     n.xno_fm[cc][:, tb4 * 512:(tb4 + 1) * 512], cc == 0, cc == NCC - 1,
                     reads=[bws0, n.buf(f"xno_fm{cc}")], writes=[pb])
            n.emit("dve", lambda v, ps=ps, tb4=tb4: v.tensor_scalar(
                n.scr_a[:, 4160 + tb4 * 512:4160 + (tb4 + 1) * 512], ps[:, 0:512],
                n.vcol("bq", hp), None, ALU.add),
                reads=[pb, n.vbuf("bq")], writes=[bscra])
        for tb8 in range(8):
            ps, pb = (n.psC, pC) if tb8 % 2 == 0 else (n.psD, pD)
            for cc in range(NCC):
                n.MM(ps[:, 0:512], n.ws[:, 1024 + cc * 128:1024 + (cc + 1) * 128],
                     n.xn_fm[cc][:, tb8 * 512:(tb8 + 1) * 512], cc == 0, cc == NCC - 1,
                     reads=[bws0, n.buf(f"xn_fm{cc}")], writes=[pb])
            n.emit("dve", lambda v, ps=ps, tb8=tb8: v.tensor_scalar(
                n.scr_a[:, 6208 + tb8 * 512:6208 + (tb8 + 1) * 512], ps[:, 0:512],
                n.vcol("bk", hp), None, ALU.add),
                reads=[pb, n.vbuf("bk")], writes=[bscra])
        ES_COL = (4096, 5120, 2568, 7168)
        for tb4 in range(4):
            def emit_v(sc):
                es = n.scr_c[:, ES_COL[sc % 4]:ES_COL[sc % 4] + 1024]
                bes = n.buf(f"expS{sc % 4}")
                for par in range(2):
                    n.MM((n.psC if par == 0 else n.psD)[0:65, :],
                         n.scr_a[:, sc * 130 + par * 65:sc * 130 + (par + 1) * 65],
                         es[:, par * 512:(par + 1) * 512], sc == 0, sc == 31,
                         reads=[bscra, bes], writes=[pC if par == 0 else pD])
            for sc in range(32):
                qk, qb0, qb1 = (n.psA, pA0, pA1) if sc % 2 == 0 else (n.psB, pB0, pB1)
                for par in range(2):
                    n.MM(qk[:, par * 512:(par + 1) * 512],
                         n.scr_a[par * 64:par * 64 + 64, 6208 + sc * 128:6208 + (sc + 1) * 128],
                         n.scr_a[par * 64:par * 64 + 64, 4160 + tb4 * 512:4160 + (tb4 + 1) * 512],
                         True, True, pos=(64 * par, 0),
                         reads=[bscra], writes=[qb0 if par == 0 else qb1])
                es = n.scr_c[:, ES_COL[sc % 4]:ES_COL[sc % 4] + 1024]
                bes = n.buf(f"expS{sc % 4}")
                n.emit("act", lambda a2, qk=qk, es=es: a2.activation(
                    out=es, in_=qk[:, :], func=AF.Exp, bias=n.ccol(ATT_SHIFT),
                    scale=n.ccol(0.125)),
                    reads=[qb0, qb1, n.buf("consts")], writes=[bes])
                if sc > 1:
                    emit_v(sc - 2)
            emit_v(30)
            emit_v(31)
            for par in range(2):
                atmp = n.scr_c[0:65, 6144 + par * 512:6144 + (par + 1) * 512]
                batm = n.buf(f"atmp{par}")
                n.emit("dve", lambda v, atmp=atmp, par=par: v.tensor_copy(
                    atmp, (n.psC if par == 0 else n.psD)[0:65, :]),
                    reads=[pC if par == 0 else pD], writes=[batm])
                for tt4 in range(4):
                    slot = par * 4 + tt4
                    n.emit("pe", lambda t, atmp=atmp, slot=slot, tt4=tt4: t.transpose(
                        n.psG[:, slot * 128:slot * 128 + 65],
                        atmp[0:65, tt4 * 128:(tt4 + 1) * 128],
                        n.ident_sb[0:65, 0:65]),
                        reads=[batm, n.buf("ident")], writes=[pG])
            for par in range(2):
                for tt4 in range(4):
                    slot = par * 4 + tt4
                    tt = tb4 * 4 + tt4
                    hq = hq2 * 2 + par
                    piece = n.scr_b if hq < 4 else n.acc_hi
                    col = tt * 256 + (hq % 4) * 64
                    rcol = n.stats[:, 112 + slot:113 + slot]
                    n.emit("dve", lambda v, slot=slot, rcol=rcol: v.reciprocal(
                        rcol, n.psG[:, slot * 128 + 64:slot * 128 + 65]),
                        reads=[pG], writes=[bst])
                    n.emit("dve", lambda v, slot=slot, rcol=rcol, col=col, piece=piece:
                           v.tensor_scalar(
                               piece[:, col:col + 64], n.psG[:, slot * 128:slot * 128 + 64],
                               rcol, None, ALU.mult),
                           reads=[pG, bst], writes=[bacc])

    def _moe_half(self, half):
        n = self
        pC, pD = n.psbuf("C"), n.psbuf("D")
        bacc, bcomb, bcombf = n.buf("acc"), n.buf("comb"), n.buf("comb_fm")
        bscra = n.buf("scra")
        dstw = n.scr_a.ap()[:, 8192:12288].rearrange("p (cc w) -> p cc w", cc=8)
        srcw = n.d_wo.ap().rearrange("(cc p) c -> p cc c", p=128)[
            :, :, half * 512:(half + 1) * 512]
        n.emit("sp", lambda e: e.dma_start(out=dstw, in_=srcw),
               reads=[n.buf("d_wo")], writes=[n.buf("ws_wo"), bscra], dma=True)
        for c4 in range(4):
            ydst = n.scr_a[:, c4 * 2048:(c4 + 1) * 2048]
            n.emit("act", lambda a2, ydst=ydst, c4=c4: a2.dma_start(
                out=ydst, in_=n.d_yn.ap()[:, c4 * 128:(c4 + 1) * 128], transpose=True),
                reads=[n.buf("d_yn")], writes=[bscra], dma=True)
        for tt in range(NTT_OWN):
            n.MM(n.psE[:, 0:512], n.scr_c[0:8, tt * 128:(tt + 1) * 128],
                 n.bexp_sb[0:8, half * 512:(half + 1) * 512], True, True,
                 reads=[bcombf, n.buf("bexp")], writes=[n.psbuf("E")])
            for piece2 in range(2):
                dstp = (n.scr_b if piece2 == 0 else n.acc_hi)[:, tt * 256:(tt + 1) * 256]
                n.emit("dve", lambda v, dstp=dstp, piece2=piece2: v.tensor_add(
                    dstp, dstp, n.psE[:, piece2 * 256:(piece2 + 1) * 256]),
                    reads=[n.psbuf("E"), bacc], writes=[bacc])
        if half == 0:
            n.debug_dump("acc_bias", n.scr_b[:, 0:4096], (128, 4096), [n.buf("acc")])
        for e8 in range(E):
            if e8 % 2 == 0:
                warena = n.ws.ap()[:, 0:4096]
                bwsh = [[n.buf("ws_a")], [n.buf("ws_b")]]
            else:
                warena = n.scr_c.ap()[:, 4096:8192]
                bwsh = [[n.buf("expS0"), n.buf("expS1")],
                        [n.buf("atmp0"), n.buf("atmp1"), n.buf("expS3")]]
            for hcc in range(2):
                src = n.d_wexp.ap()[e8 * C:(e8 + 1) * C, :].rearrange(
                    "(cc p) c -> p cc c", p=128)[
                    :, hcc * 4:(hcc + 1) * 4, half * 512:(half + 1) * 512]
                dst = warena[:, hcc * 2048:(hcc + 1) * 2048].rearrange(
                    "p (cc w) -> p cc w", cc=4)
                n.emit("sp", lambda e, dst=dst, src=src: e.dma_start(out=dst, in_=src),
                       reads=[n.buf(f"d_wexp{e8}")], writes=bwsh[hcc], dma=True)
            for tt in range(NTT_OWN):
                ps, pb = (n.psC, pC) if tt % 2 == 0 else (n.psD, pD)
                for cc in range(NCC):
                    n.MM(ps[:, 0:512], n.xno_fm[cc][:, tt * 128:(tt + 1) * 128],
                         warena[:, cc * 512:(cc + 1) * 512], cc == 0, cc == NCC - 1,
                         reads=bwsh[cc // 4] + [n.buf(f"xno_fm{cc}")], writes=[pb])
                for piece2 in range(2):
                    dstp = (n.scr_b if piece2 == 0 else n.acc_hi)[:, tt * 256:(tt + 1) * 256]
                    n.emit("dve", lambda v, ps=ps, dstp=dstp, tt=tt, e8=e8, piece2=piece2:
                           v.scalar_tensor_tensor(
                               dstp, ps[:, piece2 * 256:(piece2 + 1) * 256],
                               n.comb[:, tt * 8 + e8:tt * 8 + e8 + 1], dstp,
                               ALU.mult, ALU.add),
                           reads=[pb, bcomb, bacc], writes=[bacc])

    def _wo_epilogue_half(self, half):
        n = self
        pC, pD, pE = n.psbuf("C"), n.psbuf("D"), n.psbuf("E")
        bacc = n.buf("acc")
        bg = n.buf("g")
        bscra = n.buf("scra")
        bsacc = n.buf("s_acc")
        for tt in range(NTT_OWN):
            for piece2 in range(2):
                srcp = (n.scr_b if piece2 == 0 else n.acc_hi)[:, tt * 256:(tt + 1) * 256]
                n.emit("act", lambda a2, tt=tt, piece2=piece2, srcp=srcp: a2.activation(
                    out=n.scr_c[:, tt * 512 + piece2 * 256:tt * 512 + (piece2 + 1) * 256],
                    in_=srcp, func=AF.Silu, bias=0.0, scale=1.0),
                    reads=[bacc], writes=[bg])
        bws = n.buf("scra")
        wo = lambda cc: n.scr_a[:, 8192 + cc * 512:8192 + (cc + 1) * 512]
        for cc in range(NCC):
            n.MM(n.psE[0:1, 0:512], n.lnb_bf[:, cc:cc + 1],
                 wo(cc), cc == 0, cc == NCC - 1,
                 reads=[bws, n.buf("lnb_bf")], writes=[pE])
        n.emit("dve", lambda v: v.tensor_copy(n.bb[0:1, 0:512], n.psE[0:1, 0:512]),
               reads=[pE], writes=[n.buf("bbst")])
        for cc in range(NCC):
            n.emit("dve", lambda v, cc=cc: v.tensor_scalar(
                wo(cc), wo(cc), n.vcol("ln_w", cc), None, ALU.mult),
                reads=[bws, n.vbuf("ln_w")], writes=[bws])
        for ch in range(2):
            if ch == 1:
                for c4 in range(4):
                    cc = 4 + c4
                    ydst = n.scr_a[:, c4 * 2048:(c4 + 1) * 2048]
                    n.emit("act", lambda a2, ydst=ydst, cc=cc: a2.dma_start(
                        out=ydst, in_=n.d_yn.ap()[:, cc * 128:(cc + 1) * 128], transpose=True),
                        reads=[n.buf("d_yn")], writes=[bscra], dma=True)
            for tt in range(NTT_OWN):
                ps, pb = (n.psC, pC) if tt % 2 == 0 else (n.psD, pD)
                for c4 in range(4):
                    cc = ch * 4 + c4
                    n.MM(ps[:, 0:512],
                         n.scr_a[:, c4 * 2048 + tt * 128:c4 * 2048 + (tt + 1) * 128],
                         wo(cc), c4 == 0, ch == 1 and c4 == 3,
                         reads=[bscra, bws], writes=[pb])
                if ch == 0:
                    n.MM(ps[:, 0:512], n.onesrow_sb[0:1, 0:128], n.bb[0:1, 0:512],
                         False, True, reads=[n.buf("onesrow"), n.buf("bbst")], writes=[pb])
                    n.emit("dve", lambda v, ps=ps, tt=tt: v.tensor_copy(
                        n.scr_b[:, tt * 256:(tt + 1) * 256], ps[:, 0:256]),
                        reads=[pb, bg], writes=[bsacc])
                    n.emit("dve", lambda v, ps=ps, tt=tt: v.tensor_copy(
                        n.acc_hi[:, tt * 256:(tt + 1) * 256], ps[:, 256:512]),
                        reads=[pb, bg], writes=[bsacc])
                else:
                    xt = n.xtile[:, (tt % 2) * 512:(tt % 2) * 512 + 512]
                    bxt = n.buf(f"xt{tt % 2}")
                    n.emit("act", lambda e, xt=xt, tt=tt: e.dma_start(
                        out=xt, in_=n.in_x_own.ap()[tt * 128:(tt + 1) * 128,
                                                    half * 512:(half + 1) * 512]),
                        writes=[bxt], dma=True)
                    ot = n.outt[:, (tt % 2) * 512:(tt % 2) * 512 + 512]
                    bot = n.buf(f"ot{tt % 2}")
                    for piece2 in range(2):
                        sap = (n.scr_b if piece2 == 0 else n.acc_hi)[:, tt * 256:(tt + 1) * 256]
                        n.emit("dve", lambda v, ps=ps, ot=ot, sap=sap, piece2=piece2:
                               v.tensor_add(ot[:, piece2 * 256:(piece2 + 1) * 256],
                                            ps[:, piece2 * 256:(piece2 + 1) * 256], sap),
                               reads=[pb, bsacc], writes=[bot])
                        n.emit("dve", lambda v, ot=ot, tt=tt, piece2=piece2: v.tensor_mul(
                            ot[:, piece2 * 256:(piece2 + 1) * 256],
                            ot[:, piece2 * 256:(piece2 + 1) * 256],
                            n.scr_c[:, tt * 512 + piece2 * 256:tt * 512 + (piece2 + 1) * 256]),
                            reads=[bg, bot], writes=[bot])
                        n.emit("dve", lambda v, ot=ot, xt=xt, piece2=piece2: v.tensor_add(
                            ot[:, piece2 * 256:(piece2 + 1) * 256],
                            ot[:, piece2 * 256:(piece2 + 1) * 256],
                            xt[:, piece2 * 256:(piece2 + 1) * 256]),
                            reads=[bot, bxt], writes=[bot])
                    n.emit("sp", lambda e, tt=tt, ot=ot: e.dma_start(
                        out=n.out.ap()[tt * 128:(tt + 1) * 128, half * 512:(half + 1) * 512],
                        in_=ot), reads=[bot], writes=[n.buf("out_w")], dma=True)

    def ffn_phase(self):
        n = self
        n.alias_fence("acc", ["y_sb", "ysq", "s_tmp", "utmp", "gnstats", "sqdump",
                              "lnx0", "lnx1", "lnx2", "lnsq0"]
                      + [f"xi_{c}_{cc}" for c in (0, 2048) for cc in range(NCC)]
                      + ["tanh1_0", "tanh1_1", "eta_0", "eta_1", "pm0", "pm1", "yn_st"])
        n.alias_fence("xt0", ["S0", "S1"])
        n.alias_fence("xt1", ["S0", "S1", "Sbf"])
        n.alias_fence("scra", ["ws_wo"])
        for nm in ("g", "expS0", "expS1", "expS2", "expS3", "atmp0", "atmp1",
                   "comb_fm", "vstage"):
            n.alias_fence(nm, [f"cs{i}" for i in range(8)]
                          + [f"lnout{i}" for i in range(4)])
        n.emit("dve", lambda v: v.tensor_copy(n.lnb_bf[:, :], n.vec_sb[:, 48:56]),
               reads=[n.vbuf("ln_b")], writes=[n.buf("lnb_bf")])
        n.debug_dump("comb", n.comb[:, :], (128, 128), [n.buf("comb")])
        self._vaug_build()
        for half in range(2):
            n.alias_fence("acc", ["s_acc"])
            for nm in ("expS0", "expS1", "expS2", "expS3", "atmp0", "atmp1",
                       "comb_fm"):
                n.alias_fence(nm, ["g"])
            n.alias_fence("g", ["expS0", "expS1", "expS2", "expS3",
                                "atmp0", "atmp1"])
            self._comb_fm_build()
            for hp in range(4 * half, 4 * half + 4):
                self._attn_pair(hp, half)
            if half == 0:
                n.debug_dump("acc_attn", n.scr_b[:, 0:4096], (128, 4096), [n.buf("acc")])
            self._moe_half(half)
            if half == 0:
                n.debug_dump("acc_moe", n.scr_b[:, 0:4096], (128, 4096), [n.buf("acc")])
            self._wo_epilogue_half(half)

    def debug_dump_dram(self, key, dram_t, shape, after_bufs, dt=F32):
        if key not in self.debug_keys:
            return
        t = self.nc.dram_tensor("dbg_" + key, list(shape), dt, kind="ExternalOutput")
        self.dbg[key] = t
        return self.emit("sp", lambda e: e.dma_start(out=t.ap(), in_=dram_t.ap()),
                         reads=list(after_bufs), writes=[self.buf("dbg_" + key)], dma=True)

    def debug_dump(self, key, sbuf_ap, shape, after_bufs, dt=F32):
        if key not in self.debug_keys:
            return
        t = self.nc.dram_tensor("dbg_" + key, list(shape), dt, kind="ExternalOutput")
        self.dbg[key] = t
        tok = self.emit("sp", lambda e: e.dma_start(out=t.ap(), in_=sbuf_ap),
                        reads=list(after_bufs), writes=[self.buf("dbg_" + key)], dma=True)
        return tok

    def build(self, phases):
        self._init_consts()
        self.prologue()
        for ph in phases:
            ph()
        finals = []
        bout = self.buf("out_w")
        if bout.w is not None:
            finals.append(bout.w)
        for key in self.dbg:
            finals.append(self.buf("dbg_" + key).w)
        runner = self.p.finalize(None, finals)
        nc = self.nc
        with nc.Block() as block:
            block.tensor(runner("pe"))
            block.scalar(runner("act"))
            block.vector(runner("dve"))
            block.gpsimd(runner("pool"))
            block.sync(runner("sp"))
        self.stack.close()
        return nc


def kernel(**inputs):
    k = K()
    nc = k.build([k.ln_phase, k.gate_phase, k.tmix_phase, k.ffn_phase])
    in_maps = [_slices_for_core(inputs, c) for c in range(8)]
    res = run_bass_kernel_spmd(nc, in_maps, core_ids=list(range(8)))
    out = np.zeros((B, T, C), np.float32)
    for c in range(8):
        b, half = c // 2, c % 2
        out[b, half * OWN:(half + 1) * OWN] = res.results[c]["out"]
    return out


# version 27
# speedup vs baseline: 1.0632x; 1.0188x over previous
import numpy as np
import ml_dtypes
from contextlib import ExitStack

import concourse.bass as bass
import concourse.mybir as mybir
from concourse.bass_utils import run_bass_kernel_spmd

F32 = mybir.dt.float32
BF16 = mybir.dt.bfloat16
AF = mybir.ActivationFunctionType
ALU = mybir.AluOpType
AX = mybir.AxisListType
BFNP = ml_dtypes.bfloat16

B, T, C = 4, 4096, 1024
OWN = T // 2
HEAD = 64
H = C // HEAD
NPAIR = H // 2
LORA_MIX = 32
LORA_DECAY = 64
E = 8
GN_EPS = 0.00064
LN_EPS = 1e-5
CK = 128
NCK_OWN = OWN // CK
NCK_ST = OWN // CK
NCC = C // 128
TB = 512
NTB_OWN = OWN // TB
NTT_OWN = OWN // 128
ATT_SCALE = 0.125
ATT_SHIFT = -10.0

ENGINES = ("pe", "act", "dve", "pool", "sp")


class Prog:

    def __init__(self, nc):
        self.nc = nc
        self.ops = {e: [] for e in ENGINES}
        self.marked = {e: set() for e in ENGINES}
        self.dma_tokens = set()

    def op(self, eng, fn, deps=(), dma=False):
        clean = [d for d in deps if d is not None]
        idx = len(self.ops[eng])
        self.ops[eng].append((fn, clean, 16 if dma else 1))
        if dma:
            self.marked[eng].add(idx)
            self.dma_tokens.add((eng, idx))
        return (eng, idx)

    @staticmethod
    def _implied(e, e2):
        return e == e2 == "pe"

    def finalize(self, engs, final_tokens):
        nc = self.nc
        for e in ENGINES:
            for i, (_, deps, _) in enumerate(self.ops[e]):
                for (e2, j) in deps:
                    if not self._implied(e, e2):
                        self.marked[e2].add(j)
        for d in final_tokens:
            self.marked[d[0]].add(d[1])
        self.semkey = {}
        self.cum = {}
        for e in ENGINES:
            counts = {}
            keys = []
            cums = []
            for i, (_, _, inc) in enumerate(self.ops[e]):
                key = "actd" if (e == "act" and (e, i) in self.dma_tokens) else e
                if i in self.marked[e]:
                    counts[key] = counts.get(key, 0) + inc
                keys.append(key)
                cums.append(counts.get(key, 0))
            self.semkey[e] = keys
            self.cum[e] = cums
        sems = dict(self._sems)

        def run_engine(e):
            def body(engine):
                hw = {}
                def wait(e2, j):
                    v = self.cum[e2][j]
                    key = self.semkey[e2][j]
                    assert j in self.marked[e2]
                    if v > hw.get(key, 0):
                        engine.wait_ge(sems[key], v)
                        hw[key] = v
                for i, (fn, deps, inc) in enumerate(self.ops[e]):
                    for (e2, j) in deps:
                        if self._implied(e, e2):
                            continue
                        wait(e2, j)
                    ins = fn(engine)
                    if i in self.marked[e]:
                        assert ins is not None, f"marked op {e}[{i}] returned None"
                        ins.then_inc(sems[self.semkey[e][i]], inc)
                if e == "sp":
                    for (e2, j) in final_tokens:
                        wait(e2, j)
            return body
        return run_engine


class Buf:
    __slots__ = ("w", "r", "sc")

    def __init__(self):
        self.w = None
        self.r = []
        self.sc = False

    def rd(self):
        return [self.w] if self.w is not None else []

    def wr(self):
        d = list(self.r)
        if self.w is not None:
            d.append(self.w)
        return d

    def did_read(self, tok):
        for i, (e, j) in enumerate(self.r):
            if e == tok[0]:
                if tok[1] > j:
                    self.r[i] = tok
                return
        self.r.append(tok)

    def did_write(self, tok):
        self.w = tok
        self.r = []


SCALAR_BUFS = {"consts", "lnst0", "lnst1", "gst", "comb", "attn_st",
               "dsb", "gnstats"}


def _slices_for_core(inputs, core):
    b = core // 2
    half = core % 2
    o0 = half * OWN

    g = lambda k: np.asarray(inputs[k], dtype=np.float32)
    x = g("L0_attn_query")[b]
    x_own = x[o0:o0 + OWN]

    col8 = lambda v: np.ascontiguousarray(np.asarray(v, np.float32).reshape(NCC, 128).T)

    d = {
        "x_full": np.ascontiguousarray(x),
        "x_own": np.ascontiguousarray(x_own),
        "is_odd": np.full((128, 1), float(half), np.float32),
        "maa_x": col8(g("L0_attn_time_maa_x")),
        "maa_w": col8(g("L0_attn_time_maa_w")),
        "maa_k": col8(g("L0_attn_time_maa_k")),
        "maa_v": col8(g("L0_attn_time_maa_v")),
        "maa_r": col8(g("L0_attn_time_maa_r")),
        "w1": g("L0_attn_time_maa_w1"),
        "w2": g("L0_attn_time_maa_w2").reshape(128, 1024),
        "tdecay_row": g("L0_attn_time_decay").reshape(1, C),
        "dw1": g("L0_attn_time_decay_w1"),
        "dw2": g("L0_attn_time_decay_w2"),
        "w_r": g("L0_attn_W_r"),
        "w_k": g("L0_attn_W_k"),
        "w_v": g("L0_attn_W_v"),
        "w_o": g("L0_attn_W_o"),
        "ln_w": col8(g("L0_attn_ln_w")),
        "ln_b": col8(g("L0_attn_ln_b")),
        "w_qkv": g("L0_ffn_W_qkv"),
        "bq_col": col8(g("L0_ffn_b_qkv")[0:C]),
        "bk_col": col8(g("L0_ffn_b_qkv")[C:2 * C]),
        "bv_row": g("L0_ffn_b_qkv")[2 * C:3 * C].reshape(1, C),
        "w_gate": g("L0_ffn_W_gate"),
        "b_gate": g("L0_ffn_b_gate").reshape(1, E),
        "w_exp": g("L0_ffn_W_experts").reshape(E * C, C),
        "b_exp": g("L0_ffn_b_experts"),
    }
    fa = g("L0_attn_time_faaaa")
    useg = np.zeros((128, H), np.float32)
    for p in range(NPAIR):
        useg[0:64, 2 * p] = fa[2 * p]
        useg[64:128, 2 * p + 1] = fa[2 * p + 1]
    d["useg"] = useg
    return d


def _consts():
    ident = np.eye(128, dtype=BFNP)
    ident_f = np.eye(128, dtype=np.float32)
    ltneg = -np.triu(np.ones((128, 128), np.float32)).astype(BFNP)
    mstrict = np.triu(np.ones((128, 128), np.float32), 1)
    mstrict4 = np.tile(mstrict, (1, 4)).astype(BFNP)
    ones_row = np.ones((1, 128), BFNP)
    return ident, ident_f, ltneg, mstrict4, ones_row


class K:

    def __init__(self, debug_keys=()):
        nc = bass.Bass("TRN2", target_bir_lowering=False, debug=False)
        self.nc = nc
        self.p = Prog(nc)
        self.stack = ExitStack()
        self.debug_keys = set(debug_keys)
        self.dbg = {}

        ap = lambda name, shape, dt=F32, kind="ExternalInput": nc.dram_tensor(
            name, list(shape), dt, kind=kind)
        self.in_x_full = ap("x_full", (T, C))
        self.in_x_own = ap("x_own", (OWN, C))
        self.in_is_odd = ap("is_odd", (128, 1))
        for nm, sh in [
            ("maa_x", (128, 8)), ("maa_w", (128, 8)), ("maa_k", (128, 8)),
            ("maa_v", (128, 8)), ("maa_r", (128, 8)),
            ("w1", (C, 128)), ("w2", (128, C)),
            ("tdecay_row", (1, C)), ("dw1", (C, 64)), ("dw2", (64, C)),
            ("w_r", (C, C)), ("w_k", (C, C)), ("w_v", (C, C)), ("w_o", (C, C)),
            ("ln_w", (128, 8)), ("ln_b", (128, 8)),
            ("w_qkv", (C, 3 * C)), ("bq_col", (128, 8)), ("bk_col", (128, 8)),
            ("bv_row", (1, C)), ("w_gate", (C, E)), ("b_gate", (1, E)),
            ("w_exp", (E * C, C)), ("b_exp", (E, C)), ("useg", (128, H)),
        ]:
            setattr(self, "in_" + nm, ap(nm, sh))
        self.out = ap("out", (OWN, C), kind="ExternalOutput")

        ident, ident_f, ltneg, mstrict4, ones_row = _consts()
        self.c_ident = nc.inline_tensor(ident, "c_ident")
        self.c_ident_f = nc.inline_tensor(ident_f, "c_ident_f")
        self.c_ltneg = nc.inline_tensor(ltneg, "c_ltneg")
        self.c_mstrict4 = nc.inline_tensor(mstrict4, "c_mstrict4")
        self.c_ones_row = nc.inline_tensor(ones_row, "c_ones_row")

        dram = lambda name, shape, dt: nc.dram_tensor(name, list(shape), dt, kind="Internal")
        self.d_xn = dram("d_xn", (T, C), BF16)
        self.d_xno = dram("d_xno", (OWN, C), BF16)
        self.d_yn = dram("d_yn", (OWN, C), BF16)
        self.d_wr = dram("d_wr", (C, C), BF16)
        self.d_wk = dram("d_wk", (C, C), BF16)
        self.d_wv = dram("d_wv", (C, C), BF16)
        self.d_wo = dram("d_wo", (C, C), BF16)
        self.d_wqkv = dram("d_wqkv", (C, 3 * C), BF16)
        self.d_wexp = dram("d_wexp", (E * C, C), BF16)
        self.d_vaug = dram("d_vaug", (T, 2 * 520), BF16)

        self.p._sems = {}
        for e in list(ENGINES) + ["actd"]:
            self.p._sems[e] = self.stack.enter_context(nc.semaphore("sem_" + e))

        sb = self._sb
        self.xn_fm_t = sb("xn_fm_t", (128, NCC * T), BF16)
        self.xno_fm_t = sb("xno_fm_t", (128, NCC * OWN), BF16)
        self.xn_fm = [self.xn_fm_t[:, i * T:(i + 1) * T] for i in range(NCC)]
        self.xno_fm = [self.xno_fm_t[:, i * OWN:(i + 1) * OWN] for i in range(NCC)]
        self.scr_a = sb("scr_a", (128, 12288), BF16)
        self.scr_b = sb("scr_b", (128, 4096), F32)
        self.scr_c = sb("scr_c", (128, 8192), BF16)
        self.xi = sb("xi_tb", (128, 4096), BF16)
        self.tanh1 = sb("tanh1", (128, 1024), BF16)
        self.eta = sb("eta", (64, 1024), BF16)
        self.pmask = sb("pmask_yn", (128, 2048), BF16)
        self.S0 = sb("S0", (128, 512), F32)
        self.S1 = sb("S1", (128, 512), F32)
        self.Sbf = sb("Sbf", (128, 512), BF16)
        self.ws = sb("ws", (128, 4096), BF16)
        self.stats = sb("stats", (128, 160), F32)
        self.xxb = sb("xxb", (128, 2048), BF16)
        self.lnb_bf = sb("lnb_bf", (128, 8), BF16)
        self.outt = sb("outt", (128, 1024), F32)
        xi_addr = nc.lookup_mloc(self.xi).addr
        self.acc_hi = nc.alloc_sbuf_tensor_at("acc_hi", [128, 4096], F32, offset=xi_addr)
        self.xtile = nc.alloc_sbuf_tensor_at("xtile", [128, 1024], F32, offset=xi_addr + 16384)
        assert nc.lookup_mloc(self.tanh1).addr == xi_addr + 8192
        assert nc.lookup_mloc(self.pmask).addr == xi_addr + 12288
        assert nc.lookup_mloc(self.S0).addr == xi_addr + 16384
        assert nc.lookup_mloc(self.Sbf).addr == xi_addr + 20480
        self.comb = sb("comb", (128, 128), F32)
        self.dsb = sb("dsb", (128, 32), F32)
        self.eal = sb("eal", (128, 8), F32)
        self.prev = sb("prev", (128, 8), F32)
        self.bb = sb("bb", (1, 512), BF16)
        self.consts = sb("consts", (128, 16), F32)
        self.ident_sb = sb("ident_sb", (128, 128), BF16)
        self.ident_f32 = sb("ident_f32", (128, 128), F32)
        self.ltneg_sb = sb("ltneg_sb", (128, 128), BF16)
        self.mstrict_sb = sb("mstrict_sb", (128, 512), BF16)
        self.onesrow_sb = sb("onesrow_sb", (1, 128), BF16)
        self.w1_sb = sb("w1_sb", (128, 8 * 128), BF16)
        self.w2_sb = sb("w2_sb", (128, C), BF16)
        self.dw1_sb = sb("dw1_sb", (128, 8 * 64), BF16)
        self.dw2_sb = sb("dw2_sb", (64, C), BF16)
        self.vec_sb = sb("vec_sb", (128, 96), F32)
        self.tdec_row = sb("s_tdec_row", (1, C), BF16)
        self.bv_row = sb("s_bv_row", (1, C), BF16)
        self.bgate_row = sb("s_bgate_row", (1, E), BF16)
        self.bexp_sb = sb("s_bexp", (E, C), BF16)
        self.wgate_sb = sb("wgate_sb", (128, 8 * E), BF16)
        self.useg_sb = sb("useg_sb", (128, H), BF16)

        ps = lambda name, shape, dt=F32: self.stack.enter_context(nc.psum_tensor(name, list(shape), dt))
        self.psA = ps("psA", (128, 1024))
        self.psB = ps("psB", (128, 1024))
        self.psC = ps("psC", (128, 512))
        self.psD = ps("psD", (128, 512))
        self.psE = ps("psE", (128, 512))
        self.psG = ps("psG", (128, 1024), BF16)

        self.bufs = {}

    def _sb(self, name, shape, dt):
        return self.stack.enter_context(self.nc.sbuf_tensor(name, list(shape), dt))

    def buf(self, name):
        if name not in self.bufs:
            self.bufs[name] = Buf()
            if name in SCALAR_BUFS:
                self.bufs[name].sc = True
        return self.bufs[name]

    def emit(self, eng, fn, reads=(), writes=(), extra=(), dma=False):
        deps = list(extra)
        dmatok = self.p.dma_tokens

        def keep(d, b):
            return True

        for b in reads:
            deps += [d for d in b.rd() if keep(d, b)]
        for b in writes:
            deps += [d for d in b.wr() if keep(d, b)]
        if dma and eng in ("act", "pool", "sp"):
            last = getattr(self, "_last_dma", {}).get(eng)
            if last is not None:
                deps.append(last)
        tok = self.p.op(eng, fn, deps, dma=dma)
        if dma and eng in ("act", "pool", "sp"):
            if not hasattr(self, "_last_dma"):
                self._last_dma = {}
            self._last_dma[eng] = tok
        for b in reads:
            b.did_read(tok)
        for b in writes:
            b.did_write(tok)
        return tok

    CONST_COLS = {-1.0: 0, 0.125: 1, ATT_SHIFT: 2, LN_EPS: 3, GN_EPS: 4,
                  1.0 / 1024: 5, 1.0 / 64: 6, 1.0: 7, 0.5: 8, -1.0 / 64: 9}

    def ccol(self, val):
        return self.consts[:, self.CONST_COLS[val]:self.CONST_COLS[val] + 1]

    def _init_consts(self):
        bcon = self.buf("consts")
        for val, i in self.CONST_COLS.items():
            self.emit("dve", lambda v, val=val, i=i: v.memset(self.consts[:, i:i + 1], val),
                      writes=[bcon])

    def prologue(self):
        n = self

        def ld(eng, dst_ap, src_ap, bufname, deps=()):
            b = n.buf(bufname)
            return n.emit(eng, lambda e, d=dst_ap, s=src_ap: e.dma_start(out=d, in_=s),
                          writes=[b], extra=deps, dma=True)

        prev = [None]

        def ldp(dst_ap, src_ap, bufname):
            t = ld("pool", dst_ap, src_ap, bufname,
                   deps=[prev[0]] if prev[0] else ())
            prev[0] = t

        ldp(n.useg_sb[:, :], n.in_useg.ap(), "useg")
        ldp(n.dw2_sb[:, :], n.in_dw2.ap(), "dw2")
        ldp(n.w2_sb[:, :], n.in_w2.ap(), "w2")
        ldp(n.w1_sb.ap().rearrange("p (cc l) -> p cc l", cc=8),
            n.in_w1.ap().rearrange("(cc p) l -> p cc l", p=128), "w1")
        ldp(n.dw1_sb.ap().rearrange("p (cc l) -> p cc l", cc=8),
            n.in_dw1.ap().rearrange("(cc p) l -> p cc l", p=128), "dw1")
        ldp(n.wgate_sb.ap().rearrange("p (cc e) -> p cc e", cc=8),
            n.in_w_gate.ap().rearrange("(cc p) e -> p cc e", p=128), "wgate")
        ldp(n.tdec_row[:, :], n.in_tdecay_row.ap(), "tdec_row")
        ldp(n.bv_row[:, :], n.in_bv_row.ap(), "bv_row")
        ldp(n.bgate_row[:, :], n.in_b_gate.ap(), "bgate_row")
        ldp(n.bexp_sb[:, :], n.in_b_exp.ap(), "bexp")
        ldp(n.onesrow_sb[:, :], n.c_ones_row.ap(), "onesrow")
        for nm in ("wk", "wv", "wr", "wqkv", "wo"):
            src = getattr(n, "in_w_" + nm[1:]) if nm in ("wr", "wk", "wv", "wo") else n.in_w_qkv
            dst = getattr(n, "d_" + nm)
            ldp(dst.ap(), src.ap(), "d_" + nm)
        for e8 in range(E):
            ldp(n.d_wexp.ap()[e8 * C:(e8 + 1) * C, :],
                n.in_w_exp.ap()[e8 * C:(e8 + 1) * C, :], f"d_wexp{e8}")

        ld("sp", n.ident_sb[:, :], n.c_ident.ap(), "ident")
        ld("sp", n.ident_f32[:, :], n.c_ident_f.ap(), "ident_f")
        ld("sp", n.ltneg_sb[:, :], n.c_ltneg.ap(), "ltneg")
        ld("sp", n.mstrict_sb[:, :], n.c_mstrict4.ap(), "mstrict")
        vecs = [("maa_x", 0), ("maa_w", 8), ("maa_k", 16), ("maa_v", 24),
                ("maa_r", 32), ("ln_w", 40), ("ln_b", 48), ("bq_col", 56),
                ("bk_col", 64)]
        for nm, off in vecs:
            key = {"bq_col": "bq", "bk_col": "bk"}.get(nm, nm)
            ld("sp", n.vec_sb[:, off:off + 8], getattr(n, "in_" + nm).ap(), "vec_" + key)
        ld("sp", n.vec_sb[:, 72:73], n.in_is_odd.ap(), "vec_is_odd")

    def vcol(self, name, i=0):
        off = {"maa_x": 0, "maa_w": 8, "maa_k": 16, "maa_v": 24, "maa_r": 32,
               "ln_w": 40, "ln_b": 48, "bq": 56, "bk": 64, "is_odd": 72}[name]
        return self.vec_sb[:, off + i:off + i + 1]

    def vbuf(self, name):
        return self.buf("vec_" + name)

    def ln_phase(self):
        n = self
        x_t = n.scr_b[:, 0:3 * C]
        xsq = n.scr_b[:, 3 * C:4 * C]
        st = n.stats[:, 0:16]
        xnout = n.scr_c[:, 0:4 * C]
        bx = [n.buf(f"lnx{i}") for i in range(3)]
        bsq = n.buf("lnsq0")
        bst = [n.buf(f"lnst{i}") for i in range(2)]
        bout = [n.buf(f"lnout{i}") for i in range(4)]
        bfm = [n.buf(f"xn_fm{cc}") for cc in range(NCC)]
        bfo = [n.buf(f"xno_fm{cc}") for cc in range(NCC)]
        pG = n.psbuf("G")
        bcon = n.buf("consts")

        jobs = []
        for tt in range(T // 128):
            jobs.append((n.in_x_full.ap()[tt * 128:(tt + 1) * 128, :], True, tt))
        for tt in range(OWN // 128):
            jobs.append((n.in_x_own.ap()[tt * 128:(tt + 1) * 128, :], False, tt))

        for j, (src, full, tt) in enumerate(jobs):
            k = j % 2
            k3 = j % 3
            k4 = j % 4
            xt = x_t[:, k3 * C:(k3 + 1) * C]
            s1 = st[:, k * 8 + 0:k * 8 + 1]
            s2 = st[:, k * 8 + 1:k * 8 + 2]
            mean = st[:, k * 8 + 2:k * 8 + 3]
            var = st[:, k * 8 + 3:k * 8 + 4]
            rstd = st[:, k * 8 + 4:k * 8 + 5]
            m2 = st[:, k * 8 + 5:k * 8 + 6]
            xo = xnout[:, k4 * C:(k4 + 1) * C]
            n.emit("sp", lambda e, xt=xt, src=src: e.dma_start(out=xt, in_=src),
                   writes=[bx[k3]], dma=True)
            n.emit("dve", lambda v, xt=xt, s1=s1: v.reduce_sum(s1, xt, axis=AX.X),
                   reads=[bx[k3]], writes=[bst[k]])
            n.emit("act", lambda a, xt=xt, s2=s2: a.activation(
                out=xsq, in_=xt, func=AF.Square, bias=0.0, scale=1.0, accum_out=s2),
                   reads=[bx[k3]], writes=[bsq, bst[k]])
            n.emit("dve", lambda v, s1=s1, mean=mean: v.tensor_scalar(
                mean, s1, n.ccol(1.0 / 1024), None, ALU.mult),
                   reads=[bst[k], bcon], writes=[bst[k]])
            n.emit("dve", lambda v, mean=mean, m2=m2: v.tensor_mul(m2, mean, mean),
                   reads=[bst[k]], writes=[bst[k]])
            n.emit("dve", lambda v, s2=s2, m2=m2, var=var: v.scalar_tensor_tensor(
                var, s2, n.ccol(1.0 / 1024), m2, ALU.mult, ALU.subtract),
                   reads=[bst[k], bcon], writes=[bst[k]])
            n.emit("act", lambda a, var=var, rstd=rstd: a.activation(
                out=rstd, in_=var, func=AF.Sqrt, bias=n.ccol(LN_EPS), scale=1.0),
                   reads=[bst[k], bcon], writes=[bst[k]])
            n.emit("dve", lambda v, rstd=rstd: v.reciprocal(rstd, rstd),
                   reads=[bst[k]], writes=[bst[k]])
            n.emit("dve", lambda v, xt=xt, xo=xo, mean=mean, rstd=rstd: v.tensor_scalar(
                xo, xt, mean, rstd, ALU.subtract, ALU.mult),
                   reads=[bx[k3], bst[k]], writes=[bout[k4]])
            for cc in range(NCC):
                n.emit("pe", lambda t, xo=xo, cc=cc: t.transpose(
                    n.psG[:, cc * 128:(cc + 1) * 128],
                    xo[:, cc * 128:(cc + 1) * 128], n.ident_sb[:, :]),
                    reads=[bout[k4], n.buf("ident")], writes=[pG])
            fmt = n.xn_fm_t if full else n.xno_fm_t
            SEQ = T if full else OWN
            dst3 = fmt.ap().rearrange("p (cc t) -> p cc t", cc=NCC)[
                :, :, tt * 128:(tt + 1) * 128]
            n.emit("act", lambda a, dst3=dst3: a.activation(
                out=dst3, in_=n.psG.ap().rearrange("p (cc t) -> p cc t", cc=NCC),
                func=AF.Copy, bias=0.0, scale=1.0),
                reads=[pG], writes=(bfm if full else bfo))
        bprev = n.buf("prev")
        for cc in range(NCC):
            n.emit("dve", lambda v, cc=cc: v.tensor_scalar(
                n.prev[:, cc:cc + 1], n.xn_fm[cc][:, OWN - 1:OWN],
                n.vcol("is_odd"), None, ALU.mult),
                   reads=[n.buf(f"xn_fm{cc}"), n.vbuf("is_odd")], writes=[bprev])
        for cc in range(NCC):
            n.debug_dump(f"xn_fm{cc}", n.xn_fm[cc][:, :], (128, T),
                         [n.buf(f"xn_fm{cc}")], BF16)
            n.debug_dump(f"xno_fm{cc}", n.xno_fm[cc][:, :], (128, OWN),
                         [n.buf(f"xno_fm{cc}")], BF16)

    def MM(self, out_ap, lhsT_ap, rhs_ap, start, stop, pos=None, reads=(), writes=()):
        return self.emit("pe", lambda t: t.matmul(
            out_ap, lhsT_ap, rhs_ap, start=start, stop=stop, tile_position=pos),
            reads=reads, writes=writes)

    def psbuf(self, name):
        return self.buf("ps_" + name)

    def _xi_pass(self, stream, t0, width, own, tslot=0):
        n = self
        self._xi_par = getattr(self, "_xi_par", 0) ^ 1
        col0 = 0 if width > 256 else self._xi_par * 2048
        bxis = [n.buf(f"xi_{col0}_{cc}") for cc in range(NCC)]
        bxx = n.buf("xxb")
        f = {"w": 0, "k": 1, "v": 2, "r": 3}.get(stream)
        if width > 256:
            for cc in range(NCC):
                bxis[cc].r += n.buf(f"xi_2048_{cc}").r + n.buf(f"xi_0_{cc}").r
        big = n.xno_fm_t if own else n.xn_fm_t
        SEQ = OWN if own else T
        bsrc = [n.buf(("xno_fm" if own else "xn_fm") + str(cc)) for cc in range(NCC)]
        nh = NCC * width // 2048
        ccph = NCC // nh
        for hxx in range(nh):
            c0 = hxx * ccph
            xdst = n.xxb.ap()[:, 0:ccph * width].rearrange("p (cc w) -> p cc w", cc=ccph)                 if nh == 2 and False else None
            xdst = n.xxb.ap()[:, hxx * 0:2048][:, 0:ccph * width].rearrange(
                "p (cc w) -> p cc w", cc=ccph)
            sview = big.ap()[:, c0 * SEQ:(c0 + ccph) * SEQ].rearrange(
                "p (cc t) -> p cc t", cc=ccph)
            if t0 == 0:
                n.emit("dve", lambda v, xdst=xdst, sview=sview: v.tensor_sub(
                    xdst[:, :, 1:width], sview[:, :, 0:width - 1], sview[:, :, 1:width]),
                    reads=bsrc, writes=[bxx])
                if own:
                    n.emit("dve", lambda v, xdst=xdst, sview=sview, c0=c0: v.tensor_sub(
                        xdst[:, :, 0:1],
                        n.prev.ap()[:, c0:c0 + ccph].rearrange("p c -> p c 1" if False else "p (c o) -> p c o", o=1),
                        sview[:, :, 0:1]),
                        reads=bsrc + [n.buf("prev")], writes=[bxx])
                else:
                    n.emit("dve", lambda v, xdst=xdst, sview=sview: v.tensor_scalar(
                        xdst[:, :, 0:1], sview[:, :, 0:1], -1.0, None, ALU.mult),
                        reads=bsrc, writes=[bxx])
            else:
                n.emit("dve", lambda v, xdst=xdst, sview=sview: v.tensor_sub(
                    xdst[:, :, 0:width], sview[:, :, t0 - 1:t0 + width - 1],
                    sview[:, :, t0:t0 + width]),
                    reads=bsrc, writes=[bxx])
            for ccl in range(ccph):
                cc = c0 + ccl
                xx_ap = n.xxb[:, ccl * width:(ccl + 1) * width]
                xi_ap = n.xi[:, col0 + cc * width:col0 + (cc + 1) * width]
                src2 = big[:, cc * SEQ + t0:cc * SEQ + t0 + width]
                if stream == "x":
                    n.emit("dve", lambda v, xx_ap=xx_ap, xi_ap=xi_ap, src2=src2, cc=cc:
                           v.scalar_tensor_tensor(
                               xi_ap, xx_ap, n.vcol("maa_x", cc), src2,
                               ALU.mult, ALU.add),
                           reads=[bxx, bsrc[cc], n.vbuf("maa_x")], writes=[bxis[cc]])
                    continue
                mslot = cc % 4
                mtens = n.psA if mslot < 2 else n.psB
                mps = mtens[:, (mslot % 2) * 512:(mslot % 2) * 512 + width]
                mb = n.psbuf(("A", "A", "B", "B")[mslot] + str(mslot % 2))
                n.MM(mps, n.w2_sb[f * 32:(f + 1) * 32, cc * 128:(cc + 1) * 128],
                     n.tanh1[f * 32:(f + 1) * 32, tslot * 512:tslot * 512 + width],
                     True, True, pos=(32 * f, 0),
                     reads=[n.buf("w2"), n.buf(f"tanh1_{tslot}")], writes=[mb])
                nmcol = "maa_" + stream
                n.emit("dve", lambda v, mps=mps, xx_ap=xx_ap, xi_ap=xi_ap, cc=cc, nmcol=nmcol:
                       v.scalar_tensor_tensor(xi_ap, mps, n.vcol(nmcol, cc), xx_ap,
                                              ALU.add, ALU.mult),
                       reads=[mb, bxx, n.vbuf(nmcol)], writes=[bxis[cc]])
            if stream != "x":
                xi3 = n.xi.ap()[:, col0 + c0 * width:col0 + (c0 + ccph) * width].rearrange(
                    "p (cc w) -> p cc w", cc=ccph)
                n.emit("dve", lambda v, xi3=xi3, sview=sview: v.tensor_add(
                    xi3, xi3, sview[:, :, t0:t0 + width]),
                    reads=bsrc + [bxis[c0 + i] for i in range(ccph)],
                    writes=[bxis[c0 + i] for i in range(ccph)])
        return col0, bxis

    def _proj_fm(self, wdram, wbuf_name, out_col0, xic, width):
        n = self
        xcol0, bxis = xic
        for q in range(4):
            slot = q % 2
            bws = n.buf("ws_a" if slot == 0 else "ws_b")
            ws_ap = n.ws.ap()[:, slot * 2048:(slot + 1) * 2048].rearrange(
                "p (cc w) -> p cc w", cc=8)
            src = wdram.ap().rearrange("(cc p) c -> p cc c", p=128)[:, :, q * 256:(q + 1) * 256]
            n.emit("sp", lambda e, ws_ap=ws_ap, src=src: e.dma_start(out=ws_ap, in_=src),
                   reads=[n.buf(wbuf_name)], writes=[bws], dma=True)
            for ic2 in range(2):
                ic = q * 2 + ic2
                ps = n.psC if ic2 == 0 else n.psD
                pb = n.psbuf("C" if ic2 == 0 else "D")
                for cc in range(NCC):
                    n.MM(ps[:, 0:width],
                         n.ws[:, slot * 2048 + cc * 256 + ic2 * 128:
                              slot * 2048 + cc * 256 + ic2 * 128 + 128],
                         n.xi[:, xcol0 + cc * width:xcol0 + (cc + 1) * width],
                         cc == 0, cc == NCC - 1,
                         reads=[bws, bxis[cc]], writes=[pb])
                dst = n.scr_a[:, out_col0 + ic * width:out_col0 + (ic + 1) * width]
                n.emit("act", lambda a, dst=dst, ps=ps: a.activation(
                    out=dst, in_=ps[:, 0:width], func=AF.Copy, bias=0.0, scale=1.0),
                    reads=[pb], writes=[n.buf("scra")])

    def _proj_tm(self, wdram, wbuf_name, out_col0, xic, width):
        n = self
        xcol0, bxis = xic
        ntt = width // 128
        for fh in range(2):
            if fh == 0:
                wtile = n.ws.ap()[:, 0:4096]
                bws = [n.buf("ws_a"), n.buf("ws_b")]
            else:
                wtile = n.scr_a.ap()[:, 8192:12288]
                bws = [n.buf("ws_wo")]
            src = wdram.ap().rearrange("(cc p) c -> p cc c", p=128)[
                :, :, fh * 512:(fh + 1) * 512]
            n.emit("sp", lambda e, wtile=wtile, src=src: e.dma_start(
                out=wtile.rearrange("p (cc w) -> p cc w", cc=8), in_=src),
                reads=[n.buf(wbuf_name)], writes=bws, dma=True)
            for tt in range(ntt):
                ps = n.psC if tt % 2 == 0 else n.psD
                pb = n.psbuf("C" if tt % 2 == 0 else "D")
                for cc in range(NCC):
                    n.MM(ps[:, 0:512],
                         n.xi[:, xcol0 + cc * width + tt * 128:xcol0 + cc * width + (tt + 1) * 128],
                         wtile[:, cc * 512:(cc + 1) * 512],
                         cc == 0, cc == NCC - 1,
                         reads=bws + [bxis[cc]], writes=[pb])
                dst = n.scr_a[:, out_col0 + tt * 1024 + fh * 512:
                              out_col0 + tt * 1024 + (fh + 1) * 512]
                n.emit("act", lambda a, dst=dst, ps=ps: a.activation(
                    out=dst, in_=ps[:, 0:512], func=AF.Copy, bias=0.0, scale=1.0),
                    reads=[pb], writes=[n.buf("scra")])

    def _lora_pass(self, t0, width, own, tslot):
        n = self
        xcol0, bxis = self._xi_pass("x", t0, width, own)
        pb = n.psbuf("C")
        for cc in range(NCC):
            n.MM(n.psC[:, 0:width], n.w1_sb[:, cc * 128:(cc + 1) * 128],
                 n.xi[:, xcol0 + cc * width:xcol0 + (cc + 1) * width],
                 cc == 0, cc == NCC - 1,
                 reads=[n.buf("w1"), bxis[cc]], writes=[pb])
        n.emit("act", lambda a: a.activation(
            out=n.tanh1[:, tslot * 512:tslot * 512 + width], in_=n.psC[:, 0:width],
            func=AF.Tanh, bias=0.0, scale=1.0),
            reads=[pb], writes=[n.buf(f"tanh1_{tslot}")])

    def _eta_pass(self, t0, width, own, tslot):
        n = self
        xcol0, bxis = self._xi_pass("w", t0, width, own, tslot=tslot)
        pb = n.psbuf("E")
        for cc in range(NCC):
            n.MM(n.psE[0:64, 0:width], n.dw1_sb[:, cc * 64:(cc + 1) * 64],
                 n.xi[:, xcol0 + cc * width:xcol0 + (cc + 1) * width],
                 cc == 0, cc == NCC - 1,
                 reads=[n.buf("dw1"), bxis[cc]], writes=[pb])
        n.emit("act", lambda a: a.activation(
            out=n.eta[0:64, tslot * 512:tslot * 512 + width], in_=n.psE[0:64, 0:width],
            func=AF.Tanh, bias=0.0, scale=1.0),
            reads=[pb], writes=[n.buf(f"eta_{tslot}")])

    def _wkv_front(self, ckw, own, kcol, vcol, chunk_idx, tslot=0):
        n = self
        par2 = chunk_idx % 2
        s_old = n.S0 if par2 == 0 else n.S1
        s_new = n.S1 if par2 == 0 else n.S0
        bso = n.buf("S0" if par2 == 0 else "S1")
        bsn = n.buf("S1" if par2 == 0 else "S0")
        bsbf = n.buf("Sbf")
        pA0, pA1 = n.psbuf("A0"), n.psbuf("A1")
        pB0, pB1 = n.psbuf("B0"), n.psbuf("B1")
        pC, pD, pG = n.psbuf("C"), n.psbuf("D"), n.psbuf("G")
        CSLOT = lambda i: n.scr_c[:, i * 1024:(i + 1) * 1024]
        bsl = [n.buf(f"cs{i}") for i in range(8)]
        ktm = n.scr_a[:, kcol:kcol + 1024]
        vtm = n.scr_a[:, vcol:vcol + 1024]
        bscra = n.buf("scra")
        beta = n.buf(f"eta_{tslot}")

        for half in range(2):
            pb = pA0 if half == 0 else pA1
            n.MM(n.psA[:, half * 512:(half + 1) * 512],
                 n.eta[0:64, tslot * 512 + ckw * 128:tslot * 512 + (ckw + 1) * 128],
                 n.dw2_sb[0:64, half * 512:(half + 1) * 512], True, False,
                 reads=[beta, n.buf("dw2")], writes=[pb])
            n.MM(n.psA[:, half * 512:(half + 1) * 512],
                 n.onesrow_sb[0:1, 0:128],
                 n.tdec_row[0:1, half * 512:(half + 1) * 512], False, True,
                 reads=[n.buf("onesrow"), n.buf("tdec_row")], writes=[pb])
        n.emit("act", lambda a: a.activation(out=CSLOT(0), in_=n.psA[:, :], func=AF.Exp,
                                             bias=0.0, scale=1.0),
               reads=[pA0, pA1], writes=[bsl[0]])
        for half in range(2):
            pb = pA0 if half == 0 else pA1
            n.MM(n.psA[:, half * 512:(half + 1) * 512], n.ltneg_sb[:, :],
                 CSLOT(0)[:, half * 512:(half + 1) * 512], True, True,
                 reads=[n.buf("ltneg"), bsl[0]], writes=[pb])
        n.emit("act", lambda a: a.activation(out=CSLOT(1), in_=n.psA[:, :], func=AF.Exp,
                                             bias=0.0, scale=n.ccol(-1.0)),
               reads=[pA0, pA1, n.buf("consts")], writes=[bsl[1]])
        if own:
            for ic in range(NCC):
                pb = pB0 if ic < 4 else pB1
                n.MM(n.psB[:, ic * 128:(ic + 1) * 128],
                     CSLOT(0)[:, ic * 128:(ic + 1) * 128], n.ltneg_sb[:, :],
                     True, True, reads=[bsl[0], n.buf("ltneg")], writes=[pb])
            n.emit("act", lambda a: a.activation(
                out=n.eal[:, 0:8],
                in_=n.psB.ap().rearrange("p (ic t) -> p ic t", ic=8)[:, :, 127:128],
                func=AF.Exp, bias=0.0, scale=1.0),
                reads=[pB0, pB1], writes=[n.buf("eal")])
            n.emit("act", lambda a: a.activation(out=CSLOT(3), in_=n.psB[:, :], func=AF.Exp,
                                                 bias=0.0, scale=n.ccol(-1.0)),
                   reads=[pB0, pB1, n.buf("consts")], writes=[bsl[3]])
            n.emit("act", lambda a: a.activation(
                out=CSLOT(4).rearrange("p (ic t) -> p ic t", ic=8)[:, :, 1:128],
                in_=n.psB.ap().rearrange("p (ic t) -> p ic t", ic=8)[:, :, 0:127],
                func=AF.Exp, bias=0.0, scale=1.0),
                reads=[pB0, pB1], writes=[bsl[4]])
            n.emit("dve", lambda v: v.memset(
                CSLOT(4).rearrange("p (ic t) -> p ic t", ic=8)[:, :, 0:1], 1.0),
                writes=[bsl[4]])
            for cc in range(NCC):
                n.emit("pe", lambda t, cc=cc: t.transpose(
                    n.psG[:, cc * 128:(cc + 1) * 128],
                    ktm[:, cc * 128:(cc + 1) * 128], n.ident_sb[:, :]),
                    reads=[bscra, n.buf("ident")], writes=[pG])
            kfm_dst = n.scr_a.ap()[:, 0:2048].rearrange("p (ic w) -> p ic w", ic=8)[
                :, :, ckw * 128:(ckw + 1) * 128]
            n.emit("act", lambda a, kfm_dst=kfm_dst: a.activation(
                out=kfm_dst, in_=n.psG.ap().rearrange("p (ic w) -> p ic w", ic=8),
                func=AF.Copy, bias=0.0, scale=1.0),
                reads=[pG], writes=[bscra])
        else:
            for ic in range(NCC):
                n.MM(n.psB[:, ic:ic + 1], CSLOT(0)[:, ic * 128:(ic + 1) * 128],
                     n.ltneg_sb[:, 127:128], True, True,
                     reads=[bsl[0], n.buf("ltneg")], writes=[pB0])
            n.emit("act", lambda a: a.activation(out=n.eal[:, 0:8], in_=n.psB[:, 0:8],
                                                 func=AF.Exp, bias=0.0, scale=1.0),
                   reads=[pB0], writes=[n.buf("eal")])
        n.emit("dve", lambda v: v.tensor_mul(CSLOT(2), ktm, CSLOT(1)),
               reads=[bscra, bsl[1]], writes=[bsl[2]])
        if own:
            rfm = n.scr_a.ap()[:, 2048:4096].rearrange("p (ic w) -> p ic w", ic=8)
            kfm = n.scr_a.ap()[:, 0:2048].rearrange("p (ic w) -> p ic w", ic=8)
            sl = slice(ckw * 128, (ckw + 1) * 128)
            c3 = lambda t: t.rearrange("p (ic w) -> p ic w", ic=8)
            n.emit("dve", lambda v: v.tensor_mul(c3(CSLOT(5)), rfm[:, :, sl], c3(CSLOT(4))),
                   reads=[bscra, bsl[4]], writes=[bsl[5]])
            n.emit("dve", lambda v: v.tensor_mul(c3(CSLOT(6)), kfm[:, :, sl], c3(CSLOT(3))),
                   reads=[bscra, bsl[3]], writes=[bsl[6]])
            n.emit("dve", lambda v: v.tensor_mul(c3(CSLOT(1)), rfm[:, :, sl], kfm[:, :, sl]),
                   reads=[bscra], writes=[bsl[1]])
        for p8 in range(NPAIR):
            ps = n.psC if p8 < 4 else n.psD
            pb = pC if p8 < 4 else pD
            n.MM(ps[:, (p8 % 4) * 128:(p8 % 4 + 1) * 128],
                 CSLOT(2)[:, p8 * 128:(p8 + 1) * 128],
                 vtm[:, p8 * 128:(p8 + 1) * 128],
                 True, True, reads=[bsl[2], bscra], writes=[pb])
        tmp = n.scr_b[:, 2048:2560]
        btmp = n.buf("s_tmp")
        for bank in range(2):
            ps = n.psC if bank == 0 else n.psD
            pb = pC if bank == 0 else pD
            for par in range(2):
                u3 = ps.ap().rearrange("p (q w) -> p q w", q=4)[
                    par * 64:(par + 1) * 64, :, par * 64:(par + 1) * 64]
                so3 = s_old.ap().rearrange("p (q w) -> p q w", q=8)[
                    par * 64:(par + 1) * 64, bank * 4:(bank + 1) * 4, :]
                t3 = tmp.rearrange("p (q w) -> p q w", q=8)[
                    par * 64:(par + 1) * 64, bank * 4:(bank + 1) * 4, :]
                n.emit("dve", lambda v, t3=t3, so3=so3, u3=u3: v.tensor_add(t3, so3, u3),
                       reads=[pb, bso], writes=[btmp])
        for p8 in range(NPAIR):
            n.emit("dve", lambda v, p8=p8: v.tensor_scalar(
                s_new[:, p8 * 64:(p8 + 1) * 64], tmp[:, p8 * 64:(p8 + 1) * 64],
                n.eal[:, p8:p8 + 1], None, ALU.mult),
                reads=[btmp, n.buf("eal")], writes=[bsn])
        if not own:
            n.emit("dve", lambda v: v.tensor_copy(n.Sbf[:, :], s_new[:, :]),
                   reads=[bsn], writes=[bsbf])
            if getattr(n, "dbg_ck", None) == chunk_idx:
                for nm2, islot in [("ew", 0), ("ktil", 2)]:
                    n.debug_dump(f"ck_{nm2}", CSLOT(islot), (128, 1024), [bsl[islot]], BF16)
                n.debug_dump("ck_eal", n.eal[:, 0:8], (128, 8), [n.buf("eal")])
            return
        if own:
            qbank = [n.psA[:, 0:512], n.psA[:, 512:1024], n.psB[:, 0:512], n.psB[:, 512:1024]]
            qb = [pA0, pA1, pB0, pB1]
            for h in range(H):
                ic, par = h // 2, h % 2
                bk, slot = h % 4, h // 4
                n.MM(qbank[bk][:, slot * 128:(slot + 1) * 128],
                     CSLOT(6)[par * 64:par * 64 + 64, ic * 128:(ic + 1) * 128],
                     CSLOT(5)[par * 64:par * 64 + 64, ic * 128:(ic + 1) * 128],
                     True, True, pos=(64 * par, 0),
                     reads=[bsl[6], bsl[5]], writes=[qb[bk]])
            bpm = [n.buf("pm0"), n.buf("pm1")]
            ybank = [n.psC, n.psD]
            yb = [pC, pD]
            for bk in range(4):
                pslot = bk % 2
                pm_ap = n.pmask[:, pslot * 512:(pslot + 1) * 512]
                n.emit("dve", lambda v, pm_ap=pm_ap, bk=bk: v.tensor_mul(
                    pm_ap, qbank[bk], n.mstrict_sb[:, :]),
                    reads=[qb[bk], n.buf("mstrict")], writes=[bpm[pslot]])
                for slot in range(4):
                    h = bk + slot * 4 if False else slot * 4 + bk
                    ic, par = h // 2, h % 2
                    ypar = h % 2
                    n.MM(ybank[ypar][:, (h // 2) * 64:(h // 2) * 64 + 64],
                         pm_ap[:, slot * 128:(slot + 1) * 128],
                         vtm[:, h * 64:(h + 1) * 64],
                         True, False,
                         reads=[bpm[pslot], bscra], writes=[yb[ypar]])
                    n.MM(ybank[ypar][:, (h // 2) * 64:(h // 2) * 64 + 64],
                         CSLOT(5)[par * 64:par * 64 + 64, ic * 128:(ic + 1) * 128],
                         n.Sbf[par * 64:par * 64 + 64, (h // 2) * 64:(h // 2) * 64 + 64],
                         False, True, pos=(64 * par, 0),
                         reads=[bsl[5], bsbf], writes=[yb[ypar]])
            dsb_ap = n.dsb[:, par2 * 16:par2 * 16 + 16]
            for ic in range(NCC):
                n.MM(n.psB[:, 2 * ic:2 * ic + 2],
                     CSLOT(1)[:, ic * 128:(ic + 1) * 128], n.useg_sb[:, 2 * ic:2 * ic + 2],
                     True, True, reads=[bsl[1], n.buf("useg")], writes=[pB0])
            n.emit("dve", lambda v: v.tensor_copy(dsb_ap, n.psB[:, 0:16]),
                   reads=[pB0], writes=[n.buf("dsb")])
            ycopy = n.scr_b[:, 3072:4096]
            byc = n.buf("ycopy")
            for ypar in range(2):
                n.emit("dve", lambda v, ypar=ypar: v.tensor_copy(
                    ycopy[:, ypar * 512:(ypar + 1) * 512], ybank[ypar][:, 0:512]),
                    reads=[yb[ypar]], writes=[byc])
            y_sb = n.scr_b[:, par2 * 1024:(par2 + 1) * 1024]
            by = n.buf("y_sb" if par2 == 0 else "ysq")
            for h in range(H):
                ypar = h % 2
                n.emit("dve", lambda v, h=h, ypar=ypar: v.scalar_tensor_tensor(
                    y_sb[:, h * 64:(h + 1) * 64], vtm[:, h * 64:(h + 1) * 64],
                    dsb_ap[:, h:h + 1],
                    ycopy[:, ypar * 512 + (h // 2) * 64:ypar * 512 + (h // 2) * 64 + 64],
                    ALU.mult, ALU.add),
                    reads=[bscra, n.buf("dsb"), byc], writes=[by])
            n.emit("dve", lambda v: v.tensor_copy(n.Sbf[:, :], s_new[:, :]),
                   reads=[bsn], writes=[bsbf])
        if getattr(n, "dbg_ck", None) == chunk_idx:
            for nm2, islot in [("ew", 0), ("ktil", 2), ("ktf", 6)]:
                n.debug_dump(f"ck_{nm2}", CSLOT(islot), (128, 1024), [bsl[islot]], BF16)
            for nm2, islot in [("enA_fm", 3), ("eAex", 4), ("rt", 5)]:
                n.debug_dump(f"ck_{nm2}", CSLOT(islot), (128, 1024), [bsl[islot]], BF16)
            n.debug_dump("ck_dsb", n.dsb[:, par2 * 16:par2 * 16 + 16], (128, 16),
                         [n.buf("dsb")])
            n.debug_dump("ck_ysb", n.scr_b[:, par2 * 1024:(par2 + 1) * 1024], (128, 1024),
                         [by])
            n.debug_dump("ck_pm", n.pmask[:, 0:1024], (128, 1024),
                         [n.buf("pm0"), n.buf("pm1")], BF16)
            n.debug_dump("ck_eal", n.eal[:, 0:8], (128, 8), [n.buf("eal")])

    def _wkv_tail(self, chunk_idx, dyn_row0):
        n = self
        par2 = chunk_idx % 2
        y_sb = n.scr_b[:, par2 * 1024:(par2 + 1) * 1024]
        by = n.buf("y_sb" if par2 == 0 else "ysq")
        st = n.stats
        bst = n.buf("gnstats")
        y3 = y_sb.rearrange("p (h j) -> p h j", h=H)
        n.emit("dve", lambda v: v.reduce_sum(st[:, 16:32], y3, axis=AX.X),
               reads=[by], writes=[bst])
        ysq = n.outt[:, 0:1024]
        bsq = [n.buf("ot0"), n.buf("ot1")]
        n.emit("act", lambda a: a.activation(
            out=ysq, in_=y_sb, func=AF.Square, bias=0.0, scale=1.0),
            reads=[by], writes=bsq)
        n.emit("dve", lambda v: v.reduce_sum(
            st[:, 32:48], ysq.rearrange("p (h j) -> p h j", h=H), axis=AX.X),
            reads=bsq, writes=[bst])
        n.emit("dve", lambda v: v.tensor_scalar(
            st[:, 48:64], st[:, 16:32], n.ccol(-1.0 / 64), None, ALU.mult),
            reads=[bst, n.buf("consts")], writes=[bst])
        n.emit("dve", lambda v: v.tensor_mul(st[:, 80:96], st[:, 48:64], st[:, 48:64]),
               reads=[bst], writes=[bst])
        n.emit("dve", lambda v: v.scalar_tensor_tensor(
            st[:, 64:80], st[:, 32:48], n.ccol(1.0 / 64), st[:, 80:96],
            ALU.mult, ALU.subtract), reads=[bst, n.buf("consts")], writes=[bst])
        n.emit("act", lambda a: a.activation(
            out=st[:, 80:96], in_=st[:, 64:80], func=AF.Sqrt,
            bias=n.ccol(GN_EPS), scale=1.0),
            reads=[bst, n.buf("consts")], writes=[bst])
        n.emit("dve", lambda v: v.reciprocal(st[:, 80:96], st[:, 80:96]),
               reads=[bst], writes=[bst])
        n.emit("dve", lambda v: v.tensor_mul(st[:, 48:64], st[:, 48:64], st[:, 80:96]),
               reads=[bst], writes=[bst])
        yn_ap = n.pmask[:, 1024:2048]
        byn = n.buf("yn_st")
        for h in range(H):
            n.emit("act", lambda a2, h=h: a2.activation(
                out=yn_ap[:, h * 64:(h + 1) * 64], in_=y_sb[:, h * 64:(h + 1) * 64],
                func=AF.Identity, scale=st[:, 80 + h:81 + h], bias=st[:, 48 + h:49 + h]),
                reads=[by, bst], writes=[byn])
        n.emit("act", lambda a2: a2.dma_start(
            out=n.d_yn.ap()[dyn_row0:dyn_row0 + 128, :], in_=yn_ap),
            reads=[byn], writes=[n.buf("d_yn")], dma=True)

    def tmix_phase(self):
        n = self
        for dst, srcs in [("cs0", ["lnout0"]), ("cs1", ["lnout1"]),
                          ("cs2", ["lnout2"]), ("cs3", ["lnout3"]),
                          ("y_sb", ["lnx0"]), ("ysq", ["lnx1"]),
                          ("s_tmp", ["lnx2"]), ("ycopy", ["lnsq0"])]:
            n.alias_fence(dst, srcs)
        n.emit("dve", lambda v: v.memset(n.S0[:, :], 0.0), writes=[n.buf("S0")])
        n.emit("dve", lambda v: v.memset(n.Sbf[:, :], 0.0), writes=[n.buf("Sbf")])
        ci = 0
        NTB = 16
        tslot = lambda tb: tb % 2

        def t0_of(tb):
            return (tb % 8) * 256

        self._lora_pass(0, 256, False, 0)
        self._eta_pass(0, 256, False, 0)
        xic_k = self._xi_pass("k", 0, 256, False, tslot=0)
        pending_tails = []
        for tb in range(NTB):
            own = tb >= 8
            t0 = t0_of(tb)
            ts, tsn = tb % 2, (tb + 1) % 2
            if pending_tails:
                self._wkv_tail(*pending_tails[0])
            self._proj_tm(n.d_wk, "d_wk", 4096, xic_k, 256)
            if pending_tails:
                self._wkv_tail(*pending_tails[1])
            pending_tails = []
            xic = self._xi_pass("v", t0, 256, own, tslot=ts)
            self._proj_tm(n.d_wv, "d_wv", 6144, xic, 256)
            if own:
                xic = self._xi_pass("r", t0, 256, own, tslot=ts)
                self._proj_fm(n.d_wr, "d_wr", 2048, xic, 256)
            if tb < NTB - 1:
                nown = (tb + 1) >= 8
                nt0 = t0_of(tb + 1)
                self._lora_pass(nt0, 256, nown, tsn)
                self._eta_pass(nt0, 256, nown, tsn)
                xic_k = self._xi_pass("k", nt0, 256, nown, tslot=tsn)
            if tb == 8:
                n.debug_dump("tb0_tanh1", n.tanh1[:, 0:256], (128, 256), [n.buf("tanh1_0")], BF16)
                n.debug_dump("tb0_eta", n.eta[0:64, 0:256], (64, 256), [n.buf("eta_0")], BF16)
                n.debug_dump("tb0_kfm", n.scr_a[:, 0:2048], (128, 2048), [n.buf("scra")], BF16)
                n.debug_dump("tb0_rfm", n.scr_a[:, 2048:4096], (128, 2048), [n.buf("scra")], BF16)
                n.debug_dump("tb0_ktm", n.scr_a[:, 4096:6144], (128, 2048), [n.buf("scra")], BF16)
                n.debug_dump("tb0_vtm", n.scr_a[:, 6144:8192], (128, 2048), [n.buf("scra")], BF16)
            for ckw in range(2):
                self._wkv_front(ckw, own, 4096 + ckw * 1024, 6144 + ckw * 1024,
                                ci + ckw, tslot=ts)
            if own:
                pending_tails = [(ci, t0), (ci + 1, t0 + 128)]
            ci += 2
            if tb == 7:
                s_cur = n.S0 if ci % 2 == 0 else n.S1
                bsc = n.buf("S0" if ci % 2 == 0 else "S1")
                n.emit("dve", lambda v, s_cur=s_cur: v.tensor_scalar(
                    s_cur[:, :], s_cur[:, :], n.vcol("is_odd"), None, ALU.mult),
                    reads=[bsc, n.vbuf("is_odd")], writes=[bsc])
                n.emit("dve", lambda v, s_cur=s_cur: v.tensor_copy(n.Sbf[:, :], s_cur[:, :]),
                       reads=[bsc], writes=[n.buf("Sbf")])
                n.debug_dump("S_bound", n.Sbf[:, :], (128, 512), [n.buf("Sbf")], BF16)
                for cc in range(NCC):
                    n.alias_fence(f"xi_2048_{cc}", [f"xi_0_{cc}"])
        for args in pending_tails:
            self._wkv_tail(*args)
        n.debug_dump_dram("yn", n.d_yn, (OWN, C), [n.buf("d_yn")], BF16)

    def alias_fence(self, dst, srcs):
        bd = self.buf(dst)
        for s in srcs:
            b = self.buf(s)
            bd.r += b.r
            if b.w is not None:
                bd.r.append(b.w)

    def gate_phase(self):
        n = self
        pE = n.psbuf("E")
        pC = n.psbuf("C")
        st = n.stats
        bst = n.buf("gst")
        bcomb = n.buf("comb")
        bcombf = n.buf("comb_fm")
        for tt in range(NTT_OWN):
            zp = n.psE[:, tt * 8:(tt + 1) * 8]
            for cc in range(NCC):
                n.MM(zp, n.xno_fm[cc][:, tt * 128:(tt + 1) * 128],
                     n.wgate_sb[:, cc * 8:(cc + 1) * 8], cc == 0, False,
                     reads=[n.buf(f"xno_fm{cc}"), n.buf("wgate")], writes=[pE])
            n.MM(zp, n.onesrow_sb[0:1, 0:128], n.bgate_row[0:1, 0:8],
                 False, True, reads=[n.buf("onesrow"), n.buf("bgate_row")], writes=[pE])
            cslot = n.comb[:, tt * 8:(tt + 1) * 8]
            n.emit("dve", lambda v, zp=zp: v.reduce_max(st[:, 96:97], zp,
                                                        axis=AX.X, negate=True),
                   reads=[pE], writes=[bst])
            n.emit("act", lambda a2, zp=zp: a2.activation(out=st[:, 104:112], in_=zp,
                                                          func=AF.Exp, bias=st[:, 96:97], scale=1.0),
                   reads=[pE, bst], writes=[bst])
            n.emit("dve", lambda v: v.reduce_sum(st[:, 97:98], st[:, 104:112], axis=AX.X),
                   reads=[bst], writes=[bst])
            n.emit("dve", lambda v: v.reciprocal(st[:, 98:99], st[:, 97:98]),
                   reads=[bst], writes=[bst])
            n.emit("dve", lambda v, cslot=cslot: v.tensor_scalar(
                cslot, st[:, 104:112], st[:, 98:99], None, ALU.mult),
                reads=[bst], writes=[bcomb])
            n.emit("dve", lambda v, cslot=cslot: v.reduce_max(st[:, 99:100], cslot, axis=AX.X),
                   reads=[bcomb], writes=[bst])
            n.emit("dve", lambda v, cslot=cslot: v.tensor_scalar(
                st[:, 104:112], cslot, st[:, 99:100], None, ALU.is_lt),
                reads=[bcomb, bst], writes=[bst])
            n.emit("dve", lambda v, cslot=cslot: v.tensor_mul(st[:, 104:112], st[:, 104:112], cslot),
                   reads=[bcomb, bst], writes=[bst])
            n.emit("dve", lambda v: v.reduce_max(st[:, 100:101], st[:, 104:112], axis=AX.X),
                   reads=[bst], writes=[bst])
            n.emit("dve", lambda v, cslot=cslot: v.tensor_scalar(
                st[:, 104:112], cslot, st[:, 100:101], None, ALU.is_ge),
                reads=[bcomb, bst], writes=[bst])
            n.emit("dve", lambda v, cslot=cslot: v.tensor_mul(cslot, cslot, st[:, 104:112]),
                   reads=[bst, bcomb], writes=[bcomb])
            if tt == 0:
                n.emit("dve", lambda v: v.tensor_copy(n.outt[:, 0:8], n.psE[:, 0:8]),
                       reads=[pE], writes=[n.buf("zdbg")])
                n.debug_dump("z0", n.outt[:, 0:8], (128, 8), [n.buf("zdbg")])
        n.debug_dump("comb_early", n.comb[:, :], (128, 128), [n.buf("comb")])

    def _comb_fm_build(self):
        n = self
        pC = n.psbuf("C")
        bcomb, bcombf = n.buf("comb"), n.buf("comb_fm")
        for tt in range(NTT_OWN):
            n.emit("pe", lambda t, tt=tt: t.transpose(
                n.psC[0:8, 0:128], n.comb[:, tt * 8:(tt + 1) * 8],
                n.ident_f32[0:128, 0:128]),
                reads=[bcomb, n.buf("ident_f")], writes=[pC])
            n.emit("dve", lambda v, tt=tt: v.tensor_copy(
                n.scr_c[0:8, tt * 128:(tt + 1) * 128], n.psC[0:8, 0:128]),
                reads=[pC], writes=[bcombf])

    def _vaug_build(self):
        n = self
        pC, pD = n.psbuf("C"), n.psbuf("D")
        bvsts = [n.buf("vstage"), n.buf("expS2")]
        vstages = [n.scr_c[:, 2048:2568], n.scr_c[:, 2568:3088]]
        for sl in range(2):
            ap1 = vstages[sl].rearrange("p (k o) -> p k o", k=8)[:, :, 64:65]
            n.emit("dve", lambda v, ap1=ap1: v.memset(ap1, 1.0), writes=[bvsts[sl]])
        for jb in range(2):
            bws = n.buf("ws_a")
            bws2 = n.buf("ws_b")
            ws_ap = n.ws.ap()[:, 0:4096].rearrange("p (cc w) -> p cc w", cc=8)
            srcw = n.d_wqkv.ap().rearrange("(cc p) c -> p cc c", p=128)[
                :, :, 2 * C + jb * 512:2 * C + (jb + 1) * 512]
            n.emit("sp", lambda e, ws_ap=ws_ap, srcw=srcw: e.dma_start(out=ws_ap, in_=srcw),
                   reads=[n.buf("d_wqkv")], writes=[bws, bws2], dma=True)
            for sc in range(32):
                ps, pb = (n.psC, pC) if sc % 2 == 0 else (n.psD, pD)
                for cc in range(NCC):
                    n.MM(ps[:, 0:512], n.xn_fm[cc][:, sc * 128:(sc + 1) * 128],
                         n.ws[:, cc * 512:(cc + 1) * 512], cc == 0, False,
                         reads=[bws if cc < 4 else bws2, n.buf(f"xn_fm{cc}")], writes=[pb])
                n.MM(ps[:, 0:512], n.onesrow_sb[0:1, 0:128],
                     n.bv_row[0:1, jb * 512:(jb + 1) * 512], False, True,
                     reads=[n.buf("onesrow"), n.buf("bv_row")], writes=[pb])
                sl = sc % 2
                dst = vstages[sl].rearrange(
                    "p (hp par j) -> p hp par j", hp=4, par=2)[:, :, :, 0:64]
                srcp = ps.ap()[:, 0:512].rearrange("p (hp par j) -> p hp par j", hp=4, par=2)
                n.emit("dve", lambda v, dst=dst, srcp=srcp: v.tensor_copy(dst, srcp),
                       reads=[pb], writes=[bvsts[sl]])
                n.emit("pool", lambda e, sc=sc, jb=jb, sl=sl: e.dma_start(
                    out=n.d_vaug.ap()[sc * 128:(sc + 1) * 128, jb * 520:(jb + 1) * 520],
                    in_=vstages[sl]), reads=[bvsts[sl]], writes=[n.buf("d_vaug")], dma=True)

    def _attn_pair(self, hp, half):
        n = self
        pA0, pA1 = n.psbuf("A0"), n.psbuf("A1")
        pB0, pB1 = n.psbuf("B0"), n.psbuf("B1")
        pC, pD, pG = n.psbuf("C"), n.psbuf("D"), n.psbuf("G")
        bscra, bst = n.buf("scra"), n.buf("attn_st")
        bws0 = n.buf("ws_a")
        bacc = n.buf("acc")
        hq2 = hp - 4 * half
        jb, hp4 = hp // 4, hp % 4
        for j, col0 in enumerate([hp * 128, C + hp * 128]):
            dst = n.ws.ap()[:, j * 1024:(j + 1) * 1024].rearrange("p (cc w) -> p cc w", cc=8)
            srcw = n.d_wqkv.ap().rearrange("(cc p) c -> p cc c", p=128)[:, :, col0:col0 + 128]
            n.emit("sp", lambda e, dst=dst, srcw=srcw: e.dma_start(out=dst, in_=srcw),
                   reads=[n.buf("d_wqkv")], writes=[bws0], dma=True)
        srcv = n.d_vaug.ap().rearrange("(sc p) c -> p sc c", p=128)[
            :, :, jb * 520 + hp4 * 130:jb * 520 + (hp4 + 1) * 130]
        n.emit("sp", lambda e: e.dma_start(
            out=n.scr_a.ap()[:, 0:4160].rearrange("p (sc c) -> p sc c", sc=32), in_=srcv),
            reads=[n.buf("d_vaug")], writes=[bscra], dma=True)
        for tb4 in range(4):
            ps, pb = (n.psC, pC) if tb4 % 2 == 0 else (n.psD, pD)
            for cc in range(NCC):
                n.MM(ps[:, 0:512], n.ws[:, cc * 128:(cc + 1) * 128],
                     n.xno_fm[cc][:, tb4 * 512:(tb4 + 1) * 512], cc == 0, cc == NCC - 1,
                     reads=[bws0, n.buf(f"xno_fm{cc}")], writes=[pb])
            n.emit("dve", lambda v, ps=ps, tb4=tb4: v.tensor_scalar(
                n.scr_a[:, 4160 + tb4 * 512:4160 + (tb4 + 1) * 512], ps[:, 0:512],
                n.vcol("bq", hp), None, ALU.add),
                reads=[pb, n.vbuf("bq")], writes=[bscra])
        for tb8 in range(8):
            ps, pb = (n.psC, pC) if tb8 % 2 == 0 else (n.psD, pD)
            for cc in range(NCC):
                n.MM(ps[:, 0:512], n.ws[:, 1024 + cc * 128:1024 + (cc + 1) * 128],
                     n.xn_fm[cc][:, tb8 * 512:(tb8 + 1) * 512], cc == 0, cc == NCC - 1,
                     reads=[bws0, n.buf(f"xn_fm{cc}")], writes=[pb])
            n.emit("dve", lambda v, ps=ps, tb8=tb8: v.tensor_scalar(
                n.scr_a[:, 6208 + tb8 * 512:6208 + (tb8 + 1) * 512], ps[:, 0:512],
                n.vcol("bk", hp), None, ALU.add),
                reads=[pb, n.vbuf("bk")], writes=[bscra])
        ES_COL = (4096, 5120, 2568, 7168)
        for tb4 in range(4):
            def emit_v(sc):
                es = n.scr_c[:, ES_COL[sc % 4]:ES_COL[sc % 4] + 1024]
                bes = n.buf(f"expS{sc % 4}")
                for par in range(2):
                    n.MM((n.psC if par == 0 else n.psD)[0:65, :],
                         n.scr_a[:, sc * 130 + par * 65:sc * 130 + (par + 1) * 65],
                         es[:, par * 512:(par + 1) * 512], sc == 0, sc == 31,
                         reads=[bscra, bes], writes=[pC if par == 0 else pD])
            for sc in range(32):
                qk, qb0, qb1 = (n.psA, pA0, pA1) if sc % 2 == 0 else (n.psB, pB0, pB1)
                for par in range(2):
                    n.MM(qk[:, par * 512:(par + 1) * 512],
                         n.scr_a[par * 64:par * 64 + 64, 6208 + sc * 128:6208 + (sc + 1) * 128],
                         n.scr_a[par * 64:par * 64 + 64, 4160 + tb4 * 512:4160 + (tb4 + 1) * 512],
                         True, True, pos=(64 * par, 0),
                         reads=[bscra], writes=[qb0 if par == 0 else qb1])
                es = n.scr_c[:, ES_COL[sc % 4]:ES_COL[sc % 4] + 1024]
                bes = n.buf(f"expS{sc % 4}")
                n.emit("act", lambda a2, qk=qk, es=es: a2.activation(
                    out=es, in_=qk[:, :], func=AF.Exp, bias=n.ccol(ATT_SHIFT),
                    scale=n.ccol(0.125)),
                    reads=[qb0, qb1, n.buf("consts")], writes=[bes])
                if sc > 1:
                    emit_v(sc - 2)
            emit_v(30)
            emit_v(31)
            for par in range(2):
                atmp = n.scr_c[0:65, 6144 + par * 512:6144 + (par + 1) * 512]
                batm = n.buf(f"atmp{par}")
                n.emit("dve", lambda v, atmp=atmp, par=par: v.tensor_copy(
                    atmp, (n.psC if par == 0 else n.psD)[0:65, :]),
                    reads=[pC if par == 0 else pD], writes=[batm])
                for tt4 in range(4):
                    slot = par * 4 + tt4
                    n.emit("pe", lambda t, atmp=atmp, slot=slot, tt4=tt4: t.transpose(
                        n.psG[:, slot * 128:slot * 128 + 65],
                        atmp[0:65, tt4 * 128:(tt4 + 1) * 128],
                        n.ident_sb[0:65, 0:65]),
                        reads=[batm, n.buf("ident")], writes=[pG])
            for par in range(2):
                for tt4 in range(4):
                    slot = par * 4 + tt4
                    tt = tb4 * 4 + tt4
                    hq = hq2 * 2 + par
                    piece = n.scr_b if hq < 4 else n.acc_hi
                    col = tt * 256 + (hq % 4) * 64
                    rcol = n.stats[:, 112 + slot:113 + slot]
                    n.emit("dve", lambda v, slot=slot, rcol=rcol: v.reciprocal(
                        rcol, n.psG[:, slot * 128 + 64:slot * 128 + 65]),
                        reads=[pG], writes=[bst])
                    n.emit("dve", lambda v, slot=slot, rcol=rcol, col=col, piece=piece:
                           v.tensor_scalar(
                               piece[:, col:col + 64], n.psG[:, slot * 128:slot * 128 + 64],
                               rcol, None, ALU.mult),
                           reads=[pG, bst], writes=[bacc])

    def _moe_half(self, half):
        n = self
        pC, pD = n.psbuf("C"), n.psbuf("D")
        bacc, bcomb, bcombf = n.buf("acc"), n.buf("comb"), n.buf("comb_fm")
        bscra = n.buf("scra")
        dstw = n.scr_a.ap()[:, 8192:12288].rearrange("p (cc w) -> p cc w", cc=8)
        srcw = n.d_wo.ap().rearrange("(cc p) c -> p cc c", p=128)[
            :, :, half * 512:(half + 1) * 512]
        n.emit("sp", lambda e: e.dma_start(out=dstw, in_=srcw),
               reads=[n.buf("d_wo")], writes=[n.buf("ws_wo"), bscra], dma=True)
        for c4 in range(4):
            ydst = n.scr_a[:, c4 * 2048:(c4 + 1) * 2048]
            n.emit("act", lambda a2, ydst=ydst, c4=c4: a2.dma_start(
                out=ydst, in_=n.d_yn.ap()[:, c4 * 128:(c4 + 1) * 128], transpose=True),
                reads=[n.buf("d_yn")], writes=[bscra], dma=True)
        for tt in range(NTT_OWN):
            n.MM(n.psE[:, 0:512], n.scr_c[0:8, tt * 128:(tt + 1) * 128],
                 n.bexp_sb[0:8, half * 512:(half + 1) * 512], True, True,
                 reads=[bcombf, n.buf("bexp")], writes=[n.psbuf("E")])
            for piece2 in range(2):
                dstp = (n.scr_b if piece2 == 0 else n.acc_hi)[:, tt * 256:(tt + 1) * 256]
                n.emit("dve", lambda v, dstp=dstp, piece2=piece2: v.tensor_add(
                    dstp, dstp, n.psE[:, piece2 * 256:(piece2 + 1) * 256]),
                    reads=[n.psbuf("E"), bacc], writes=[bacc])
        if half == 0:
            n.debug_dump("acc_bias", n.scr_b[:, 0:4096], (128, 4096), [n.buf("acc")])
        for e8 in range(E):
            if e8 % 2 == 0:
                warena = n.ws.ap()[:, 0:4096]
                bwsh = [[n.buf("ws_a")], [n.buf("ws_b")]]
            else:
                warena = n.scr_c.ap()[:, 4096:8192]
                bwsh = [[n.buf("expS0"), n.buf("expS1")],
                        [n.buf("atmp0"), n.buf("atmp1"), n.buf("expS3")]]
            for hcc in range(2):
                src = n.d_wexp.ap()[e8 * C:(e8 + 1) * C, :].rearrange(
                    "(cc p) c -> p cc c", p=128)[
                    :, hcc * 4:(hcc + 1) * 4, half * 512:(half + 1) * 512]
                dst = warena[:, hcc * 2048:(hcc + 1) * 2048].rearrange(
                    "p (cc w) -> p cc w", cc=4)
                n.emit("sp", lambda e, dst=dst, src=src: e.dma_start(out=dst, in_=src),
                       reads=[n.buf(f"d_wexp{e8}")], writes=bwsh[hcc], dma=True)
            for tt in range(NTT_OWN):
                ps, pb = (n.psC, pC) if tt % 2 == 0 else (n.psD, pD)
                for cc in range(NCC):
                    n.MM(ps[:, 0:512], n.xno_fm[cc][:, tt * 128:(tt + 1) * 128],
                         warena[:, cc * 512:(cc + 1) * 512], cc == 0, cc == NCC - 1,
                         reads=bwsh[cc // 4] + [n.buf(f"xno_fm{cc}")], writes=[pb])
                for piece2 in range(2):
                    dstp = (n.scr_b if piece2 == 0 else n.acc_hi)[:, tt * 256:(tt + 1) * 256]
                    n.emit("dve", lambda v, ps=ps, dstp=dstp, tt=tt, e8=e8, piece2=piece2:
                           v.scalar_tensor_tensor(
                               dstp, ps[:, piece2 * 256:(piece2 + 1) * 256],
                               n.comb[:, tt * 8 + e8:tt * 8 + e8 + 1], dstp,
                               ALU.mult, ALU.add),
                           reads=[pb, bcomb, bacc], writes=[bacc])

    def _wo_epilogue_half(self, half):
        n = self
        pC, pD, pE = n.psbuf("C"), n.psbuf("D"), n.psbuf("E")
        bacc = n.buf("acc")
        bg = n.buf("g")
        bscra = n.buf("scra")
        bsacc = n.buf("s_acc")
        for tt in range(NTT_OWN):
            for piece2 in range(2):
                srcp = (n.scr_b if piece2 == 0 else n.acc_hi)[:, tt * 256:(tt + 1) * 256]
                n.emit("act", lambda a2, tt=tt, piece2=piece2, srcp=srcp: a2.activation(
                    out=n.scr_c[:, tt * 512 + piece2 * 256:tt * 512 + (piece2 + 1) * 256],
                    in_=srcp, func=AF.Silu, bias=0.0, scale=1.0),
                    reads=[bacc], writes=[bg])
        bws = n.buf("scra")
        wo = lambda cc: n.scr_a[:, 8192 + cc * 512:8192 + (cc + 1) * 512]
        for cc in range(NCC):
            n.MM(n.psE[0:1, 0:512], n.lnb_bf[:, cc:cc + 1],
                 wo(cc), cc == 0, cc == NCC - 1,
                 reads=[bws, n.buf("lnb_bf")], writes=[pE])
        n.emit("dve", lambda v: v.tensor_copy(n.bb[0:1, 0:512], n.psE[0:1, 0:512]),
               reads=[pE], writes=[n.buf("bbst")])
        for cc in range(NCC):
            n.emit("dve", lambda v, cc=cc: v.tensor_scalar(
                wo(cc), wo(cc), n.vcol("ln_w", cc), None, ALU.mult),
                reads=[bws, n.vbuf("ln_w")], writes=[bws])
        for ch in range(2):
            if ch == 1:
                for c4 in range(4):
                    cc = 4 + c4
                    ydst = n.scr_a[:, c4 * 2048:(c4 + 1) * 2048]
                    n.emit("act", lambda a2, ydst=ydst, cc=cc: a2.dma_start(
                        out=ydst, in_=n.d_yn.ap()[:, cc * 128:(cc + 1) * 128], transpose=True),
                        reads=[n.buf("d_yn")], writes=[bscra], dma=True)
            for tt in range(NTT_OWN):
                ps, pb = (n.psC, pC) if tt % 2 == 0 else (n.psD, pD)
                for c4 in range(4):
                    cc = ch * 4 + c4
                    n.MM(ps[:, 0:512],
                         n.scr_a[:, c4 * 2048 + tt * 128:c4 * 2048 + (tt + 1) * 128],
                         wo(cc), c4 == 0, ch == 1 and c4 == 3,
                         reads=[bscra, bws], writes=[pb])
                if ch == 0:
                    n.MM(ps[:, 0:512], n.onesrow_sb[0:1, 0:128], n.bb[0:1, 0:512],
                         False, True, reads=[n.buf("onesrow"), n.buf("bbst")], writes=[pb])
                    n.emit("dve", lambda v, ps=ps, tt=tt: v.tensor_copy(
                        n.scr_b[:, tt * 256:(tt + 1) * 256], ps[:, 0:256]),
                        reads=[pb, bg], writes=[bsacc])
                    n.emit("dve", lambda v, ps=ps, tt=tt: v.tensor_copy(
                        n.acc_hi[:, tt * 256:(tt + 1) * 256], ps[:, 256:512]),
                        reads=[pb, bg], writes=[bsacc])
                else:
                    xt = n.xtile[:, (tt % 2) * 512:(tt % 2) * 512 + 512]
                    bxt = n.buf(f"xt{tt % 2}")
                    n.emit("act", lambda e, xt=xt, tt=tt: e.dma_start(
                        out=xt, in_=n.in_x_own.ap()[tt * 128:(tt + 1) * 128,
                                                    half * 512:(half + 1) * 512]),
                        writes=[bxt], dma=True)
                    ot = n.outt[:, (tt % 2) * 512:(tt % 2) * 512 + 512]
                    bot = n.buf(f"ot{tt % 2}")
                    for piece2 in range(2):
                        sap = (n.scr_b if piece2 == 0 else n.acc_hi)[:, tt * 256:(tt + 1) * 256]
                        n.emit("dve", lambda v, ps=ps, ot=ot, sap=sap, piece2=piece2:
                               v.tensor_add(ot[:, piece2 * 256:(piece2 + 1) * 256],
                                            ps[:, piece2 * 256:(piece2 + 1) * 256], sap),
                               reads=[pb, bsacc], writes=[bot])
                        n.emit("dve", lambda v, ot=ot, tt=tt, piece2=piece2: v.tensor_mul(
                            ot[:, piece2 * 256:(piece2 + 1) * 256],
                            ot[:, piece2 * 256:(piece2 + 1) * 256],
                            n.scr_c[:, tt * 512 + piece2 * 256:tt * 512 + (piece2 + 1) * 256]),
                            reads=[bg, bot], writes=[bot])
                        n.emit("dve", lambda v, ot=ot, xt=xt, piece2=piece2: v.tensor_add(
                            ot[:, piece2 * 256:(piece2 + 1) * 256],
                            ot[:, piece2 * 256:(piece2 + 1) * 256],
                            xt[:, piece2 * 256:(piece2 + 1) * 256]),
                            reads=[bot, bxt], writes=[bot])
                    n.emit("sp", lambda e, tt=tt, ot=ot: e.dma_start(
                        out=n.out.ap()[tt * 128:(tt + 1) * 128, half * 512:(half + 1) * 512],
                        in_=ot), reads=[bot], writes=[n.buf("out_w")], dma=True)

    def ffn_phase(self):
        n = self
        n.alias_fence("acc", ["y_sb", "ysq", "s_tmp", "utmp", "gnstats", "sqdump",
                              "lnx0", "lnx1", "lnx2", "lnsq0"]
                      + [f"xi_{c}_{cc}" for c in (0, 2048) for cc in range(NCC)]
                      + ["tanh1_0", "tanh1_1", "eta_0", "eta_1", "pm0", "pm1", "yn_st"])
        n.alias_fence("xt0", ["S0", "S1"])
        n.alias_fence("xt1", ["S0", "S1", "Sbf"])
        n.alias_fence("scra", ["ws_wo"])
        for nm in ("g", "expS0", "expS1", "expS2", "expS3", "atmp0", "atmp1",
                   "comb_fm", "vstage"):
            n.alias_fence(nm, [f"cs{i}" for i in range(8)]
                          + [f"lnout{i}" for i in range(4)])
        n.emit("dve", lambda v: v.tensor_copy(n.lnb_bf[:, :], n.vec_sb[:, 48:56]),
               reads=[n.vbuf("ln_b")], writes=[n.buf("lnb_bf")])
        n.debug_dump("comb", n.comb[:, :], (128, 128), [n.buf("comb")])
        self._vaug_build()
        for half in range(2):
            n.alias_fence("acc", ["s_acc"])
            for nm in ("expS0", "expS1", "expS2", "expS3", "atmp0", "atmp1",
                       "comb_fm"):
                n.alias_fence(nm, ["g"])
            n.alias_fence("g", ["expS0", "expS1", "expS2", "expS3",
                                "atmp0", "atmp1"])
            self._comb_fm_build()
            for hp in range(4 * half, 4 * half + 4):
                self._attn_pair(hp, half)
            if half == 0:
                n.debug_dump("acc_attn", n.scr_b[:, 0:4096], (128, 4096), [n.buf("acc")])
            self._moe_half(half)
            if half == 0:
                n.debug_dump("acc_moe", n.scr_b[:, 0:4096], (128, 4096), [n.buf("acc")])
            self._wo_epilogue_half(half)

    def debug_dump_dram(self, key, dram_t, shape, after_bufs, dt=F32):
        if key not in self.debug_keys:
            return
        t = self.nc.dram_tensor("dbg_" + key, list(shape), dt, kind="ExternalOutput")
        self.dbg[key] = t
        return self.emit("sp", lambda e: e.dma_start(out=t.ap(), in_=dram_t.ap()),
                         reads=list(after_bufs), writes=[self.buf("dbg_" + key)], dma=True)

    def debug_dump(self, key, sbuf_ap, shape, after_bufs, dt=F32):
        if key not in self.debug_keys:
            return
        t = self.nc.dram_tensor("dbg_" + key, list(shape), dt, kind="ExternalOutput")
        self.dbg[key] = t
        tok = self.emit("sp", lambda e: e.dma_start(out=t.ap(), in_=sbuf_ap),
                        reads=list(after_bufs), writes=[self.buf("dbg_" + key)], dma=True)
        return tok

    def build(self, phases):
        self._init_consts()
        self.prologue()
        for ph in phases:
            ph()
        finals = []
        bout = self.buf("out_w")
        if bout.w is not None:
            finals.append(bout.w)
        for key in self.dbg:
            finals.append(self.buf("dbg_" + key).w)
        runner = self.p.finalize(None, finals)
        nc = self.nc
        with nc.Block() as block:
            block.tensor(runner("pe"))
            block.scalar(runner("act"))
            block.vector(runner("dve"))
            block.gpsimd(runner("pool"))
            block.sync(runner("sp"))
        self.stack.close()
        return nc


def kernel(**inputs):
    k = K()
    nc = k.build([k.ln_phase, k.gate_phase, k.tmix_phase, k.ffn_phase])
    in_maps = [_slices_for_core(inputs, c) for c in range(8)]
    res = run_bass_kernel_spmd(nc, in_maps, core_ids=list(range(8)))
    out = np.zeros((B, T, C), np.float32)
    for c in range(8):
        b, half = c // 2, c % 2
        out[b, half * OWN:(half + 1) * OWN] = res.results[c]["out"]
    return out
